# version 20
# baseline (speedup 1.0000x reference)
import numpy as np
import ml_dtypes
from contextlib import ExitStack

import concourse.bass as bass
import concourse.mybir as mybir
from concourse.bass_utils import run_bass_kernel_spmd

F32 = mybir.dt.float32
BF16 = mybir.dt.bfloat16
AF = mybir.ActivationFunctionType
ALU = mybir.AluOpType
AX = mybir.AxisListType

FUSED = True
SAME_ENGINE_SYNC = True

ENGINES = ("pe", "act", "dve", "pool", "qs")
_ENG_ATTR = {"pe": "tensor", "act": "scalar", "dve": "vector", "pool": "gpsimd", "qs": "sync"}
NQS = 12
NQP = 8


class Buf:
    __slots__ = ("name", "last_write", "readers", "exclusive", "last_access")

    def __init__(self, name, exclusive=False):
        self.name = name
        self.last_write = None
        self.readers = []
        self.exclusive = exclusive
        self.last_access = None


class Node:
    __slots__ = ("stream", "chan", "fn", "deps", "seq", "flush_id", "milestone", "rank", "tag")

    def __init__(self, stream, chan, fn, seq, flush_id, tag=""):
        self.stream = stream
        self.chan = chan
        self.fn = fn
        self.deps = []
        self.seq = seq
        self.flush_id = flush_id
        self.milestone = False
        self.rank = None
        self.tag = tag


class TT:
    __slots__ = ("h", "buf")

    def __init__(self, h, buf):
        self.h = h
        self.buf = buf

    def __getitem__(self, idx):
        return self.h[idx]

    @property
    def shape(self):
        return self.h.shape

    def rearrange(self, *a, **k):
        return self.h.rearrange(*a, **k)


class Builder:
    def __init__(self, nc, es):
        self.nc = nc
        self.es = es
        self.sems = {e: es.enter_context(nc.semaphore(f"sem_{e}")) for e in ("pe", "act", "dve", "pool")}
        for i in range(NQS):
            self.sems[f"qs{i}"] = es.enter_context(nc.semaphore(f"qsem{i}"))
        for i in range(NQP):
            self.sems[f"qp{i}"] = es.enter_context(nc.semaphore(f"qpsem{i}"))
        self.streams = {e: [] for e in ENGINES}
        self.seqs = {e: 0 for e in ENGINES}
        self.ranks = {e: 0 for e in ("pe", "act", "dve", "pool")}
        self.ranks.update({f"qs{i}": 0 for i in range(NQS)})
        self.ranks.update({f"qp{i}": 0 for i in range(NQP)})
        self.qs_count = 0
        self.qp_count = 0
        self.flush_id = 0
        self.stage_es = None
        self._consts = {}
        self.pump = None

    def pump_n(self, k):
        if self.pump is None:
            return
        for _ in range(k):
            try:
                next(self.pump)
            except StopIteration:
                self.pump = None
                return

    def constant(self, value):
        if value not in self._consts:
            t = self.es.enter_context(self.nc.sbuf_tensor(f"const_{len(self._consts)}", [128, 1], F32, side="right"))
            tt = TT(t, Buf(f"const_{value}"))
            self.op("dve", lambda e, tt=tt, value=value: e.memset(tt[:], float(value)), writes=[tt])
            self._consts[value] = tt
        return self._consts[value]

    def _uname(self, name):
        self._ctr = getattr(self, "_ctr", 0) + 1
        return f"{name}_{self._ctr}"

    def sbuf(self, name, shape, dtype, persistent=False):
        es = self.es if persistent else self.stage_es
        side = "right" if persistent else None
        t = es.enter_context(self.nc.sbuf_tensor(self._uname(name), shape, dtype, side=side))
        return TT(t, Buf(name))

    def psum(self, name, shape, dtype=F32):
        t = self.stage_es.enter_context(self.nc.psum_tensor(self._uname(name), shape, dtype))
        return TT(t, Buf(name, exclusive=True))

    def op(self, stream, fn, reads=(), writes=(), tag=""):
        chan = "qs" if stream == "qs" else stream
        n = Node(stream, chan, fn, self.seqs[stream], self.flush_id, tag)
        self.seqs[stream] += 1
        for r in reads:
            bf = r.buf if isinstance(r, TT) else r
            if bf.last_write is not None and bf.last_write is not n:
                n.deps.append(bf.last_write)
            if bf.exclusive and bf.last_access is not None and bf.last_access is not n:
                n.deps.append(bf.last_access)
            bf.readers.append(n)
            if bf.exclusive:
                bf.last_access = n
        for w in writes:
            bf = w.buf if isinstance(w, TT) else w
            if bf.last_write is not None and bf.last_write is not n:
                n.deps.append(bf.last_write)
            for rd in bf.readers:
                if rd is not n:
                    n.deps.append(rd)
            if bf.exclusive and bf.last_access is not None and bf.last_access is not n:
                n.deps.append(bf.last_access)
            bf.readers = []
            bf.last_write = n
            if bf.exclusive:
                bf.last_access = n
        self.streams[stream].append(n)
        return n

    def dma(self, out_ap, in_ap, reads=(), writes=(), tag="", stream="qs"):
        def fn(eng, out_ap=out_ap, in_ap=in_ap):
            return eng.dma_start(out=out_ap, in_=in_ap)
        n = self.op(stream, fn, reads=reads, writes=writes, tag=tag)
        n.chan = "qs"
        n.milestone = True
        return n

    def stage(self, name):
        builder = self

        class _Stage:
            def __enter__(self):
                if builder.stage_es is not None:
                    self.nested = True
                    return builder
                self.nested = False
                builder.stage_es = ExitStack()
                builder.stage_es.__enter__()
                return builder

            def __exit__(self, et, ev, tb):
                if self.nested:
                    return False
                if et is None:
                    builder.flush(name)
                builder.stage_es.__exit__(et, ev, tb)
                builder.stage_es = None
                return False

        return _Stage()

    def flush(self, name):
        fid = self.flush_id
        def skip_edge(d, n):
            if d.stream != n.stream:
                return False
            if d.stream == "pe":
                return True
            return not SAME_ENGINE_SYNC

        for e in ENGINES:
            for n in self.streams[e]:
                for d in n.deps:
                    if d.flush_id != fid:
                        continue
                    if skip_edge(d, n):
                        continue
                    d.milestone = True
        for e in ENGINES:
            for n in self.streams[e]:
                if not n.milestone:
                    continue
                if n.chan == "qs":
                    if n.stream == "pool":
                        lane = self.qp_count % NQP
                        self.qp_count += 1
                        n.chan = f"qp{lane}"
                    else:
                        lane = self.qs_count % NQS
                        self.qs_count += 1
                        n.chan = f"qs{lane}"
                self.ranks[n.chan] += 1
                n.rank = self.ranks[n.chan]

        sems = self.sems
        final_qs = {c: self.ranks[c] for c in self.ranks if c.startswith("q")}

        def waitval(d):
            return d.rank * 16 if d.chan.startswith("q") else d.rank

        with self.nc.Block(name) as block:
            for e in ENGINES:
                nodes = self.streams[e]
                if not nodes and e != "qs":
                    continue

                def body(eng, nodes=nodes, e=e):
                    vclock = {c: 0 for c in self.ranks}
                    for n in nodes:
                        deps = {id(d): d for d in n.deps}.values()
                        for d in sorted(deps, key=lambda d: (d.chan, d.rank or 0)):
                            if d.flush_id != fid or not d.milestone:
                                continue
                            if skip_edge(d, n):
                                continue
                            if vclock[d.chan] >= d.rank:
                                continue
                            eng.wait_ge(sems[d.chan], waitval(d))
                            vclock[d.chan] = d.rank
                        if n.milestone and n.chan.startswith("q") and n.rank > 1 and vclock[n.chan] < n.rank - 1:
                            eng.wait_ge(sems[n.chan], (n.rank - 1) * 16)
                            vclock[n.chan] = n.rank - 1
                        ins = n.fn(eng)
                        if n.milestone:
                            ins.then_inc(sems[n.chan], 16 if n.chan.startswith("q") else 1)
                    if e == "qs":
                        for c, r in final_qs.items():
                            if r > 0:
                                eng.wait_ge(sems[c], r * 16)

                getattr(block, _ENG_ATTR[e])(body)

        self.streams = {e: [] for e in ENGINES}
        self.flush_id += 1


B = 2
L = 2048
D = 1024
T = B * L
NH, HD = 16, 64
E, F = 8, 4096
DIN, DST, DCONV, DTR = 2048, 16, 4, 64

NDC = D // 128
NTC = T // 128
NFC = F // 128
NIC = DIN // 128

INTERMEDIATES = {
    "acc": ([T, D], F32),
    "ln0bt": ([NDC, 128, T], BF16),
    "attn": ([T, D], F32),
    "x1": ([T, D], F32),
    "ln1bt": ([NDC, 128, T], BF16),
    "w1t_bf": ([E, D, F], BF16),
    "w2t_bf": ([E, F, D], BF16),
    "inpt_bf": ([D, 2 * DIN], BF16),
    "outpt_bf": ([DIN, D], BF16),
    "xsb": ([B, NIC, 128, L], F32),
    "zsil": ([B, NIC, 128, L], F32),
}

EXT_IN = {
    "x": ([B, L, D], F32),
    "w_qkv": ([NH, 3, D, HD], F32),
    "w_o": ([D, D], F32),
    "wrT": ([D, E], F32),
    "br": ([1, E], F32),
    "w1T": ([E, D, F], F32),
    "b1r": ([128, E * NFC], F32),
    "w2T": ([E, F, D], F32),
    "b2": ([E, D], F32),
    "ln_w": ([1, D], F32),
    "ln_b": ([1, D], F32),
    "inpT": ([D, 2 * DIN], F32),
    "conv_wr": ([128, NIC * DCONV], F32),
    "conv_br": ([128, NIC], F32),
    "xpT": ([DIN, 96], F32),
    "dtwT": ([DTR, DIN], F32),
    "dtbr": ([128, NIC], F32),
    "alogr": ([128, NIC * DST], F32),
    "Dr": ([128, NIC], F32),
    "outpT": ([DIN, D], F32),
    "Wt": ([D, D], F32),
    "bL1": ([1, D], F32),
    "ldT": ([100, 4], F32),
    "luT": ([4, 50], F32),
    "daux": ([100, D], F32),
    "uauxT": ([50, D], F32),
    "id128": ([128, 128], F32),
    "sel32": ([32, 32 * 128], BF16),
}

OUTPUT = ("out", [B, L, D], F32)


class Ctx:
    def __init__(self, nc, inter_in=(), inter_out=(), with_out=False, internal=False):
        self.nc = nc
        self.t = {}
        for name, (shape, dt) in EXT_IN.items():
            self.t[name] = nc.dram_tensor(name, shape, dt, kind="ExternalInput")
        for name in inter_in:
            shape, dt = INTERMEDIATES[name]
            self.t[name] = nc.dram_tensor(name, shape, dt, kind="ExternalInput")
        for name in inter_out:
            shape, dt = INTERMEDIATES[name]
            self.t[name] = nc.dram_tensor(name, shape, dt, kind="Internal" if internal else "ExternalOutput")
        if with_out:
            self.t["out"] = nc.dram_tensor(OUTPUT[0], OUTPUT[1], OUTPUT[2], kind="ExternalOutput")
        self.bufs = {}
        self.shared = {}

    def dbuf(self, name, sub=""):
        key = (name, sub)
        if key not in self.bufs:
            self.bufs[key] = Buf(f"{name}.{sub}")
        return self.bufs[key]


def stage_ln(b, ctx, src_name, dst_f32_name, dst_bt_name, eps=1e-5, tagp="ln"):
    src = ctx.t[src_name]
    src_flat = src if len(src.shape) == 2 else src.rearrange("b l d -> (b l) d")
    dst_f32 = ctx.t[dst_f32_name] if dst_f32_name else None
    dst_bt = ctx.t[dst_bt_name] if dst_bt_name else None
    with b.stage(tagp):
        xt = [b.sbuf(f"{tagp}_x{i}", [128, D], F32) for i in range(2)]
        nt = [b.sbuf(f"{tagp}_n{i}", [128, D], F32) for i in range(2)]
        nb = [b.sbuf(f"{tagp}_nb{i}", [128, D], BF16) for i in range(2)]
        stg = [b.sbuf(f"{tagp}_stg{i}", [128, 128], BF16) for i in range(4)]
        bn6 = [b.sbuf(f"{tagp}_bn6{i}", [128, 12], F32) for i in range(2)]
        bn2 = [b.sbuf(f"{tagp}_bn2{i}", [128, 2], F32) for i in range(2)]
        rstd = [b.sbuf(f"{tagp}_rstd{i}", [128, 1], F32) for i in range(2)]
        nmr = [b.sbuf(f"{tagp}_nmr{i}", [128, 1], F32) for i in range(2)]
        idb = b.sbuf(f"{tagp}_idb", [128, 128], BF16)
        idf = b.sbuf(f"{tagp}_idf", [128, 128], F32)
        ptr = [b.psum(f"{tagp}_ptr{i}", [128, 128], BF16) for i in range(2)]

        b.dma(idf[:], ctx.t["id128"][:], reads=[ctx.dbuf("id128")], writes=[idf])
        b.op("dve", lambda e: e.tensor_copy(idb[:], idf[:]), reads=[idf], writes=[idb])

        for tc in range(NTC):
            b.pump_n(4)
            i = tc % 2
            x, n_, nbt = xt[i], nt[i], nb[i]
            b6, b2_, rs, nm = bn6[i], bn2[i], rstd[i], nmr[i]
            b.dma(x[:], src_flat[tc * 128:(tc + 1) * 128, :], reads=[ctx.dbuf(src_name)], writes=[x])
            b.op("dve", lambda e, x=x, b6=b6: e.bn_stats(b6[:, 0:6], x[:, 0:512]), reads=[x], writes=[b6])
            b.op("dve", lambda e, x=x, b6=b6: e.bn_stats(b6[:, 6:12], x[:, 512:1024]), reads=[x], writes=[b6])
            b.op("dve", lambda e, b6=b6, b2_=b2_: e.bn_aggr(b2_[:], b6[:]), reads=[b6], writes=[b2_])
            ceps = b.constant(float(eps))
            b.op("act", lambda e, b2_=b2_, rs=rs, ceps=ceps: e.activation(rs[:], b2_[:, 1:2], AF.Sqrt, bias=ceps[:, 0:1]),
                 reads=[b2_, ceps], writes=[rs])
            b.op("dve", lambda e, rs=rs: e.reciprocal(rs[:], rs[:]), reads=[rs], writes=[rs])
            b.op("dve", lambda e, b2_=b2_, rs=rs, nm=nm: e.scalar_tensor_tensor(nm[:], b2_[:, 0:1], -1.0, rs[:], ALU.mult, ALU.mult),
                 reads=[b2_, rs], writes=[nm])
            if dst_f32 is not None:
                b.op("dve", lambda e, x=x, n_=n_, rs=rs, nm=nm: e.tensor_scalar(n_[:], x[:], rs[:, 0:1], nm[:, 0:1], ALU.mult, ALU.add),
                     reads=[x, rs, nm], writes=[n_])
                b.dma(dst_f32[tc * 128:(tc + 1) * 128, :], n_[:], reads=[n_], writes=[ctx.dbuf(dst_f32_name)])
            if dst_bt is not None:
                b.op("dve", lambda e, x=x, nbt=nbt, rs=rs, nm=nm: e.tensor_scalar(nbt[:], x[:], rs[:, 0:1], nm[:, 0:1], ALU.mult, ALU.add),
                     reads=[x, rs, nm], writes=[nbt])
                for dc in range(NDC):
                    p = ptr[dc % 2]
                    s = stg[dc % 4]
                    b.op("pe", lambda e, nbt=nbt, dc=dc, p=p: e.transpose(p[:], nbt[:, dc * 128:(dc + 1) * 128], idb[:]),
                         reads=[nbt, idb], writes=[p])
                    b.op("dve", lambda e, p=p, s=s: e.tensor_copy(s[:], p[:]), reads=[p], writes=[s])
                    b.dma(dst_bt[dc, :, tc * 128:(tc + 1) * 128], s[:], reads=[s], writes=[ctx.dbuf(dst_bt_name)])


def prep_chunks(b, ctx, bufs=None):
    jobs = [
        ("w1T", "w1t_bf", E * D, F),
        ("w2T", "w2t_bf", E * F, D),
        ("inpT", "inpt_bf", D, 2 * DIN),
        ("outpT", "outpt_bf", DIN, D),
    ]
    if bufs is None:
        stg = [b.sbuf(f"pp_s{i}", [128, 2048], F32, persistent=True) for i in range(3)]
        stb = [b.sbuf(f"pp_b{i}", [128, 2048], BF16, persistent=True) for i in range(3)]
    else:
        stg, stb = bufs
    eng_rr = ["dve", "act"]
    k = 0
    for src_name, dst_name, rows, cols in jobs:
        src = ctx.t[src_name]
        dst = ctx.t[dst_name]
        src2 = src.rearrange(_flat2(src.shape))
        dst2 = dst.rearrange(_flat2(dst.shape))
        cb = min(cols, 2048)
        ncb = cols // cb
        g = max(1, 2048 // cols)
        rpc = 128 * g
        assert rows % rpc == 0 and cols % cb == 0
        for ch in range(rows // rpc):
            for cbi in range(ncb):
                i = k % 3
                eng = eng_rr[k % 2]
                s, sb = stg[i], stb[i]
                w = g * cb
                sview = src2[ch * rpc:(ch + 1) * rpc, cbi * cb:(cbi + 1) * cb].rearrange("(g p) c -> p g c", p=128)
                dview = dst2[ch * rpc:(ch + 1) * rpc, cbi * cb:(cbi + 1) * cb].rearrange("(g p) c -> p g c", p=128)
                b.dma(s[:, 0:w].rearrange("p (g c) -> p g c", g=g), sview, reads=[ctx.dbuf(src_name)], writes=[s], stream="pool")
                b.op("pool", lambda e, s=s, sb=sb, w=w: e.tensor_copy(sb[:, 0:w], s[:, 0:w]), reads=[s], writes=[sb])
                b.dma(dview, sb[:, 0:w].rearrange("p (g c) -> p g c", g=g), reads=[sb], writes=[ctx.dbuf(dst_name)], stream="pool")
                k += 1
                yield


def stage_prep(b, ctx):
    with b.stage("prep"):
        for _ in prep_chunks(b, ctx):
            pass


def _flat2(shape):
    nd = len(shape)
    if nd == 2:
        return "a b -> a b"
    names = [chr(ord('a') + i) for i in range(nd)]
    return f"{' '.join(names)} -> ({' '.join(names[:-1])}) {names[-1]}"


def stage_attn(b, ctx):
    ln0bt = ctx.t["ln0bt"]
    attn = ctx.t["attn"]
    with b.stage("attn"):
        wq_f = b.sbuf("at_wqf", [128, 3 * NDC * HD], F32)
        wq_b = b.sbuf("at_wqb", [128, 3 * NDC * HD], BF16)
        wo_f = b.sbuf("at_wof", [128, D], F32)
        wo_b = b.sbuf("at_wob", [128, NDC * D], BF16)
        lb = b.sbuf("at_lb", [128, NDC * L], BF16)
        qT = b.sbuf("at_qT", [64, L], BF16)
        kT = b.sbuf("at_kT", [64, L], BF16)
        vb = b.sbuf("at_vb", [128, 16 * 65], BF16)
        et = [b.sbuf(f"at_et{i}", [128, 1024], BF16) for i in range(4)]
        stmp = b.sbuf("at_stmp", [1, L], F32)
        s_all = b.sbuf("at_sall", [16, L], F32)
        s_inv = b.sbuf("at_sinv", [16, L], F32)
        s_invb = b.sbuf("at_sinvb", [16, L], BF16)
        ct_un = b.sbuf("at_ctun", [128, NDC * L], BF16)
        ct = ct_un
        sel_b = b.sbuf("at_selb", [32, 32 * 128], BF16)
        wo_stage = [b.sbuf(f"at_wos{i}", [128, D], F32) for i in range(2)]
        p_s = [b.psum(f"at_ps{i}", [128, 1024], F32) for i in range(2)]
        p_o = b.psum("at_po", [65, L], F32)

        b.dma(sel_b[:], ctx.t["sel32"][:], reads=[ctx.dbuf("sel32")], writes=[sel_b])
        for ic in range(NDC):
            b.dma(wo_f[:], ctx.t["w_o"][ic * 128:(ic + 1) * 128, :], reads=[ctx.dbuf("w_o")], writes=[wo_f])
            b.op("dve", lambda e, ic=ic: e.tensor_copy(wo_b[:, ic * D:(ic + 1) * D], wo_f[:]),
                 reads=[wo_f], writes=[wo_b])
        for kc in range(16):
            b.op("dve", lambda e, kc=kc: e.memset(vb[:, kc * 65 + 64:kc * 65 + 65], 1.0), writes=[vb])

        for bb in range(B):
            t0 = bb * L
            for dc in range(NDC):
                b.dma(lb[:, dc * L:(dc + 1) * L], ln0bt[dc, :, t0:t0 + L],
                      reads=[ctx.dbuf("ln0bt")], writes=[lb])
            for h in range(NH):
                b.pump_n(4)
                for a in range(3):
                    wsrc = ctx.t["w_qkv"][h, a].rearrange("(dc p) n -> p dc n", p=128)
                    wdst = wq_f[:, a * NDC * HD:(a + 1) * NDC * HD].rearrange("p (dc n) -> p dc n", dc=NDC)
                    b.dma(wdst, wsrc, reads=[ctx.dbuf("w_qkv")], writes=[wq_f])
                b.op("dve", lambda e: e.tensor_copy(wq_b[:], wq_f[:]), reads=[wq_f], writes=[wq_b])
                for a, dstT, scl in ((0, qT, 0.125), (1, kT, 1.0)):
                    for hb in range(2):
                        ps = p_s[hb]
                        for qb2 in range(2):
                            c0 = qb2 * 512
                            q0 = hb * 1024 + qb2 * 512
                            for dc in range(NDC):
                                b.op("pe", lambda e, a=a, dc=dc, c0=c0, q0=q0, ps=ps: e.matmul(
                                    ps[0:64, c0:c0 + 512],
                                    wq_b[:, (a * NDC + dc) * HD:(a * NDC + dc + 1) * HD],
                                    lb[:, dc * L + q0:dc * L + q0 + 512],
                                    start=(dc == 0), stop=(dc == NDC - 1)),
                                    reads=[wq_b, lb], writes=[ps])
                        b.op("dve", lambda e, dstT=dstT, hb=hb, scl=scl, ps=ps: e.tensor_scalar_mul(
                            dstT[:, hb * 1024:(hb + 1) * 1024], ps[0:64, :], scl), reads=[ps], writes=[dstT])
                for kc in range(16):
                    ps = p_s[kc % 2]
                    for dc in range(NDC):
                        b.op("pe", lambda e, kc=kc, dc=dc, ps=ps: e.matmul(
                            ps[:, 0:64],
                            lb[:, dc * L + kc * 128:dc * L + (kc + 1) * 128],
                            wq_b[:, (2 * NDC + dc) * HD:(2 * NDC + dc + 1) * HD],
                            start=(dc == 0), stop=(dc == NDC - 1)),
                            reads=[lb, wq_b], writes=[ps])
                    b.op("dve", lambda e, kc=kc, ps=ps: e.tensor_copy(vb[:, kc * 65:kc * 65 + 64], ps[:, 0:64]),
                         reads=[ps], writes=[vb])
                for kc in range(16):
                    for hb in range(2):
                        ps = p_s[hb]
                        eti = et[2 * (kc % 2) + hb]
                        for qb2 in range(2):
                            q0 = hb * 1024 + qb2 * 512
                            b.op("pe", lambda e, kc=kc, q0=q0, qb2=qb2, ps=ps: e.matmul(
                                ps[:, qb2 * 512:(qb2 + 1) * 512],
                                kT[:, kc * 128:(kc + 1) * 128],
                                qT[:, q0:q0 + 512],
                                start=True, stop=True), reads=[kT, qT], writes=[ps])
                        b.op("act", lambda e, eti=eti, ps=ps: e.activation(eti[:], ps[:], AF.Exp),
                             reads=[ps], writes=[eti])
                        for qb2 in range(2):
                            q0 = hb * 1024 + qb2 * 512
                            b.op("pe", lambda e, kc=kc, eti=eti, q0=q0, qb2=qb2: e.matmul(
                                p_o[:, q0:q0 + 512],
                                vb[:, kc * 65:(kc + 1) * 65],
                                eti[:, qb2 * 512:(qb2 + 1) * 512],
                                start=(kc == 0), stop=(kc == 15)), reads=[vb, eti], writes=[p_o])
                b.op("act", lambda e: e.copy(stmp[:], p_o[64:65, :]), reads=[p_o], writes=[stmp])
                b.dma(s_all[h:h + 1, :], stmp[:], reads=[stmp], writes=[s_all], tag="s_gather")
                ic, r0 = h // 2, (h % 2) * 64
                b.op("dve", lambda e, ic=ic, r0=r0: e.tensor_copy(ct_un[r0:r0 + 64, ic * L:(ic + 1) * L], p_o[0:64, :]),
                     reads=[p_o], writes=[ct_un])
            b.op("dve", lambda e: e.reciprocal(s_inv[:], s_all[:]), reads=[s_all], writes=[s_inv])
            b.op("dve", lambda e: e.tensor_copy(s_invb[:], s_inv[:]), reads=[s_inv], writes=[s_invb])
            for h in range(NH):
                ic, r0 = h // 2, (h % 2) * 64
                ps = p_s[h % 2]
                for hb in range(2):
                    b.op("pe", lambda e, h=h, hb=hb, ps=ps: e.matmul(
                        ps[0:64, hb * 512:(hb + 1) * 512],
                        sel_b[0:16, h * 128:h * 128 + 64],
                        s_invb[:, hb * 512:(hb + 1) * 512],
                        start=True, stop=True), reads=[sel_b, s_invb], writes=[ps])
                    b.op("pe", lambda e, h=h, hb=hb, ps=ps: e.matmul(
                        ps[64:128, hb * 512:(hb + 1) * 512],
                        sel_b[0:16, h * 128 + 64:h * 128 + 128],
                        s_invb[:, 1024 + hb * 512:1024 + (hb + 1) * 512],
                        start=True, stop=True), reads=[sel_b, s_invb], writes=[ps])
                b.op("dve", lambda e, ic=ic, r0=r0, ps=ps: e.tensor_tensor(
                    ct[r0:r0 + 64, ic * L:ic * L + 1024], ct_un[r0:r0 + 64, ic * L:ic * L + 1024],
                    ps[0:64, :], ALU.mult), reads=[ct_un, ps], writes=[ct])
                b.op("dve", lambda e, ic=ic, r0=r0, ps=ps: e.tensor_tensor(
                    ct[r0:r0 + 64, ic * L + 1024:ic * L + 2048], ct_un[r0:r0 + 64, ic * L + 1024:ic * L + 2048],
                    ps[64:128, :], ALU.mult), reads=[ct_un, ps], writes=[ct])
            for tc in range(16):
                ps = p_s[tc % 2]
                for half in range(2):
                    for ic in range(NDC):
                        b.op("pe", lambda e, tc=tc, ic=ic, half=half, ps=ps: e.matmul(
                            ps[:, half * 512:(half + 1) * 512],
                            ct[:, ic * L + tc * 128:ic * L + tc * 128 + 128],
                            wo_b[:, ic * D + half * 512:ic * D + (half + 1) * 512],
                            start=(ic == 0), stop=(ic == NDC - 1)), reads=[ct, wo_b], writes=[ps])
                ws = wo_stage[tc % 2]
                b.op("dve", lambda e, ws=ws, ps=ps: e.tensor_copy(ws[:], ps[:]), reads=[ps], writes=[ws])
                b.dma(attn[t0 + tc * 128:t0 + (tc + 1) * 128, :], ws[:],
                      reads=[ws], writes=[ctx.dbuf("attn")])


def stage_router(b, ctx):
    ln0bt = ctx.t["ln0bt"]
    acc = ctx.t["acc"]
    with b.stage("router"):
        cw_all = b.sbuf("r_cwall", [128, NTC * E], F32, persistent=True)
        cwt_all = b.sbuf("r_cwtall", [8, NTC * 128], BF16, persistent=True)
        ctx.shared["cw_all"] = cw_all
        ctx.shared["cwt_all"] = cwt_all
        wr_f = b.sbuf("r_wrf", [128, NDC * E], F32)
        wr_b = b.sbuf("r_wrb", [128, NDC * E], BF16)
        br_f = b.sbuf("r_brf", [1, E], F32)
        br_b = b.sbuf("r_brb", [1, E], BF16)
        b2_f = b.sbuf("r_b2f", [E, D], F32)
        b2_b = b.sbuf("r_b2b", [E, D], BF16)
        ones_b = b.sbuf("r_ones", [1, 128], BF16)
        idf = b.sbuf("r_idf", [128, 128], F32)
        lt = [b.sbuf(f"r_lt{i}", [128, NDC * 128], BF16) for i in range(2)]
        lg = b.sbuf("r_lg", [128, E], F32)
        m1 = b.sbuf("r_m1", [128, 1], F32)
        m2 = b.sbuf("r_m2", [128, 1], F32)
        nm1 = b.sbuf("r_nm1", [128, 1], F32)
        msk = b.sbuf("r_msk", [128, E], F32)
        lg2 = b.sbuf("r_lg2", [128, E], F32)
        sel = b.sbuf("r_sel", [128, E], F32)
        expt = b.sbuf("r_expt", [128, E], F32)
        wun = b.sbuf("r_wun", [128, E], F32)
        zz = b.sbuf("r_z", [128, 1], F32)
        zi = b.sbuf("r_zi", [128, 1], F32)
        ctf = b.sbuf("r_ctf", [8, 128], F32)
        at = [b.sbuf(f"r_at{i}", [128, D], F32) for i in range(2)]
        ot = [b.sbuf(f"r_ot{i}", [128, D], F32) for i in range(2)]
        p_r = b.psum("r_pr", [128, 512], F32)
        p_ct = b.psum("r_pct", [8, 128], F32)
        p_b2 = b.psum("r_pb2", [128, D], F32)

        wsrc = ctx.t["wrT"].rearrange("(dc p) e -> p dc e", p=128)
        b.dma(wr_f.rearrange("p (dc e) -> p dc e", dc=NDC), wsrc, reads=[ctx.dbuf("wrT")], writes=[wr_f])
        b.op("dve", lambda e: e.tensor_copy(wr_b[:], wr_f[:]), reads=[wr_f], writes=[wr_b])
        b.dma(br_f[:], ctx.t["br"][:], reads=[ctx.dbuf("br")], writes=[br_f])
        b.op("dve", lambda e: e.tensor_copy(br_b[:], br_f[:]), reads=[br_f], writes=[br_b])
        b.dma(b2_f[:], ctx.t["b2"][:], reads=[ctx.dbuf("b2")], writes=[b2_f])
        b.op("dve", lambda e: e.tensor_copy(b2_b[:], b2_f[:]), reads=[b2_f], writes=[b2_b])
        b.op("dve", lambda e: e.memset(ones_b[:], 1.0), writes=[ones_b])
        b.dma(idf[:], ctx.t["id128"][:], reads=[ctx.dbuf("id128")], writes=[idf])

        for tc in range(NTC):
            lti = lt[tc % 2]
            for dc in range(NDC):
                b.dma(lti[:, dc * 128:(dc + 1) * 128], ln0bt[dc, :, tc * 128:(tc + 1) * 128],
                      reads=[ctx.dbuf("ln0bt")], writes=[lti])
            for dc in range(NDC):
                b.op("pe", lambda e, lti=lti, dc=dc: e.matmul(
                    p_r[:, 0:E], lti[:, dc * 128:(dc + 1) * 128], wr_b[:, dc * E:(dc + 1) * E],
                    start=(dc == 0), stop=False), reads=[lti, wr_b], writes=[p_r])
            b.op("pe", lambda e: e.matmul(p_r[:, 0:E], ones_b[:], br_b[:], start=False, stop=True),
                 reads=[ones_b, br_b], writes=[p_r])
            b.op("dve", lambda e: e.tensor_copy(lg[:], p_r[:, 0:E]), reads=[p_r], writes=[lg])
            b.op("dve", lambda e: e.reduce_max(m1[:], lg[:], axis=AX.X), reads=[lg], writes=[m1])
            b.op("dve", lambda e: e.tensor_scalar(msk[:], lg[:], m1[:, 0:1], None, ALU.is_ge), reads=[lg, m1], writes=[msk])
            b.op("dve", lambda e: e.scalar_tensor_tensor(lg2[:], msk[:], -1e30, lg[:], ALU.mult, ALU.add),
                 reads=[msk, lg], writes=[lg2])
            b.op("dve", lambda e: e.reduce_max(m2[:], lg2[:], axis=AX.X), reads=[lg2], writes=[m2])
            b.op("dve", lambda e: e.tensor_scalar(sel[:], lg[:], m2[:, 0:1], None, ALU.is_ge), reads=[lg, m2], writes=[sel])
            b.op("dve", lambda e: e.tensor_scalar_mul(nm1[:], m1[:], -1.0), reads=[m1], writes=[nm1])
            b.op("act", lambda e: e.activation(expt[:], lg[:], AF.Exp, bias=nm1[:, 0:1]), reads=[lg, nm1], writes=[expt])
            b.op("dve", lambda e: e.tensor_mul(wun[:], sel[:], expt[:]), reads=[sel, expt], writes=[wun])
            b.op("dve", lambda e: e.reduce_sum(zz[:], wun[:], axis=AX.X), reads=[wun], writes=[zz])
            b.op("dve", lambda e: e.reciprocal(zi[:], zz[:]), reads=[zz], writes=[zi])
            b.op("dve", lambda e, tc=tc: e.tensor_scalar(cw_all[:, tc * E:(tc + 1) * E], wun[:], zi[:, 0:1], None, ALU.mult),
                 reads=[wun, zi], writes=[cw_all])
            b.op("pe", lambda e, tc=tc: e.transpose(p_ct[:], cw_all[:, tc * E:(tc + 1) * E], idf[:]),
                 reads=[cw_all, idf], writes=[p_ct])
            b.op("dve", lambda e, tc=tc: e.tensor_copy(cwt_all[:, tc * 128:(tc + 1) * 128], p_ct[:]),
                 reads=[p_ct], writes=[cwt_all])
            ati, oti = at[tc % 2], ot[tc % 2]
            b.dma(ati[:], acc[tc * 128:(tc + 1) * 128, :], reads=[ctx.dbuf("acc", str(tc))], writes=[ati])
            for half in range(2):
                b.op("pe", lambda e, tc=tc, half=half: e.matmul(
                    p_b2[:, half * 512:(half + 1) * 512], cwt_all[:, tc * 128:(tc + 1) * 128],
                    b2_b[:, half * 512:(half + 1) * 512], start=True, stop=True),
                    reads=[cwt_all, b2_b], writes=[p_b2])
            b.op("dve", lambda e, ati=ati, oti=oti: e.tensor_add(oti[:], ati[:], p_b2[:]),
                 reads=[ati, p_b2], writes=[oti])
            b.dma(acc[tc * 128:(tc + 1) * 128, :], oti[:], reads=[oti], writes=[ctx.dbuf("acc", str(tc))])


def stage_experts(b, ctx):
    ln0bt = ctx.t["ln0bt"]
    acc = ctx.t["acc"]
    w1t = ctx.t["w1t_bf"]
    w2t = ctx.t["w2t_bf"]
    cw_all = ctx.shared["cw_all"]
    with b.stage("experts"):
        b1_sb = b.sbuf("e_b1", [128, E * NFC], F32)
        w2_b = b.sbuf("e_w2", [128, NFC * D], BF16)
        w1_sl = [b.sbuf(f"e_w1{i}", [128, NDC * 512], BF16) for i in range(2)]
        lt = [b.sbuf(f"e_lt{i}", [128, NDC * 1024], BF16) for i in range(2)]
        h_bf = b.sbuf("e_h", [128, NFC * 1024], BF16)
        at = [b.sbuf(f"e_at{i}", [128, D], F32) for i in range(2)]
        ot = [b.sbuf(f"e_ot{i}", [128, D], F32) for i in range(2)]
        p_h = [b.psum(f"e_ph{i}", [128, 1024], F32) for i in range(2)]
        p_y = [b.psum(f"e_py{i}", [128, 1024], F32) for i in range(2)]

        b.dma(b1_sb[:], ctx.t["b1r"][:], reads=[ctx.dbuf("b1r")], writes=[b1_sb])

        for ee in range(E):
            for q4 in range(4):
                nsub = NFC // 4
                wsrc = w2t[ee, q4 * nsub * 128:(q4 + 1) * nsub * 128, :].rearrange("(fc p) o -> p fc o", p=128)
                wdst = w2_b[:, q4 * nsub * D:(q4 + 1) * nsub * D].rearrange("p (fc o) -> p fc o", fc=nsub)
                b.dma(wdst, wsrc, reads=[ctx.dbuf("w2t_bf")], writes=[w2_b])
            for tblk in range(4):
                lti = lt[tblk % 2]
                b.dma(lti.rearrange("p (dc t) -> p dc t", dc=NDC),
                      ln0bt[:, :, tblk * 1024:(tblk + 1) * 1024].rearrange("dc p t -> p dc t"),
                      reads=[ctx.dbuf("ln0bt")], writes=[lti])
                for fg in range(8):
                    w1i = w1_sl[fg % 2]
                    b.dma(w1i.rearrange("p (dc f) -> p dc f", dc=NDC),
                          w1t[ee].rearrange("(dc p) f -> p dc f", p=128)[:, :, fg * 512:(fg + 1) * 512],
                          reads=[ctx.dbuf("w1t_bf")], writes=[w1i])
                    for fcl in range(4):
                        fc = fg * 4 + fcl
                        p = p_h[fc % 2]
                        for dc in range(NDC):
                            for h2 in range(2):
                                b.op("pe", lambda e, w1i=w1i, lti=lti, p=p, dc=dc, h2=h2, fcl=fcl: e.matmul(
                                    p[:, h2 * 512:(h2 + 1) * 512],
                                    w1i[:, dc * 512 + fcl * 128:dc * 512 + (fcl + 1) * 128],
                                    lti[:, dc * 1024 + h2 * 512:dc * 1024 + (h2 + 1) * 512],
                                    start=(dc == 0), stop=(dc == NDC - 1)),
                                    reads=[w1i, lti], writes=[p])
                        b.op("act", lambda e, p=p, fc=fc, ee=ee: e.activation(
                            h_bf[:, fc * 1024:(fc + 1) * 1024], p[:], AF.Gelu,
                            bias=b1_sb[:, ee * NFC + fc:ee * NFC + fc + 1]),
                            reads=[p, b1_sb], writes=[h_bf])
                for tc8 in range(8):
                    tgl = tblk * 8 + tc8
                    p = p_y[tc8 % 2]
                    ati, oti = at[tc8 % 2], ot[tc8 % 2]
                    b.dma(ati[:], acc[tgl * 128:(tgl + 1) * 128, :], reads=[ctx.dbuf("acc", str(tgl))], writes=[ati])
                    for fc in range(NFC):
                        for h2 in range(2):
                            b.op("pe", lambda e, p=p, fc=fc, tc8=tc8, h2=h2: e.matmul(
                                p[:, h2 * 512:(h2 + 1) * 512],
                                h_bf[:, fc * 1024 + tc8 * 128:fc * 1024 + tc8 * 128 + 128],
                                w2_b[:, fc * D + h2 * 512:fc * D + (h2 + 1) * 512],
                                start=(fc == 0), stop=(fc == NFC - 1)),
                                reads=[h_bf, w2_b], writes=[p])
                    b.op("dve", lambda e, p=p, ati=ati, oti=oti, tgl=tgl, ee=ee: e.scalar_tensor_tensor(
                        oti[:], p[:], cw_all[:, tgl * E + ee:tgl * E + ee + 1], ati[:], ALU.mult, ALU.add),
                        reads=[p, cw_all, ati], writes=[oti])
                    b.dma(acc[tgl * 128:(tgl + 1) * 128, :], oti[:], reads=[oti], writes=[ctx.dbuf("acc", str(tgl))])


def stage_assembly(b, ctx):
    acc = ctx.t["acc"]
    attn = ctx.t["attn"]
    x = ctx.t["x"].rearrange("b l d -> (b l) d")
    x1 = ctx.t["x1"]
    ln1bt = ctx.t["ln1bt"]
    with b.stage("asm"):
        lnw_f = b.sbuf("a_lnw", [1, D], F32)
        lnb_f = b.sbuf("a_lnb", [1, D], F32)
        ones_f = b.sbuf("a_ones", [1, 128], F32)
        wb = b.sbuf("a_wb", [128, D], F32)
        bb_ = b.sbuf("a_bb", [128, D], F32)
        idb = b.sbuf("a_idb", [128, 128], BF16)
        idf = b.sbuf("a_idf", [128, 128], F32)
        zt = [b.sbuf(f"a_z{i}", [128, D], F32) for i in range(3)]
        att = [b.sbuf(f"a_at{i}", [128, D], F32) for i in range(3)]
        xt = [b.sbuf(f"a_x{i}", [128, D], F32) for i in range(3)]
        n2l = [b.sbuf(f"a_n2{i}", [128, D], F32) for i in range(3)]
        z2l = [b.sbuf(f"a_z2{i}", [128, D], F32) for i in range(3)]
        x1t = [b.sbuf(f"a_x1{i}", [128, D], F32) for i in range(3)]
        nb = [b.sbuf(f"a_nb{i}", [128, D], BF16) for i in range(3)]
        stg = [b.sbuf(f"a_stg{i}", [128, 128], BF16) for i in range(4)]
        bn6 = [b.sbuf(f"a_bn6{i}", [128, 12], F32) for i in range(3)]
        bn2 = [b.sbuf(f"a_bn2{i}", [128, 2], F32) for i in range(3)]
        rstd = [b.sbuf(f"a_rstd{i}", [128, 1], F32) for i in range(3)]
        nmr = [b.sbuf(f"a_nmr{i}", [128, 1], F32) for i in range(3)]
        p_w = b.psum("a_pw", [128, D], F32)
        ptr = [b.psum(f"a_ptr{i}", [128, 128], BF16) for i in range(2)]

        b.dma(idf[:], ctx.t["id128"][:], reads=[ctx.dbuf("id128")], writes=[idf])
        b.op("dve", lambda e: e.tensor_copy(idb[:], idf[:]), reads=[idf], writes=[idb])
        b.dma(lnw_f[:], ctx.t["ln_w"][:], reads=[ctx.dbuf("ln_w")], writes=[lnw_f])
        b.dma(lnb_f[:], ctx.t["ln_b"][:], reads=[ctx.dbuf("ln_b")], writes=[lnb_f])
        b.op("dve", lambda e: e.memset(ones_f[:], 1.0), writes=[ones_f])
        for half in range(2):
            b.op("pe", lambda e, half=half: e.matmul(
                p_w[:, half * 512:(half + 1) * 512], ones_f[:], lnw_f[:, half * 512:(half + 1) * 512],
                start=True, stop=True), reads=[ones_f, lnw_f], writes=[p_w])
        b.op("dve", lambda e: e.tensor_copy(wb[:], p_w[:]), reads=[p_w], writes=[wb])
        for half in range(2):
            b.op("pe", lambda e, half=half: e.matmul(
                p_w[:, half * 512:(half + 1) * 512], ones_f[:], lnb_f[:, half * 512:(half + 1) * 512],
                start=True, stop=True), reads=[ones_f, lnb_f], writes=[p_w])
        b.op("dve", lambda e: e.tensor_copy(bb_[:], p_w[:]), reads=[p_w], writes=[bb_])

        ceps2 = b.constant(1e-12)
        ceps1 = b.constant(1e-5)
        for tc in range(NTC):
            i = tc % 3
            z, a_, xx, x1i, nbt = zt[i], att[i], xt[i], x1t[i], nb[i]
            n2, z2 = n2l[i], z2l[i]
            b6, b2_, rs, nm = bn6[i], bn2[i], rstd[i], nmr[i]
            b.dma(z[:], acc[tc * 128:(tc + 1) * 128, :], reads=[ctx.dbuf("acc", str(tc))], writes=[z])
            b.dma(a_[:], attn[tc * 128:(tc + 1) * 128, :], reads=[ctx.dbuf("attn")], writes=[a_])
            b.dma(xx[:], x[tc * 128:(tc + 1) * 128, :], reads=[ctx.dbuf("x")], writes=[xx])
            b.op("dve", lambda e, z=z, b6=b6: e.bn_stats(b6[:, 0:6], z[:, 0:512]), reads=[z], writes=[b6])
            b.op("dve", lambda e, z=z, b6=b6: e.bn_stats(b6[:, 6:12], z[:, 512:1024]), reads=[z], writes=[b6])
            b.op("dve", lambda e, b6=b6, b2_=b2_: e.bn_aggr(b2_[:], b6[:]), reads=[b6], writes=[b2_])
            b.op("act", lambda e, b2_=b2_, rs=rs: e.activation(rs[:], b2_[:, 1:2], AF.Sqrt, bias=ceps2[:, 0:1]),
                 reads=[b2_, ceps2], writes=[rs])
            b.op("dve", lambda e, rs=rs: e.reciprocal(rs[:], rs[:]), reads=[rs], writes=[rs])
            b.op("dve", lambda e, b2_=b2_, rs=rs, nm=nm: e.scalar_tensor_tensor(nm[:], b2_[:, 0:1], -1.0, rs[:], ALU.mult, ALU.mult),
                 reads=[b2_, rs], writes=[nm])
            b.op("dve", lambda e, z=z, rs=rs, nm=nm, n2=n2: e.tensor_scalar(n2[:], z[:], rs[:, 0:1], nm[:, 0:1], ALU.mult, ALU.add),
                 reads=[z, rs, nm], writes=[n2])
            b.op("dve", lambda e, n2=n2, z2=z2: e.tensor_mul(z2[:], n2[:], wb[:]), reads=[n2, wb], writes=[z2])
            b.op("pool", lambda e, z2=z2: e.tensor_add(z2[:], z2[:], bb_[:]), reads=[z2, bb_], writes=[z2])
            b.op("pool", lambda e, a_=a_, xx=xx: e.tensor_add(xx[:], xx[:], a_[:]), reads=[xx, a_], writes=[xx])
            b.op("dve", lambda e, xx=xx, x1i=x1i, z2=z2: e.tensor_add(x1i[:], xx[:], z2[:]), reads=[xx, z2], writes=[x1i])
            b.dma(x1[tc * 128:(tc + 1) * 128, :], x1i[:], reads=[x1i], writes=[ctx.dbuf("x1")])
            b.op("dve", lambda e, x1i=x1i, b6=b6: e.bn_stats(b6[:, 0:6], x1i[:, 0:512]), reads=[x1i], writes=[b6])
            b.op("dve", lambda e, x1i=x1i, b6=b6: e.bn_stats(b6[:, 6:12], x1i[:, 512:1024]), reads=[x1i], writes=[b6])
            b.op("dve", lambda e, b6=b6, b2_=b2_: e.bn_aggr(b2_[:], b6[:]), reads=[b6], writes=[b2_])
            b.op("act", lambda e, b2_=b2_, rs=rs: e.activation(rs[:], b2_[:, 1:2], AF.Sqrt, bias=ceps1[:, 0:1]),
                 reads=[b2_, ceps1], writes=[rs])
            b.op("dve", lambda e, rs=rs: e.reciprocal(rs[:], rs[:]), reads=[rs], writes=[rs])
            b.op("dve", lambda e, b2_=b2_, rs=rs, nm=nm: e.scalar_tensor_tensor(nm[:], b2_[:, 0:1], -1.0, rs[:], ALU.mult, ALU.mult),
                 reads=[b2_, rs], writes=[nm])
            b.op("dve", lambda e, x1i=x1i, nbt=nbt, rs=rs, nm=nm: e.tensor_scalar(nbt[:], x1i[:], rs[:, 0:1], nm[:, 0:1], ALU.mult, ALU.add),
                 reads=[x1i, rs, nm], writes=[nbt])
            for dc in range(NDC):
                p = ptr[dc % 2]
                s = stg[dc % 4]
                b.op("pe", lambda e, nbt=nbt, dc=dc, p=p: e.transpose(p[:], nbt[:, dc * 128:(dc + 1) * 128], idb[:]),
                     reads=[nbt, idb], writes=[p])
                b.op("dve", lambda e, p=p, s=s: e.tensor_copy(s[:], p[:]), reads=[p], writes=[s])
                b.dma(ln1bt[dc, :, tc * 128:(tc + 1) * 128], s[:], reads=[s], writes=[ctx.dbuf("ln1bt")])


def stage_lilora(b, ctx):
    x1 = ctx.t["x1"]
    ln1bt = ctx.t["ln1bt"]
    out = ctx.t["out"].rearrange("b l d -> (b l) d")
    with b.stage("lilora"):
        ldt_f = b.sbuf("l_ldt", [100, 4], F32)
        ldt_b = b.sbuf("l_ldtb", [100, 4], BF16)
        lut_f = b.sbuf("l_lut", [4, 50], F32)
        lut_b = b.sbuf("l_lutb", [4, 50], BF16)
        da_f = b.sbuf("l_da", [100, D], F32)
        da_b = b.sbuf("l_dab", [100, D], BF16)
        ua_f = b.sbuf("l_ua", [50, D], F32)
        ua_b = b.sbuf("l_uab", [50, D], BF16)
        a1_b = b.sbuf("l_a1b", [4, D], BF16)
        a2_b = b.sbuf("l_a2b", [50, D], BF16)
        wt_f = [b.sbuf(f"l_wt{i}", [128, D], F32) for i in range(2)]
        weff = b.sbuf("l_weff", [128, NDC * D], BF16)
        bl_f = b.sbuf("l_blf", [1, D], F32)
        ones_f = b.sbuf("l_ones", [1, 128], F32)
        bb2 = b.sbuf("l_bb2", [128, D], F32)
        ltb = [b.sbuf(f"l_ltb{i}", [128, NDC * 128], BF16) for i in range(3)]
        x1t = [b.sbuf(f"l_x1{i}", [128, D], F32) for i in range(3)]
        ot = [b.sbuf(f"l_ot{i}", [128, D], F32) for i in range(3)]
        p_a = b.psum("l_pa", [128, D], F32)
        p_o = [b.psum(f"l_po{i}", [128, 512], F32) for i in range(2)]

        b.dma(ldt_f[:], ctx.t["ldT"][:], reads=[ctx.dbuf("ldT")], writes=[ldt_f])
        b.dma(lut_f[:], ctx.t["luT"][:], reads=[ctx.dbuf("luT")], writes=[lut_f])
        b.dma(da_f[:], ctx.t["daux"][:], reads=[ctx.dbuf("daux")], writes=[da_f])
        b.dma(ua_f[:], ctx.t["uauxT"][:], reads=[ctx.dbuf("uauxT")], writes=[ua_f])
        b.dma(bl_f[:], ctx.t["bL1"][:], reads=[ctx.dbuf("bL1")], writes=[bl_f])
        for t_, s_ in ((ldt_b, ldt_f), (lut_b, lut_f), (da_b, da_f), (ua_b, ua_f)):
            b.op("dve", lambda e, t_=t_, s_=s_: e.tensor_copy(t_[:], s_[:]), reads=[s_], writes=[t_])
        b.op("dve", lambda e: e.memset(ones_f[:], 1.0), writes=[ones_f])
        for half in range(2):
            b.op("pe", lambda e, half=half: e.matmul(
                p_a[:, half * 512:(half + 1) * 512], ones_f[:], bl_f[:, half * 512:(half + 1) * 512],
                start=True, stop=True), reads=[ones_f, bl_f], writes=[p_a])
        b.op("dve", lambda e: e.tensor_copy(bb2[:], p_a[:]), reads=[p_a], writes=[bb2])
        for half in range(2):
            b.op("pe", lambda e, half=half: e.matmul(
                p_a[0:4, half * 512:(half + 1) * 512], ldt_b[:], da_b[:, half * 512:(half + 1) * 512],
                start=True, stop=True), reads=[ldt_b, da_b], writes=[p_a])
        b.op("dve", lambda e: e.tensor_copy(a1_b[:], p_a[0:4, :]), reads=[p_a], writes=[a1_b])
        for half in range(2):
            b.op("pe", lambda e, half=half: e.matmul(
                p_a[0:50, half * 512:(half + 1) * 512], lut_b[:], a1_b[:, half * 512:(half + 1) * 512],
                start=True, stop=True), reads=[lut_b, a1_b], writes=[p_a])
        b.op("dve", lambda e: e.tensor_copy(a2_b[:], p_a[0:50, :]), reads=[p_a], writes=[a2_b])
        for ic in range(NDC):
            wtf = wt_f[ic % 2]
            b.dma(wtf[:], ctx.t["Wt"][ic * 128:(ic + 1) * 128, :], reads=[ctx.dbuf("Wt")], writes=[wtf])
            for half in range(2):
                b.op("pe", lambda e, ic=ic, half=half: e.matmul(
                    p_a[:, half * 512:(half + 1) * 512], a2_b[:, ic * 128:(ic + 1) * 128],
                    ua_b[:, half * 512:(half + 1) * 512], start=True, stop=True),
                    reads=[a2_b, ua_b], writes=[p_a])
            b.op("dve", lambda e, ic=ic, wtf=wtf: e.tensor_add(weff[:, ic * D:(ic + 1) * D], p_a[:], wtf[:]),
                 reads=[p_a, wtf], writes=[weff])
        for tc in range(NTC):
            i = tc % 3
            lti, x1i, oti = ltb[i], x1t[i], ot[i]
            for dc in range(NDC):
                b.dma(lti[:, dc * 128:(dc + 1) * 128], ln1bt[dc, :, tc * 128:(tc + 1) * 128],
                      reads=[ctx.dbuf("ln1bt")], writes=[lti])
            b.dma(x1i[:], x1[tc * 128:(tc + 1) * 128, :], reads=[ctx.dbuf("x1")], writes=[x1i])
            for half in range(2):
                po = p_o[half]
                for ic in range(NDC):
                    b.op("pe", lambda e, lti=lti, po=po, ic=ic, half=half: e.matmul(
                        po[:, 0:512], lti[:, ic * 128:(ic + 1) * 128],
                        weff[:, ic * D + half * 512:ic * D + (half + 1) * 512],
                        start=(ic == 0), stop=(ic == NDC - 1)), reads=[lti, weff], writes=[po])
                b.op("dve", lambda e, po=po, x1i=x1i, oti=oti, half=half: e.tensor_add(
                    oti[:, half * 512:(half + 1) * 512], po[:, 0:512], x1i[:, half * 512:(half + 1) * 512]),
                    reads=[po, x1i], writes=[oti])
            b.op("pool", lambda e, oti=oti: e.tensor_add(oti[:], oti[:], bb2[:]), reads=[oti, bb2], writes=[oti])
            b.dma(out[tc * 128:(tc + 1) * 128, :], oti[:], reads=[oti], writes=[ctx.dbuf("out", str(tc))])


def stage_mamba_pre(b, ctx, bb):
    ln1bt = ctx.t["ln1bt"]
    inpt = ctx.t["inpt_bf"]
    xsb = ctx.t["xsb"]
    zsil = ctx.t["zsil"]
    t0 = bb * L
    with b.stage(f"mpre{bb}"):
        lb = b.sbuf("m_lb", [128, NDC * L], BF16)
        wsl = [b.sbuf(f"m_wsl{i}", [128, NDC * 512], BF16) for i in range(2)]
        xp_f = b.sbuf("m_xpf", [128, NIC * 96], F32)
        cwr = b.sbuf("m_cwr", [128, NIC * DCONV], F32)
        cbr = b.sbuf("m_cbr", [128, NIC], F32)
        xr = b.sbuf("m_xr", [128, L + DCONV - 1], F32)
        cv1 = b.sbuf("m_cv1", [128, L], F32)
        cv2 = b.sbuf("m_cv2", [128, L], F32)
        xs_t = b.sbuf("m_xst", [128, L], F32)
        zs_t = [b.sbuf(f"m_zst{i}", [128, L], F32) for i in range(2)]
        if "m_dt" not in ctx.shared:
            ctx.shared["m_dt"] = b.sbuf("m_dt_p", [64, L], F32, persistent=True)
            ctx.shared["m_bc"] = b.sbuf("m_bc_p", [32, L], BF16, persistent=True)
        p_xz = [b.psum(f"m_pxz{i}", [128, 512], F32) for i in range(2)]
        p_x = b.psum("m_px", [96, L], F32)

        b.dma(cwr[:], ctx.t["conv_wr"][:], reads=[ctx.dbuf("conv_wr")], writes=[cwr])
        b.dma(cbr[:], ctx.t["conv_br"][:], reads=[ctx.dbuf("conv_br")], writes=[cbr])
        xsrc = ctx.t["xpT"].rearrange("(ic p) k -> p ic k", p=128)
        b.dma(xp_f.rearrange("p (ic k) -> p ic k", ic=NIC), xsrc, reads=[ctx.dbuf("xpT")], writes=[xp_f])
        for dc in range(NDC):
            b.dma(lb[:, dc * L:(dc + 1) * L], ln1bt[dc, :, t0:t0 + L], reads=[ctx.dbuf("ln1bt")], writes=[lb])

        for eg in range(8):
            wi = wsl[eg % 2]
            for dc in range(NDC):
                b.dma(wi[:, dc * 512:(dc + 1) * 512],
                      inpt[dc * 128:(dc + 1) * 128, eg * 512:(eg + 1) * 512],
                      reads=[ctx.dbuf("inpt_bf")], writes=[wi])
            for ec in range(4):
                ic_g = eg * 4 + ec
                is_x = ic_g < NIC
                ic = ic_g if is_x else ic_g - NIC
                for lb4 in range(4):
                    p = p_xz[lb4 % 2]
                    for dc in range(NDC):
                        b.op("pe", lambda e, wi=wi, p=p, dc=dc, ec=ec, lb4=lb4: e.matmul(
                            p[:], wi[:, dc * 512 + ec * 128:dc * 512 + (ec + 1) * 128],
                            lb[:, dc * L + lb4 * 512:dc * L + (lb4 + 1) * 512],
                            start=(dc == 0), stop=(dc == NDC - 1)), reads=[wi, lb], writes=[p])
                    if is_x:
                        b.op("dve", lambda e, p=p, lb4=lb4: e.tensor_copy(
                            xr[:, DCONV - 1 + lb4 * 512:DCONV - 1 + (lb4 + 1) * 512], p[:]),
                            reads=[p], writes=[xr])
                    else:
                        zi = zs_t[ec % 2]
                        b.op("act", lambda e, p=p, lb4=lb4, zi=zi: e.activation(
                            zi[:, lb4 * 512:(lb4 + 1) * 512], p[:], AF.Silu), reads=[p], writes=[zi])
                if is_x:
                    b.op("dve", lambda e, ic=ic: e.memset(xr[:, 0:DCONV - 1], 0.0), writes=[xr])
                    b.op("dve", lambda e, ic=ic: e.tensor_scalar(cv1[:], xr[:, 0:L], cwr[:, ic * DCONV:ic * DCONV + 1], None, ALU.mult),
                         reads=[xr, cwr], writes=[cv1])
                    b.op("dve", lambda e, ic=ic: e.scalar_tensor_tensor(cv2[:], xr[:, 1:1 + L], cwr[:, ic * DCONV + 1:ic * DCONV + 2], cv1[:], ALU.mult, ALU.add),
                         reads=[xr, cwr, cv1], writes=[cv2])
                    b.op("dve", lambda e, ic=ic: e.scalar_tensor_tensor(cv1[:], xr[:, 2:2 + L], cwr[:, ic * DCONV + 2:ic * DCONV + 3], cv2[:], ALU.mult, ALU.add),
                         reads=[xr, cwr, cv2], writes=[cv1])
                    b.op("dve", lambda e, ic=ic: e.scalar_tensor_tensor(cv2[:], xr[:, 3:3 + L], cwr[:, ic * DCONV + 3:ic * DCONV + 4], cv1[:], ALU.mult, ALU.add),
                         reads=[xr, cwr, cv1], writes=[cv2])
                    b.op("act", lambda e, ic=ic: e.activation(xs_t[:], cv2[:], AF.Silu, bias=cbr[:, ic:ic + 1]),
                         reads=[cv2, cbr], writes=[xs_t])
                    b.dma(xsb[bb, ic], xs_t[:], reads=[xs_t], writes=[ctx.dbuf("xsb")])
                    for lb4 in range(4):
                        b.op("pe", lambda e, ic=ic, lb4=lb4: e.matmul(
                            p_x[:, lb4 * 512:(lb4 + 1) * 512], xp_f[:, ic * 96:(ic + 1) * 96],
                            xs_t[:, lb4 * 512:(lb4 + 1) * 512],
                            start=(ic == 0), stop=(ic == NIC - 1)), reads=[xp_f, xs_t], writes=[p_x])
                else:
                    b.dma(zsil[bb, ic], zs_t[ec % 2][:], reads=[zs_t[ec % 2]], writes=[ctx.dbuf("zsil")])
        b.op("act", lambda e: e.copy(ctx.shared["m_dt"][:], p_x[0:64, :]), reads=[p_x], writes=[ctx.shared["m_dt"]])
        b.op("dve", lambda e: e.tensor_copy(ctx.shared["m_bc"][:], p_x[64:96, :]), reads=[p_x], writes=[ctx.shared["m_bc"]])


def stage_mamba_scan(b, ctx, bb):
    xsb = ctx.t["xsb"]
    zsil = ctx.t["zsil"]
    outp = ctx.t["outpt_bf"]
    out = ctx.t["out"].rearrange("b l d -> (b l) d")
    t0 = bb * L
    with b.stage(f"mscan{bb}"):
        dtw_f = b.sbuf("s_dtwf", [64, DIN], F32)
        dtbr = b.sbuf("s_dtbr", [128, NIC], F32)
        alog = b.sbuf("s_alog", [128, NIC * DST], F32)
        a_sb = b.sbuf("s_a", [128, NIC * DST], F32)
        dr = b.sbuf("s_dr", [128, NIC], F32)
        sel_b = b.sbuf("s_selb", [32, 32 * 128], BF16)
        ones_c = b.constant(1.0)
        h_st = b.sbuf("s_hst", [128, NIC * DST], F32)
        xs_t = [b.sbuf(f"s_xst{i}", [128, 1024], F32) for i in range(2)]
        zs_t = b.sbuf("s_zst", [128, 1024], F32)
        dta = b.sbuf("s_dta", [128, 1024], F32)
        delta = [b.sbuf(f"s_delta{i}", [128, 1024], F32) for i in range(4)]
        dx = [b.sbuf(f"s_dx{i}", [128, 1024], F32) for i in range(4)]
        da_t2 = [b.sbuf(f"s_dat{i}", [128, 1024], F32) for i in range(2)]
        dbu_t2 = [b.sbuf(f"s_dbut{i}", [128, 1024], F32) for i in range(2)]
        h_t2 = [b.sbuf(f"s_ht{i}", [128, 1024], F32) for i in range(2)]
        hc_t2 = [b.sbuf(f"s_hct{i}", [128, 1024], F32) for i in range(2)]
        hc_t = hc_t2[0]
        y_ac = [b.sbuf(f"s_y{i}", [128, 1024], F32) for i in range(4)]
        y_b = b.sbuf("s_yb", [128, NIC * 1024], BF16)
        ot = [b.sbuf(f"s_ot{i}", [128, 512], F32) for i in range(2)]
        wo_half = b.sbuf("s_wohalf", [128, NIC * 512], BF16)
        pbm_s = [b.sbuf(f"s_pbms{i}", [128, 1024], F32) for i in range(2)]
        pcm_s = [b.sbuf(f"s_pcms{i}", [128, 1024], F32) for i in range(2)]
        p_d = b.psum("s_pd", [128, 1024], F32)
        p_bm = b.psum("s_pbm", [128, 1024], F32)
        p_cm = b.psum("s_pcm", [128, 1024], F32)
        p_op = b.psum("s_pop", [128, 512], F32)

        b.dma(dtw_f[:], ctx.t["dtwT"][:], reads=[ctx.dbuf("dtwT")], writes=[dtw_f])
        b.dma(dtbr[:], ctx.t["dtbr"][:], reads=[ctx.dbuf("dtbr")], writes=[dtbr])
        b.dma(alog[:], ctx.t["alogr"][:], reads=[ctx.dbuf("alogr")], writes=[alog])
        b.dma(dr[:], ctx.t["Dr"][:], reads=[ctx.dbuf("Dr")], writes=[dr])
        b.dma(sel_b[:], ctx.t["sel32"][:], reads=[ctx.dbuf("sel32")], writes=[sel_b])
        b.op("act", lambda e: e.activation(a_sb[:], alog[:], AF.Exp), reads=[alog], writes=[a_sb])
        b.op("dve", lambda e: e.tensor_scalar_mul(a_sb[:], a_sb[:], -1.0), reads=[a_sb], writes=[a_sb])
        b.op("dve", lambda e: e.memset(h_st[:], 0.0), writes=[h_st])

        m_dt = ctx.shared["m_dt"]
        m_bc = ctx.shared["m_bc"]
        cbias = b.constant(1.0)

        for lh in range(2):
            l0 = lh * 1024
            for ds in range(4):
                for j in range(4):
                    ic = ds * 4 + j
                    xst = xs_t[j % 2]
                    b.dma(xst[:], xsb[bb, ic, :, l0:l0 + 1024], reads=[ctx.dbuf("xsb")], writes=[xst])
                    dlt = delta[j]
                    dxj = dx[j]
                    for h2 in range(2):
                        b.op("pe", lambda e, ic=ic, h2=h2: e.matmul(
                            p_d[:, h2 * 512:(h2 + 1) * 512], dtw_f[:, ic * 128:(ic + 1) * 128],
                            m_dt[:, l0 + h2 * 512:l0 + (h2 + 1) * 512], start=True, stop=True),
                            reads=[dtw_f, m_dt], writes=[p_d])
                    b.op("dve", lambda e, ic=ic: e.tensor_scalar(dta[:], p_d[:], dtbr[:, ic:ic + 1], None, ALU.add),
                         reads=[p_d, dtbr], writes=[dta])
                    b.op("act", lambda e, dxj=dxj: e.activation(dxj[:], dta[:], AF.Abs), reads=[dta], writes=[dxj])
                    b.op("act", lambda e, dxj=dxj, dlt=dlt: e.activation(dlt[:], dxj[:], AF.Exp, scale=-1.0), reads=[dxj], writes=[dlt])
                    b.op("act", lambda e, dlt=dlt, dxj=dxj: e.activation(dxj[:], dlt[:], AF.Ln, bias=cbias[:, 0:1]), reads=[dlt, cbias], writes=[dxj])
                    b.op("dve", lambda e, dlt=dlt: e.tensor_scalar_max(dlt[:], dta[:], 0.0), reads=[dta], writes=[dlt])
                    b.op("dve", lambda e, dlt=dlt, dxj=dxj: e.tensor_add(dlt[:], dlt[:], dxj[:]), reads=[dlt, dxj], writes=[dlt])
                    b.op("dve", lambda e, dlt=dlt, dxj=dxj, xst=xst: e.tensor_mul(dxj[:], dlt[:], xst[:]), reads=[dlt, xst], writes=[dxj])
                    b.op("dve", lambda e, j=j: e.memset(y_ac[j][:], 0.0), writes=[y_ac[j]])
                for n in range(DST):
                    for h2 in range(2):
                        b.op("pe", lambda e, n=n, h2=h2: e.matmul(
                            p_bm[:, h2 * 512:(h2 + 1) * 512], sel_b[:, n * 128:(n + 1) * 128],
                            m_bc[:, l0 + h2 * 512:l0 + (h2 + 1) * 512], start=True, stop=True),
                            reads=[sel_b, m_bc], writes=[p_bm])
                        b.op("pe", lambda e, n=n, h2=h2: e.matmul(
                            p_cm[:, h2 * 512:(h2 + 1) * 512], sel_b[:, (16 + n) * 128:(17 + n) * 128],
                            m_bc[:, l0 + h2 * 512:l0 + (h2 + 1) * 512], start=True, stop=True),
                            reads=[sel_b, m_bc], writes=[p_cm])
                    pb, pc = pbm_s[n % 2], pcm_s[n % 2]
                    b.op("act", lambda e, pb=pb: e.copy(pb[:], p_bm[:]), reads=[p_bm], writes=[pb])
                    b.op("act", lambda e, pc=pc: e.copy(pc[:], p_cm[:]), reads=[p_cm], writes=[pc])
                    for j in range(4):
                        ic = ds * 4 + j
                        col = ic * DST + n
                        p2 = j % 2
                        da_t, dbu_t, h_t, hc_t = da_t2[p2], dbu_t2[p2], h_t2[p2], hc_t2[p2]
                        b.op("act", lambda e, j=j, col=col, da_t=da_t: e.activation(
                            da_t[:], delta[j][:], AF.Exp, scale=a_sb[:, col:col + 1]),
                            reads=[delta[j], a_sb], writes=[da_t])
                        b.op("pool", lambda e, j=j, dbu_t=dbu_t, pb=pb: e.tensor_tensor(dbu_t[:], dx[j][:], pb[:], ALU.mult),
                             reads=[dx[j], pb], writes=[dbu_t])
                        b.op("dve", lambda e, col=col, h_t=h_t, da_t=da_t, dbu_t=dbu_t: e.tensor_tensor_scan(
                            h_t[:], da_t[:], dbu_t[:], h_st[:, col:col + 1], ALU.mult, ALU.add),
                            reads=[da_t, dbu_t, h_st], writes=[h_t])
                        b.op("dve", lambda e, col=col, h_t=h_t: e.tensor_copy(h_st[:, col:col + 1], h_t[:, 1023:1024]),
                             reads=[h_t], writes=[h_st])
                        b.op("dve", lambda e, h_t=h_t, hc_t=hc_t, pc=pc: e.tensor_tensor(hc_t[:], h_t[:], pc[:], ALU.mult),
                             reads=[h_t, pc], writes=[hc_t])
                        b.op("pool", lambda e, j=j, hc_t=hc_t: e.tensor_add(y_ac[j][:], y_ac[j][:], hc_t[:]),
                             reads=[y_ac[j], hc_t], writes=[y_ac[j]])
                for j in range(4):
                    ic = ds * 4 + j
                    xst = xs_t[j % 2]
                    b.dma(zs_t[:], zsil[bb, ic, :, l0:l0 + 1024], reads=[ctx.dbuf("zsil")], writes=[zs_t])
                    b.dma(xst[:], xsb[bb, ic, :, l0:l0 + 1024], reads=[ctx.dbuf("xsb")], writes=[xst])
                    b.op("dve", lambda e, j=j, ic=ic, xst=xst: e.scalar_tensor_tensor(
                        hc_t[:], xst[:], dr[:, ic:ic + 1], y_ac[j][:], ALU.mult, ALU.add),
                        reads=[xst, dr, y_ac[j]], writes=[hc_t])
                    b.op("dve", lambda e, ic=ic: e.tensor_mul(
                        y_b[:, ic * 1024:(ic + 1) * 1024], hc_t[:], zs_t[:]),
                        reads=[hc_t, zs_t], writes=[y_b])
            for oh in range(2):
                for ic in range(NIC):
                    b.dma(wo_half[:, ic * 512:(ic + 1) * 512],
                          outp[ic * 128:(ic + 1) * 128, oh * 512:(oh + 1) * 512],
                          reads=[ctx.dbuf("outpt_bf")], writes=[wo_half])
                for ls in range(8):
                    tg = t0 + l0 + ls * 128
                    oti = ot[ls % 2]
                    b.dma(oti[:, 0:512], out[tg:tg + 128, oh * 512:(oh + 1) * 512],
                          reads=[ctx.dbuf("out", str(tg // 128) + "_" + str(oh))], writes=[oti])
                    for ic in range(NIC):
                        b.op("pe", lambda e, ic=ic, ls=ls: e.matmul(
                            p_op[:, 0:512],
                            y_b[:, ic * 1024 + ls * 128:ic * 1024 + ls * 128 + 128],
                            wo_half[:, ic * 512:(ic + 1) * 512],
                            start=(ic == 0), stop=(ic == NIC - 1)), reads=[y_b, wo_half], writes=[p_op])
                    b.op("dve", lambda e, oti=oti: e.tensor_add(oti[:, 0:512], oti[:, 0:512], p_op[:, 0:512]),
                         reads=[p_op], writes=[oti])
                    b.dma(out[tg:tg + 128, oh * 512:(oh + 1) * 512], oti[:, 0:512],
                          reads=[oti], writes=[ctx.dbuf("out", str(tg // 128) + "_" + str(oh))])


STAGE_DEFS = {}


def stage_l0front(b, ctx):
    es_p = ExitStack()
    stg = [TT(es_p.enter_context(b.nc.sbuf_tensor(b._uname("pp_s"), [128, 2048], F32, side="right")), Buf(f"pp_s{i}")) for i in range(3)]
    stb = [TT(es_p.enter_context(b.nc.sbuf_tensor(b._uname("pp_b"), [128, 2048], BF16, side="right")), Buf(f"pp_b{i}")) for i in range(3)]
    b.pump = prep_chunks(b, ctx, bufs=(stg, stb))
    stage_ln(b, ctx, "x", "acc", "ln0bt")
    stage_attn(b, ctx)
    if b.pump is not None:
        with b.stage("prepdrain"):
            for _ in b.pump:
                pass
        b.pump = None
    es_p.close()


def _register_stages():
    STAGE_DEFS.clear()
    STAGE_DEFS["s0"] = (stage_prep, (), ("w1t_bf", "w2t_bf", "inpt_bf", "outpt_bf"), False)
    STAGE_DEFS["l0f"] = (stage_l0front, (),
                         ("acc", "ln0bt", "attn", "w1t_bf", "w2t_bf", "inpt_bf", "outpt_bf"), False)
    STAGE_DEFS["s1"] = (lambda b, ctx: stage_ln(b, ctx, "x", "acc", "ln0bt"), (), ("acc", "ln0bt"), False)
    STAGE_DEFS["s2"] = (stage_attn, ("ln0bt",), ("attn",), False)
    STAGE_DEFS["s3r"] = (stage_router, ("ln0bt", "acc"), ("acc",), False)
    STAGE_DEFS["s3e"] = (stage_experts, ("ln0bt", "acc", "w1t_bf", "w2t_bf"), ("acc",), False)
    STAGE_DEFS["s4"] = (stage_assembly, ("acc", "attn"), ("x1", "ln1bt"), False)
    def tail1(b, ctx):
        with b.stage("tail1"):
            stage_assembly(b, ctx)
            stage_lilora(b, ctx)
    STAGE_DEFS["tail1"] = (tail1, ("acc", "attn"), ("x1", "ln1bt"), True)
    STAGE_DEFS["s6"] = (stage_lilora, ("x1", "ln1bt"), (), True)
    STAGE_DEFS["m0p"] = (lambda b, ctx: stage_mamba_pre(b, ctx, 0), ("ln1bt", "inpt_bf"), ("xsb", "zsil"), False)
    STAGE_DEFS["m0s"] = (lambda b, ctx: stage_mamba_scan(b, ctx, 0), ("xsb", "zsil", "outpt_bf"), (), True)
    STAGE_DEFS["m1p"] = (lambda b, ctx: stage_mamba_pre(b, ctx, 1), ("ln1bt", "inpt_bf"), ("xsb", "zsil"), False)
    STAGE_DEFS["m1s"] = (lambda b, ctx: stage_mamba_scan(b, ctx, 1), ("xsb", "zsil", "outpt_bf"), (), True)


_register_stages()


def build_program(stage_names):
    nc = bass.Bass("TRN2", target_bir_lowering=False)
    inter_in, inter_out, with_out = [], [], False
    produced = set()
    for sn in stage_names:
        fn, ii, io, wo = STAGE_DEFS[sn]
        for nm in ii:
            if nm not in produced and nm not in inter_in:
                inter_in.append(nm)
        for nm in io:
            produced.add(nm)
            if nm not in inter_out:
                inter_out.append(nm)
        with_out = with_out or wo
    if FUSED:
        pass
    assert not (set(inter_in) & set(inter_out)), (inter_in, inter_out)
    internal = FUSED and with_out
    ctx = Ctx(nc, inter_in=inter_in, inter_out=inter_out, with_out=with_out, internal=internal)
    if internal:
        inter_out = []
    es = ExitStack()
    with es:
        b = Builder(nc, es)
        for v in (1e-5, 1e-12, 1.0):
            b.constant(v)
        for sn in stage_names:
            STAGE_DEFS[sn][0](b, ctx)
    return nc, ctx, inter_in, inter_out, with_out


def prep_host_inputs(inputs):
    g = {k: np.asarray(v) for k, v in inputs.items()}
    shared = {}
    shared["w_qkv"] = np.ascontiguousarray(g["L0_attn_w_qkv"])
    shared["w_o"] = np.ascontiguousarray(g["L0_attn_w_o"])
    shared["wrT"] = np.ascontiguousarray(g["L0_ffn_Wr"].T)
    shared["br"] = np.ascontiguousarray(g["L0_ffn_br"].reshape(1, E))
    shared["w1T"] = np.ascontiguousarray(np.transpose(g["L0_ffn_W1"], (0, 2, 1)))
    shared["b1r"] = np.ascontiguousarray(g["L0_ffn_b1"].reshape(E, NFC, 128).transpose(2, 0, 1).reshape(128, E * NFC))
    shared["w2T"] = np.ascontiguousarray(np.transpose(g["L0_ffn_W2"], (0, 2, 1)))
    shared["b2"] = np.ascontiguousarray(g["L0_ffn_b2"])
    shared["ln_w"] = np.ascontiguousarray(g["L0_ffn_ln_w"].reshape(1, D))
    shared["ln_b"] = np.ascontiguousarray(g["L0_ffn_ln_b"].reshape(1, D))
    shared["inpT"] = np.ascontiguousarray(g["L1_attn_in_proj_w"].T)
    shared["conv_wr"] = np.ascontiguousarray(g["L1_attn_conv_w"].reshape(NIC, 128, DCONV).transpose(1, 0, 2).reshape(128, NIC * DCONV))
    shared["conv_br"] = np.ascontiguousarray(g["L1_attn_conv_b"].reshape(NIC, 128).T)
    shared["xpT"] = np.ascontiguousarray(g["L1_attn_x_proj_w"].T)
    shared["dtwT"] = np.ascontiguousarray(g["L1_attn_dt_proj_w"].T)
    shared["dtbr"] = np.ascontiguousarray(g["L1_attn_dt_proj_b"].reshape(NIC, 128).T)
    shared["alogr"] = np.ascontiguousarray(g["L1_attn_A_log"].reshape(NIC, 128, DST).transpose(1, 0, 2).reshape(128, NIC * DST))
    shared["Dr"] = np.ascontiguousarray(g["L1_attn_D"].reshape(NIC, 128).T)
    shared["outpT"] = np.ascontiguousarray(g["L1_attn_out_proj_w"].T)
    shared["Wt"] = np.ascontiguousarray(g["L1_ffn_W"].T)
    shared["bL1"] = np.ascontiguousarray(g["L1_ffn_b"].reshape(1, D))
    shared["ldT"] = np.ascontiguousarray(g["L1_ffn_lora_down"].T)
    shared["luT"] = np.ascontiguousarray(g["L1_ffn_lora_up"].T)
    shared["daux"] = np.ascontiguousarray(g["L1_ffn_down_aux"])
    shared["uauxT"] = np.ascontiguousarray(g["L1_ffn_up_aux"].T)
    shared["id128"] = np.eye(128, dtype=np.float32)
    sel = np.zeros((32, 32, 128), dtype=np.float32)
    for n in range(32):
        sel[n, n, :] = 1.0
    shared["sel32"] = np.ascontiguousarray(sel.transpose(1, 0, 2).reshape(32, 32 * 128)).astype(ml_dtypes.bfloat16)
    xs = [np.ascontiguousarray(g["L0_attn_inputs"][2 * c:2 * c + 2]) for c in range(8)]
    return shared, xs


def run_stage_groups(inputs, groups):
    shared, xs = prep_host_inputs(inputs)
    carry = [dict() for _ in range(8)]
    out = None
    for gi, grp in enumerate(groups):
        nc, ctx, inter_in, inter_out, with_out = build_program(grp)
        in_maps = []
        for c in range(8):
            m = dict(shared)
            m["x"] = xs[c]
            for nm in inter_in:
                m[nm] = carry[c][nm]
            in_maps.append(m)
        res = run_bass_kernel_spmd(nc, in_maps, core_ids=list(range(8)))
        for c in range(8):
            for nm in inter_out:
                carry[c][nm] = np.asarray(res.results[c][nm])
        if with_out:
            out = [np.asarray(res.results[c]["out"]) for c in range(8)]
    return carry, out


GROUPS_UNFUSED = [["s0"], ["s1", "s3r", "s3e"], ["s2"], ["s4"],
                  ["s6", "m0p", "m0s", "m1p", "m1s"]]
GROUPS_FUSED = [["l0f", "s3r", "s3e", "tail1", "m0p", "m0s", "m1p", "m1s"]]


def kernel(**inputs):
    groups = GROUPS_FUSED if FUSED else GROUPS_UNFUSED
    carry, out = run_stage_groups(inputs, groups)
    return np.concatenate(out, axis=0).reshape(16, L, D)


# version 21
# speedup vs baseline: 1.0424x; 1.0424x over previous
import numpy as np
import ml_dtypes
from contextlib import ExitStack

import concourse.bass as bass
import concourse.mybir as mybir
from concourse.bass_utils import run_bass_kernel_spmd

F32 = mybir.dt.float32
BF16 = mybir.dt.bfloat16
AF = mybir.ActivationFunctionType
ALU = mybir.AluOpType
AX = mybir.AxisListType

FUSED = True
SAME_ENGINE_SYNC = True

ENGINES = ("pe", "act", "dve", "pool", "qs")
_ENG_ATTR = {"pe": "tensor", "act": "scalar", "dve": "vector", "pool": "gpsimd", "qs": "sync"}
NQS = 12
NQP = 8


class Buf:
    __slots__ = ("name", "last_write", "readers", "exclusive", "last_access")

    def __init__(self, name, exclusive=False):
        self.name = name
        self.last_write = None
        self.readers = []
        self.exclusive = exclusive
        self.last_access = None


class Node:
    __slots__ = ("stream", "chan", "fn", "deps", "seq", "flush_id", "milestone", "rank", "tag")

    def __init__(self, stream, chan, fn, seq, flush_id, tag=""):
        self.stream = stream
        self.chan = chan
        self.fn = fn
        self.deps = []
        self.seq = seq
        self.flush_id = flush_id
        self.milestone = False
        self.rank = None
        self.tag = tag


class TT:
    __slots__ = ("h", "buf")

    def __init__(self, h, buf):
        self.h = h
        self.buf = buf

    def __getitem__(self, idx):
        return self.h[idx]

    @property
    def shape(self):
        return self.h.shape

    def rearrange(self, *a, **k):
        return self.h.rearrange(*a, **k)


class Builder:
    def __init__(self, nc, es):
        self.nc = nc
        self.es = es
        self.sems = {e: es.enter_context(nc.semaphore(f"sem_{e}")) for e in ("pe", "act", "dve", "pool")}
        for i in range(NQS):
            self.sems[f"qs{i}"] = es.enter_context(nc.semaphore(f"qsem{i}"))
        for i in range(NQP):
            self.sems[f"qp{i}"] = es.enter_context(nc.semaphore(f"qpsem{i}"))
        self.streams = {e: [] for e in ENGINES}
        self.seqs = {e: 0 for e in ENGINES}
        self.ranks = {e: 0 for e in ("pe", "act", "dve", "pool")}
        self.ranks.update({f"qs{i}": 0 for i in range(NQS)})
        self.ranks.update({f"qp{i}": 0 for i in range(NQP)})
        self.qs_count = 0
        self.qp_count = 0
        self.flush_id = 0
        self.stage_es = None
        self._consts = {}
        self.pump = None

    def pump_n(self, k):
        if self.pump is None:
            return
        for _ in range(k):
            try:
                next(self.pump)
            except StopIteration:
                self.pump = None
                return

    def constant(self, value):
        if value not in self._consts:
            t = self.es.enter_context(self.nc.sbuf_tensor(f"const_{len(self._consts)}", [128, 1], F32, side="right"))
            tt = TT(t, Buf(f"const_{value}"))
            self.op("dve", lambda e, tt=tt, value=value: e.memset(tt[:], float(value)), writes=[tt])
            self._consts[value] = tt
        return self._consts[value]

    def _uname(self, name):
        self._ctr = getattr(self, "_ctr", 0) + 1
        return f"{name}_{self._ctr}"

    def sbuf(self, name, shape, dtype, persistent=False):
        es = self.es if persistent else self.stage_es
        side = "right" if persistent else None
        t = es.enter_context(self.nc.sbuf_tensor(self._uname(name), shape, dtype, side=side))
        return TT(t, Buf(name))

    def psum(self, name, shape, dtype=F32):
        t = self.stage_es.enter_context(self.nc.psum_tensor(self._uname(name), shape, dtype))
        return TT(t, Buf(name, exclusive=True))

    def op(self, stream, fn, reads=(), writes=(), tag=""):
        chan = "qs" if stream == "qs" else stream
        n = Node(stream, chan, fn, self.seqs[stream], self.flush_id, tag)
        self.seqs[stream] += 1
        for r in reads:
            bf = r.buf if isinstance(r, TT) else r
            if bf.last_write is not None and bf.last_write is not n:
                n.deps.append(bf.last_write)
            if bf.exclusive and bf.last_access is not None and bf.last_access is not n:
                n.deps.append(bf.last_access)
            bf.readers.append(n)
            if bf.exclusive:
                bf.last_access = n
        for w in writes:
            bf = w.buf if isinstance(w, TT) else w
            if bf.last_write is not None and bf.last_write is not n:
                n.deps.append(bf.last_write)
            for rd in bf.readers:
                if rd is not n:
                    n.deps.append(rd)
            if bf.exclusive and bf.last_access is not None and bf.last_access is not n:
                n.deps.append(bf.last_access)
            bf.readers = []
            bf.last_write = n
            if bf.exclusive:
                bf.last_access = n
        self.streams[stream].append(n)
        return n

    def dma(self, out_ap, in_ap, reads=(), writes=(), tag="", stream="qs"):
        def fn(eng, out_ap=out_ap, in_ap=in_ap):
            return eng.dma_start(out=out_ap, in_=in_ap)
        n = self.op(stream, fn, reads=reads, writes=writes, tag=tag)
        n.chan = "qs"
        n.milestone = True
        return n

    def stage(self, name):
        builder = self

        class _Stage:
            def __enter__(self):
                if builder.stage_es is not None:
                    self.nested = True
                    return builder
                self.nested = False
                builder.stage_es = ExitStack()
                builder.stage_es.__enter__()
                return builder

            def __exit__(self, et, ev, tb):
                if self.nested:
                    return False
                if et is None:
                    builder.flush(name)
                builder.stage_es.__exit__(et, ev, tb)
                builder.stage_es = None
                return False

        return _Stage()

    def flush(self, name):
        fid = self.flush_id
        def skip_edge(d, n):
            if d.stream != n.stream:
                return False
            if d.stream == "pe":
                return True
            return not SAME_ENGINE_SYNC

        for e in ENGINES:
            for n in self.streams[e]:
                for d in n.deps:
                    if d.flush_id != fid:
                        continue
                    if skip_edge(d, n):
                        continue
                    d.milestone = True
        for e in ENGINES:
            for n in self.streams[e]:
                if not n.milestone:
                    continue
                if n.chan == "qs":
                    if n.stream == "pool":
                        lane = self.qp_count % NQP
                        self.qp_count += 1
                        n.chan = f"qp{lane}"
                    else:
                        lane = self.qs_count % NQS
                        self.qs_count += 1
                        n.chan = f"qs{lane}"
                self.ranks[n.chan] += 1
                n.rank = self.ranks[n.chan]

        sems = self.sems
        final_qs = {c: self.ranks[c] for c in self.ranks if c.startswith("q")}

        def waitval(d):
            return d.rank * 16 if d.chan.startswith("q") else d.rank

        with self.nc.Block(name) as block:
            for e in ENGINES:
                nodes = self.streams[e]
                if not nodes and e != "qs":
                    continue

                def body(eng, nodes=nodes, e=e):
                    vclock = {c: 0 for c in self.ranks}
                    for n in nodes:
                        deps = {id(d): d for d in n.deps}.values()
                        for d in sorted(deps, key=lambda d: (d.chan, d.rank or 0)):
                            if d.flush_id != fid or not d.milestone:
                                continue
                            if skip_edge(d, n):
                                continue
                            if vclock[d.chan] >= d.rank:
                                continue
                            eng.wait_ge(sems[d.chan], waitval(d))
                            vclock[d.chan] = d.rank
                        if n.milestone and n.chan.startswith("q") and n.rank > 1 and vclock[n.chan] < n.rank - 1:
                            eng.wait_ge(sems[n.chan], (n.rank - 1) * 16)
                            vclock[n.chan] = n.rank - 1
                        ins = n.fn(eng)
                        if n.milestone:
                            ins.then_inc(sems[n.chan], 16 if n.chan.startswith("q") else 1)
                    if e == "qs":
                        for c, r in final_qs.items():
                            if r > 0:
                                eng.wait_ge(sems[c], r * 16)

                getattr(block, _ENG_ATTR[e])(body)

        self.streams = {e: [] for e in ENGINES}
        self.flush_id += 1


B = 2
L = 2048
D = 1024
T = B * L
NH, HD = 16, 64
E, F = 8, 4096
DIN, DST, DCONV, DTR = 2048, 16, 4, 64

NDC = D // 128
NTC = T // 128
NFC = F // 128
NIC = DIN // 128

INTERMEDIATES = {
    "acc": ([T, D], F32),
    "ln0bt": ([NDC, 128, T], BF16),
    "attn": ([T, D], F32),
    "x1": ([T, D], F32),
    "ln1bt": ([NDC, 128, T], BF16),
    "w1t_bf": ([E, D, F], BF16),
    "w2t_bf": ([E, F, D], BF16),
    "inpt_bf": ([D, 2 * DIN], BF16),
    "outpt_bf": ([DIN, D], BF16),
    "xsb": ([B, NIC, 128, L], F32),
    "zsil": ([B, NIC, 128, L], F32),
}

EXT_IN = {
    "x": ([B, L, D], F32),
    "w_qkv": ([NH, 3, D, HD], F32),
    "w_o": ([D, D], F32),
    "wrT": ([D, E], F32),
    "br": ([1, E], F32),
    "w1T": ([E, D, F], F32),
    "b1r": ([128, E * NFC], F32),
    "w2T": ([E, F, D], F32),
    "b2": ([E, D], F32),
    "ln_w": ([1, D], F32),
    "ln_b": ([1, D], F32),
    "inpT": ([D, 2 * DIN], F32),
    "conv_wr": ([128, NIC * DCONV], F32),
    "conv_br": ([128, NIC], F32),
    "xpT": ([DIN, 96], F32),
    "dtwT": ([DTR, DIN], F32),
    "dtbr": ([128, NIC], F32),
    "alogr": ([128, NIC * DST], F32),
    "Dr": ([128, NIC], F32),
    "outpT": ([DIN, D], F32),
    "Wt": ([D, D], F32),
    "bL1": ([1, D], F32),
    "ldT": ([100, 4], F32),
    "luT": ([4, 50], F32),
    "daux": ([100, D], F32),
    "uauxT": ([50, D], F32),
    "id128": ([128, 128], F32),
    "sel32": ([32, 32 * 128], BF16),
}

OUTPUT = ("out", [B, L, D], F32)


class Ctx:
    def __init__(self, nc, inter_in=(), inter_out=(), with_out=False, internal=False):
        self.nc = nc
        self.t = {}
        for name, (shape, dt) in EXT_IN.items():
            self.t[name] = nc.dram_tensor(name, shape, dt, kind="ExternalInput")
        for name in inter_in:
            shape, dt = INTERMEDIATES[name]
            self.t[name] = nc.dram_tensor(name, shape, dt, kind="ExternalInput")
        for name in inter_out:
            shape, dt = INTERMEDIATES[name]
            self.t[name] = nc.dram_tensor(name, shape, dt, kind="Internal" if internal else "ExternalOutput")
        if with_out:
            self.t["out"] = nc.dram_tensor(OUTPUT[0], OUTPUT[1], OUTPUT[2], kind="ExternalOutput")
        self.bufs = {}
        self.shared = {}

    def dbuf(self, name, sub=""):
        key = (name, sub)
        if key not in self.bufs:
            self.bufs[key] = Buf(f"{name}.{sub}")
        return self.bufs[key]


def stage_ln(b, ctx, src_name, dst_f32_name, dst_bt_name, eps=1e-5, tagp="ln"):
    src = ctx.t[src_name]
    src_flat = src if len(src.shape) == 2 else src.rearrange("b l d -> (b l) d")
    dst_f32 = ctx.t[dst_f32_name] if dst_f32_name else None
    dst_bt = ctx.t[dst_bt_name] if dst_bt_name else None
    with b.stage(tagp):
        xt = [b.sbuf(f"{tagp}_x{i}", [128, D], F32) for i in range(2)]
        nt = [b.sbuf(f"{tagp}_n{i}", [128, D], F32) for i in range(2)]
        nb = [b.sbuf(f"{tagp}_nb{i}", [128, D], BF16) for i in range(2)]
        stg = [b.sbuf(f"{tagp}_stg{i}", [128, 128], BF16) for i in range(4)]
        bn6 = [b.sbuf(f"{tagp}_bn6{i}", [128, 12], F32) for i in range(2)]
        bn2 = [b.sbuf(f"{tagp}_bn2{i}", [128, 2], F32) for i in range(2)]
        rstd = [b.sbuf(f"{tagp}_rstd{i}", [128, 1], F32) for i in range(2)]
        nmr = [b.sbuf(f"{tagp}_nmr{i}", [128, 1], F32) for i in range(2)]
        idb = b.sbuf(f"{tagp}_idb", [128, 128], BF16)
        idf = b.sbuf(f"{tagp}_idf", [128, 128], F32)
        ptr = [b.psum(f"{tagp}_ptr{i}", [128, 128], BF16) for i in range(2)]

        b.dma(idf[:], ctx.t["id128"][:], reads=[ctx.dbuf("id128")], writes=[idf])
        b.op("dve", lambda e: e.tensor_copy(idb[:], idf[:]), reads=[idf], writes=[idb])

        for tc in range(NTC):
            b.pump_n(4)
            i = tc % 2
            x, n_, nbt = xt[i], nt[i], nb[i]
            b6, b2_, rs, nm = bn6[i], bn2[i], rstd[i], nmr[i]
            b.dma(x[:], src_flat[tc * 128:(tc + 1) * 128, :], reads=[ctx.dbuf(src_name)], writes=[x])
            b.op("dve", lambda e, x=x, b6=b6: e.bn_stats(b6[:, 0:6], x[:, 0:512]), reads=[x], writes=[b6])
            b.op("dve", lambda e, x=x, b6=b6: e.bn_stats(b6[:, 6:12], x[:, 512:1024]), reads=[x], writes=[b6])
            b.op("dve", lambda e, b6=b6, b2_=b2_: e.bn_aggr(b2_[:], b6[:]), reads=[b6], writes=[b2_])
            ceps = b.constant(float(eps))
            b.op("act", lambda e, b2_=b2_, rs=rs, ceps=ceps: e.activation(rs[:], b2_[:, 1:2], AF.Sqrt, bias=ceps[:, 0:1]),
                 reads=[b2_, ceps], writes=[rs])
            b.op("dve", lambda e, rs=rs: e.reciprocal(rs[:], rs[:]), reads=[rs], writes=[rs])
            b.op("dve", lambda e, b2_=b2_, rs=rs, nm=nm: e.scalar_tensor_tensor(nm[:], b2_[:, 0:1], -1.0, rs[:], ALU.mult, ALU.mult),
                 reads=[b2_, rs], writes=[nm])
            if dst_f32 is not None:
                b.op("dve", lambda e, x=x, n_=n_, rs=rs, nm=nm: e.tensor_scalar(n_[:], x[:], rs[:, 0:1], nm[:, 0:1], ALU.mult, ALU.add),
                     reads=[x, rs, nm], writes=[n_])
                b.dma(dst_f32[tc * 128:(tc + 1) * 128, :], n_[:], reads=[n_], writes=[ctx.dbuf(dst_f32_name)])
            if dst_bt is not None:
                b.op("dve", lambda e, x=x, nbt=nbt, rs=rs, nm=nm: e.tensor_scalar(nbt[:], x[:], rs[:, 0:1], nm[:, 0:1], ALU.mult, ALU.add),
                     reads=[x, rs, nm], writes=[nbt])
                for dc in range(NDC):
                    p = ptr[dc % 2]
                    s = stg[dc % 4]
                    b.op("pe", lambda e, nbt=nbt, dc=dc, p=p: e.transpose(p[:], nbt[:, dc * 128:(dc + 1) * 128], idb[:]),
                         reads=[nbt, idb], writes=[p])
                    b.op("dve", lambda e, p=p, s=s: e.tensor_copy(s[:], p[:]), reads=[p], writes=[s])
                    b.dma(dst_bt[dc, :, tc * 128:(tc + 1) * 128], s[:], reads=[s], writes=[ctx.dbuf(dst_bt_name)])


def prep_chunks(b, ctx, bufs=None):
    jobs = [
        ("w1T", "w1t_bf", E * D, F),
        ("w2T", "w2t_bf", E * F, D),
        ("inpT", "inpt_bf", D, 2 * DIN),
        ("outpT", "outpt_bf", DIN, D),
    ]
    if bufs is None:
        stg = [b.sbuf(f"pp_s{i}", [128, 2048], F32, persistent=True) for i in range(3)]
        stb = [b.sbuf(f"pp_b{i}", [128, 2048], BF16, persistent=True) for i in range(3)]
    else:
        stg, stb = bufs
    eng_rr = ["dve", "act"]
    k = 0
    for src_name, dst_name, rows, cols in jobs:
        src = ctx.t[src_name]
        dst = ctx.t[dst_name]
        src2 = src.rearrange(_flat2(src.shape))
        dst2 = dst.rearrange(_flat2(dst.shape))
        cb = min(cols, 2048)
        ncb = cols // cb
        g = max(1, 2048 // cols)
        rpc = 128 * g
        assert rows % rpc == 0 and cols % cb == 0
        for ch in range(rows // rpc):
            for cbi in range(ncb):
                i = k % 3
                eng = eng_rr[k % 2]
                s, sb = stg[i], stb[i]
                w = g * cb
                sview = src2[ch * rpc:(ch + 1) * rpc, cbi * cb:(cbi + 1) * cb].rearrange("(g p) c -> p g c", p=128)
                dview = dst2[ch * rpc:(ch + 1) * rpc, cbi * cb:(cbi + 1) * cb].rearrange("(g p) c -> p g c", p=128)
                b.dma(s[:, 0:w].rearrange("p (g c) -> p g c", g=g), sview, reads=[ctx.dbuf(src_name)], writes=[s], stream="pool")
                b.op("pool", lambda e, s=s, sb=sb, w=w: e.tensor_copy(sb[:, 0:w], s[:, 0:w]), reads=[s], writes=[sb])
                b.dma(dview, sb[:, 0:w].rearrange("p (g c) -> p g c", g=g), reads=[sb], writes=[ctx.dbuf(dst_name)], stream="pool")
                k += 1
                yield


def stage_prep(b, ctx):
    with b.stage("prep"):
        for _ in prep_chunks(b, ctx):
            pass


def _flat2(shape):
    nd = len(shape)
    if nd == 2:
        return "a b -> a b"
    names = [chr(ord('a') + i) for i in range(nd)]
    return f"{' '.join(names)} -> ({' '.join(names[:-1])}) {names[-1]}"


def stage_attn(b, ctx):
    ln0bt = ctx.t["ln0bt"]
    attn = ctx.t["attn"]
    with b.stage("attn"):
        wq_f = b.sbuf("at_wqf", [128, 3 * NDC * HD], F32)
        wq_b = b.sbuf("at_wqb", [128, 3 * NDC * HD], BF16)
        wo_f = b.sbuf("at_wof", [128, D], F32)
        wo_b = b.sbuf("at_wob", [128, NDC * D], BF16)
        lb = b.sbuf("at_lb", [128, NDC * L], BF16)
        qT = b.sbuf("at_qT", [64, L], BF16)
        kT = b.sbuf("at_kT", [64, L], BF16)
        vb = b.sbuf("at_vb", [128, 16 * 65], BF16)
        et = [b.sbuf(f"at_et{i}", [128, 1024], BF16) for i in range(4)]
        stmp = b.sbuf("at_stmp", [1, L], F32)
        s_all = b.sbuf("at_sall", [16, L], F32)
        s_inv = b.sbuf("at_sinv", [16, L], F32)
        s_invb = b.sbuf("at_sinvb", [16, L], BF16)
        ct_un = b.sbuf("at_ctun", [128, NDC * L], BF16)
        ct = ct_un
        sel_b = b.sbuf("at_selb", [32, 32 * 128], BF16)
        wo_stage = [b.sbuf(f"at_wos{i}", [128, D], F32) for i in range(2)]
        p_s = [b.psum(f"at_ps{i}", [128, 1024], F32) for i in range(2)]
        p_o = b.psum("at_po", [65, L], F32)

        b.dma(sel_b[:], ctx.t["sel32"][:], reads=[ctx.dbuf("sel32")], writes=[sel_b])
        for ic in range(NDC):
            b.dma(wo_f[:], ctx.t["w_o"][ic * 128:(ic + 1) * 128, :], reads=[ctx.dbuf("w_o")], writes=[wo_f])
            b.op("dve", lambda e, ic=ic: e.tensor_copy(wo_b[:, ic * D:(ic + 1) * D], wo_f[:]),
                 reads=[wo_f], writes=[wo_b])
        for kc in range(16):
            b.op("dve", lambda e, kc=kc: e.memset(vb[:, kc * 65 + 64:kc * 65 + 65], 1.0), writes=[vb])

        for bb in range(B):
            t0 = bb * L
            for dc in range(NDC):
                b.dma(lb[:, dc * L:(dc + 1) * L], ln0bt[dc, :, t0:t0 + L],
                      reads=[ctx.dbuf("ln0bt")], writes=[lb])
            for h in range(NH):
                b.pump_n(4)
                for a in range(3):
                    wsrc = ctx.t["w_qkv"][h, a].rearrange("(dc p) n -> p dc n", p=128)
                    wdst = wq_f[:, a * NDC * HD:(a + 1) * NDC * HD].rearrange("p (dc n) -> p dc n", dc=NDC)
                    b.dma(wdst, wsrc, reads=[ctx.dbuf("w_qkv")], writes=[wq_f])
                b.op("dve", lambda e: e.tensor_copy(wq_b[:], wq_f[:]), reads=[wq_f], writes=[wq_b])
                for a, dstT, scl in ((0, qT, 0.125), (1, kT, 1.0)):
                    for hb in range(2):
                        ps = p_s[hb]
                        for qb2 in range(2):
                            c0 = qb2 * 512
                            q0 = hb * 1024 + qb2 * 512
                            for dc in range(NDC):
                                b.op("pe", lambda e, a=a, dc=dc, c0=c0, q0=q0, ps=ps: e.matmul(
                                    ps[0:64, c0:c0 + 512],
                                    wq_b[:, (a * NDC + dc) * HD:(a * NDC + dc + 1) * HD],
                                    lb[:, dc * L + q0:dc * L + q0 + 512],
                                    start=(dc == 0), stop=(dc == NDC - 1)),
                                    reads=[wq_b, lb], writes=[ps])
                        b.op("dve", lambda e, dstT=dstT, hb=hb, scl=scl, ps=ps: e.tensor_scalar_mul(
                            dstT[:, hb * 1024:(hb + 1) * 1024], ps[0:64, :], scl), reads=[ps], writes=[dstT])
                for kc in range(16):
                    ps = p_s[kc % 2]
                    for dc in range(NDC):
                        b.op("pe", lambda e, kc=kc, dc=dc, ps=ps: e.matmul(
                            ps[:, 0:64],
                            lb[:, dc * L + kc * 128:dc * L + (kc + 1) * 128],
                            wq_b[:, (2 * NDC + dc) * HD:(2 * NDC + dc + 1) * HD],
                            start=(dc == 0), stop=(dc == NDC - 1)),
                            reads=[lb, wq_b], writes=[ps])
                    b.op("dve", lambda e, kc=kc, ps=ps: e.tensor_copy(vb[:, kc * 65:kc * 65 + 64], ps[:, 0:64]),
                         reads=[ps], writes=[vb])
                for kc in range(16):
                    for hb in range(2):
                        ps = p_s[hb]
                        eti = et[2 * (kc % 2) + hb]
                        for qb2 in range(2):
                            q0 = hb * 1024 + qb2 * 512
                            b.op("pe", lambda e, kc=kc, q0=q0, qb2=qb2, ps=ps: e.matmul(
                                ps[:, qb2 * 512:(qb2 + 1) * 512],
                                kT[:, kc * 128:(kc + 1) * 128],
                                qT[:, q0:q0 + 512],
                                start=True, stop=True), reads=[kT, qT], writes=[ps])
                        b.op("act", lambda e, eti=eti, ps=ps: e.activation(eti[:], ps[:], AF.Exp),
                             reads=[ps], writes=[eti])
                        for qb2 in range(2):
                            q0 = hb * 1024 + qb2 * 512
                            b.op("pe", lambda e, kc=kc, eti=eti, q0=q0, qb2=qb2: e.matmul(
                                p_o[:, q0:q0 + 512],
                                vb[:, kc * 65:(kc + 1) * 65],
                                eti[:, qb2 * 512:(qb2 + 1) * 512],
                                start=(kc == 0), stop=(kc == 15)), reads=[vb, eti], writes=[p_o])
                b.op("act", lambda e: e.copy(stmp[:], p_o[64:65, :]), reads=[p_o], writes=[stmp])
                b.dma(s_all[h:h + 1, :], stmp[:], reads=[stmp], writes=[s_all], tag="s_gather")
                ic, r0 = h // 2, (h % 2) * 64
                b.op("dve", lambda e, ic=ic, r0=r0: e.tensor_copy(ct_un[r0:r0 + 64, ic * L:(ic + 1) * L], p_o[0:64, :]),
                     reads=[p_o], writes=[ct_un])
            b.op("dve", lambda e: e.reciprocal(s_inv[:], s_all[:]), reads=[s_all], writes=[s_inv])
            b.op("dve", lambda e: e.tensor_copy(s_invb[:], s_inv[:]), reads=[s_inv], writes=[s_invb])
            for h in range(NH):
                ic, r0 = h // 2, (h % 2) * 64
                ps = p_s[h % 2]
                for hb in range(2):
                    b.op("pe", lambda e, h=h, hb=hb, ps=ps: e.matmul(
                        ps[0:64, hb * 512:(hb + 1) * 512],
                        sel_b[0:16, h * 128:h * 128 + 64],
                        s_invb[:, hb * 512:(hb + 1) * 512],
                        start=True, stop=True), reads=[sel_b, s_invb], writes=[ps])
                    b.op("pe", lambda e, h=h, hb=hb, ps=ps: e.matmul(
                        ps[64:128, hb * 512:(hb + 1) * 512],
                        sel_b[0:16, h * 128 + 64:h * 128 + 128],
                        s_invb[:, 1024 + hb * 512:1024 + (hb + 1) * 512],
                        start=True, stop=True), reads=[sel_b, s_invb], writes=[ps])
                b.op("dve", lambda e, ic=ic, r0=r0, ps=ps: e.tensor_tensor(
                    ct[r0:r0 + 64, ic * L:ic * L + 1024], ct_un[r0:r0 + 64, ic * L:ic * L + 1024],
                    ps[0:64, :], ALU.mult), reads=[ct_un, ps], writes=[ct])
                b.op("dve", lambda e, ic=ic, r0=r0, ps=ps: e.tensor_tensor(
                    ct[r0:r0 + 64, ic * L + 1024:ic * L + 2048], ct_un[r0:r0 + 64, ic * L + 1024:ic * L + 2048],
                    ps[64:128, :], ALU.mult), reads=[ct_un, ps], writes=[ct])
            for tc in range(16):
                ps = p_s[tc % 2]
                for half in range(2):
                    for ic in range(NDC):
                        b.op("pe", lambda e, tc=tc, ic=ic, half=half, ps=ps: e.matmul(
                            ps[:, half * 512:(half + 1) * 512],
                            ct[:, ic * L + tc * 128:ic * L + tc * 128 + 128],
                            wo_b[:, ic * D + half * 512:ic * D + (half + 1) * 512],
                            start=(ic == 0), stop=(ic == NDC - 1)), reads=[ct, wo_b], writes=[ps])
                ws = wo_stage[tc % 2]
                b.op("dve", lambda e, ws=ws, ps=ps: e.tensor_copy(ws[:], ps[:]), reads=[ps], writes=[ws])
                b.dma(attn[t0 + tc * 128:t0 + (tc + 1) * 128, :], ws[:],
                      reads=[ws], writes=[ctx.dbuf("attn")])


def stage_router(b, ctx):
    ln0bt = ctx.t["ln0bt"]
    acc = ctx.t["acc"]
    with b.stage("router"):
        cw_all = b.sbuf("r_cwall", [128, NTC * E], F32, persistent=True)
        cwt_all = b.sbuf("r_cwtall", [8, NTC * 128], BF16, persistent=True)
        ctx.shared["cw_all"] = cw_all
        ctx.shared["cwt_all"] = cwt_all
        wr_f = b.sbuf("r_wrf", [128, NDC * E], F32)
        wr_b = b.sbuf("r_wrb", [128, NDC * E], BF16)
        br_f = b.sbuf("r_brf", [1, E], F32)
        br_b = b.sbuf("r_brb", [1, E], BF16)
        b2_f = b.sbuf("r_b2f", [E, D], F32)
        b2_b = b.sbuf("r_b2b", [E, D], BF16)
        ones_b = b.sbuf("r_ones", [1, 128], BF16)
        idf = b.sbuf("r_idf", [128, 128], F32)
        lt = [b.sbuf(f"r_lt{i}", [128, NDC * 128], BF16) for i in range(2)]
        lg = b.sbuf("r_lg", [128, E], F32)
        m1 = b.sbuf("r_m1", [128, 1], F32)
        m2 = b.sbuf("r_m2", [128, 1], F32)
        nm1 = b.sbuf("r_nm1", [128, 1], F32)
        msk = b.sbuf("r_msk", [128, E], F32)
        lg2 = b.sbuf("r_lg2", [128, E], F32)
        sel = b.sbuf("r_sel", [128, E], F32)
        expt = b.sbuf("r_expt", [128, E], F32)
        wun = b.sbuf("r_wun", [128, E], F32)
        zz = b.sbuf("r_z", [128, 1], F32)
        zi = b.sbuf("r_zi", [128, 1], F32)
        ctf = b.sbuf("r_ctf", [8, 128], F32)
        at = [b.sbuf(f"r_at{i}", [128, D], F32) for i in range(2)]
        ot = [b.sbuf(f"r_ot{i}", [128, D], F32) for i in range(2)]
        p_r = b.psum("r_pr", [128, 512], F32)
        p_ct = b.psum("r_pct", [8, 128], F32)
        p_b2 = b.psum("r_pb2", [128, D], F32)

        wsrc = ctx.t["wrT"].rearrange("(dc p) e -> p dc e", p=128)
        b.dma(wr_f.rearrange("p (dc e) -> p dc e", dc=NDC), wsrc, reads=[ctx.dbuf("wrT")], writes=[wr_f])
        b.op("dve", lambda e: e.tensor_copy(wr_b[:], wr_f[:]), reads=[wr_f], writes=[wr_b])
        b.dma(br_f[:], ctx.t["br"][:], reads=[ctx.dbuf("br")], writes=[br_f])
        b.op("dve", lambda e: e.tensor_copy(br_b[:], br_f[:]), reads=[br_f], writes=[br_b])
        b.dma(b2_f[:], ctx.t["b2"][:], reads=[ctx.dbuf("b2")], writes=[b2_f])
        b.op("dve", lambda e: e.tensor_copy(b2_b[:], b2_f[:]), reads=[b2_f], writes=[b2_b])
        b.op("dve", lambda e: e.memset(ones_b[:], 1.0), writes=[ones_b])
        b.dma(idf[:], ctx.t["id128"][:], reads=[ctx.dbuf("id128")], writes=[idf])

        for tc in range(NTC):
            lti = lt[tc % 2]
            for dc in range(NDC):
                b.dma(lti[:, dc * 128:(dc + 1) * 128], ln0bt[dc, :, tc * 128:(tc + 1) * 128],
                      reads=[ctx.dbuf("ln0bt")], writes=[lti])
            for dc in range(NDC):
                b.op("pe", lambda e, lti=lti, dc=dc: e.matmul(
                    p_r[:, 0:E], lti[:, dc * 128:(dc + 1) * 128], wr_b[:, dc * E:(dc + 1) * E],
                    start=(dc == 0), stop=False), reads=[lti, wr_b], writes=[p_r])
            b.op("pe", lambda e: e.matmul(p_r[:, 0:E], ones_b[:], br_b[:], start=False, stop=True),
                 reads=[ones_b, br_b], writes=[p_r])
            b.op("dve", lambda e: e.tensor_copy(lg[:], p_r[:, 0:E]), reads=[p_r], writes=[lg])
            b.op("dve", lambda e: e.reduce_max(m1[:], lg[:], axis=AX.X), reads=[lg], writes=[m1])
            b.op("dve", lambda e: e.tensor_scalar(msk[:], lg[:], m1[:, 0:1], None, ALU.is_ge), reads=[lg, m1], writes=[msk])
            b.op("dve", lambda e: e.scalar_tensor_tensor(lg2[:], msk[:], -1e30, lg[:], ALU.mult, ALU.add),
                 reads=[msk, lg], writes=[lg2])
            b.op("dve", lambda e: e.reduce_max(m2[:], lg2[:], axis=AX.X), reads=[lg2], writes=[m2])
            b.op("dve", lambda e: e.tensor_scalar(sel[:], lg[:], m2[:, 0:1], None, ALU.is_ge), reads=[lg, m2], writes=[sel])
            b.op("dve", lambda e: e.tensor_scalar_mul(nm1[:], m1[:], -1.0), reads=[m1], writes=[nm1])
            b.op("act", lambda e: e.activation(expt[:], lg[:], AF.Exp, bias=nm1[:, 0:1]), reads=[lg, nm1], writes=[expt])
            b.op("dve", lambda e: e.tensor_mul(wun[:], sel[:], expt[:]), reads=[sel, expt], writes=[wun])
            b.op("dve", lambda e: e.reduce_sum(zz[:], wun[:], axis=AX.X), reads=[wun], writes=[zz])
            b.op("dve", lambda e: e.reciprocal(zi[:], zz[:]), reads=[zz], writes=[zi])
            b.op("dve", lambda e, tc=tc: e.tensor_scalar(cw_all[:, tc * E:(tc + 1) * E], wun[:], zi[:, 0:1], None, ALU.mult),
                 reads=[wun, zi], writes=[cw_all])
            b.op("pe", lambda e, tc=tc: e.transpose(p_ct[:], cw_all[:, tc * E:(tc + 1) * E], idf[:]),
                 reads=[cw_all, idf], writes=[p_ct])
            b.op("dve", lambda e, tc=tc: e.tensor_copy(cwt_all[:, tc * 128:(tc + 1) * 128], p_ct[:]),
                 reads=[p_ct], writes=[cwt_all])
            ati, oti = at[tc % 2], ot[tc % 2]
            b.dma(ati[:], acc[tc * 128:(tc + 1) * 128, :], reads=[ctx.dbuf("acc", str(tc))], writes=[ati])
            for half in range(2):
                b.op("pe", lambda e, tc=tc, half=half: e.matmul(
                    p_b2[:, half * 512:(half + 1) * 512], cwt_all[:, tc * 128:(tc + 1) * 128],
                    b2_b[:, half * 512:(half + 1) * 512], start=True, stop=True),
                    reads=[cwt_all, b2_b], writes=[p_b2])
            b.op("dve", lambda e, ati=ati, oti=oti: e.tensor_add(oti[:], ati[:], p_b2[:]),
                 reads=[ati, p_b2], writes=[oti])
            b.dma(acc[tc * 128:(tc + 1) * 128, :], oti[:], reads=[oti], writes=[ctx.dbuf("acc", str(tc))])


def stage_experts(b, ctx):
    ln0bt = ctx.t["ln0bt"]
    acc = ctx.t["acc"]
    w1t = ctx.t["w1t_bf"]
    w2t = ctx.t["w2t_bf"]
    cw_all = ctx.shared["cw_all"]
    with b.stage("experts"):
        b1_sb = b.sbuf("e_b1", [128, E * NFC], F32)
        w2_b = b.sbuf("e_w2", [128, NFC * D], BF16)
        w1_sl = [b.sbuf(f"e_w1{i}", [128, NDC * 512], BF16) for i in range(2)]
        lt = [b.sbuf(f"e_lt{i}", [128, NDC * 1024], BF16) for i in range(2)]
        h_bf = b.sbuf("e_h", [128, NFC * 1024], BF16)
        at = [b.sbuf(f"e_at{i}", [128, D], F32) for i in range(2)]
        ot = [b.sbuf(f"e_ot{i}", [128, D], F32) for i in range(2)]
        p_h = [b.psum(f"e_ph{i}", [128, 1024], F32) for i in range(2)]
        p_y = [b.psum(f"e_py{i}", [128, 1024], F32) for i in range(2)]

        b.dma(b1_sb[:], ctx.t["b1r"][:], reads=[ctx.dbuf("b1r")], writes=[b1_sb])

        for ee in range(E):
            for q4 in range(4):
                nsub = NFC // 4
                wsrc = w2t[ee, q4 * nsub * 128:(q4 + 1) * nsub * 128, :].rearrange("(fc p) o -> p fc o", p=128)
                wdst = w2_b[:, q4 * nsub * D:(q4 + 1) * nsub * D].rearrange("p (fc o) -> p fc o", fc=nsub)
                b.dma(wdst, wsrc, reads=[ctx.dbuf("w2t_bf")], writes=[w2_b])
            for tblk in range(4):
                lti = lt[tblk % 2]
                b.dma(lti.rearrange("p (dc t) -> p dc t", dc=NDC),
                      ln0bt[:, :, tblk * 1024:(tblk + 1) * 1024].rearrange("dc p t -> p dc t"),
                      reads=[ctx.dbuf("ln0bt")], writes=[lti])
                for fg in range(8):
                    w1i = w1_sl[fg % 2]
                    b.dma(w1i.rearrange("p (dc f) -> p dc f", dc=NDC),
                          w1t[ee].rearrange("(dc p) f -> p dc f", p=128)[:, :, fg * 512:(fg + 1) * 512],
                          reads=[ctx.dbuf("w1t_bf")], writes=[w1i])
                    for fcl in range(4):
                        fc = fg * 4 + fcl
                        p = p_h[fc % 2]
                        for dc in range(NDC):
                            for h2 in range(2):
                                b.op("pe", lambda e, w1i=w1i, lti=lti, p=p, dc=dc, h2=h2, fcl=fcl: e.matmul(
                                    p[:, h2 * 512:(h2 + 1) * 512],
                                    w1i[:, dc * 512 + fcl * 128:dc * 512 + (fcl + 1) * 128],
                                    lti[:, dc * 1024 + h2 * 512:dc * 1024 + (h2 + 1) * 512],
                                    start=(dc == 0), stop=(dc == NDC - 1)),
                                    reads=[w1i, lti], writes=[p])
                        b.op("act", lambda e, p=p, fc=fc, ee=ee: e.activation(
                            h_bf[:, fc * 1024:(fc + 1) * 1024], p[:], AF.Gelu,
                            bias=b1_sb[:, ee * NFC + fc:ee * NFC + fc + 1]),
                            reads=[p, b1_sb], writes=[h_bf])
                for tc8 in range(8):
                    tgl = tblk * 8 + tc8
                    p = p_y[tc8 % 2]
                    ati, oti = at[tc8 % 2], ot[tc8 % 2]
                    b.dma(ati[:], acc[tgl * 128:(tgl + 1) * 128, :], reads=[ctx.dbuf("acc", str(tgl))], writes=[ati])
                    for fc in range(NFC):
                        for h2 in range(2):
                            b.op("pe", lambda e, p=p, fc=fc, tc8=tc8, h2=h2: e.matmul(
                                p[:, h2 * 512:(h2 + 1) * 512],
                                h_bf[:, fc * 1024 + tc8 * 128:fc * 1024 + tc8 * 128 + 128],
                                w2_b[:, fc * D + h2 * 512:fc * D + (h2 + 1) * 512],
                                start=(fc == 0), stop=(fc == NFC - 1)),
                                reads=[h_bf, w2_b], writes=[p])
                    b.op("dve", lambda e, p=p, ati=ati, oti=oti, tgl=tgl, ee=ee: e.scalar_tensor_tensor(
                        oti[:], p[:], cw_all[:, tgl * E + ee:tgl * E + ee + 1], ati[:], ALU.mult, ALU.add),
                        reads=[p, cw_all, ati], writes=[oti])
                    b.dma(acc[tgl * 128:(tgl + 1) * 128, :], oti[:], reads=[oti], writes=[ctx.dbuf("acc", str(tgl))])


def stage_assembly(b, ctx):
    acc = ctx.t["acc"]
    attn = ctx.t["attn"]
    x = ctx.t["x"].rearrange("b l d -> (b l) d")
    x1 = ctx.t["x1"]
    ln1bt = ctx.t["ln1bt"]
    with b.stage("asm"):
        lnw_f = b.sbuf("a_lnw", [1, D], F32)
        lnb_f = b.sbuf("a_lnb", [1, D], F32)
        ones_f = b.sbuf("a_ones", [1, 128], F32)
        wb = b.sbuf("a_wb", [128, D], F32)
        bb_ = b.sbuf("a_bb", [128, D], F32)
        idb = b.sbuf("a_idb", [128, 128], BF16)
        idf = b.sbuf("a_idf", [128, 128], F32)
        zt = [b.sbuf(f"a_z{i}", [128, D], F32) for i in range(3)]
        att = [b.sbuf(f"a_at{i}", [128, D], F32) for i in range(3)]
        xt = [b.sbuf(f"a_x{i}", [128, D], F32) for i in range(3)]
        n2l = [b.sbuf(f"a_n2{i}", [128, D], F32) for i in range(3)]
        z2l = [b.sbuf(f"a_z2{i}", [128, D], F32) for i in range(3)]
        x1t = [b.sbuf(f"a_x1{i}", [128, D], F32) for i in range(3)]
        nb = [b.sbuf(f"a_nb{i}", [128, D], BF16) for i in range(3)]
        stg = [b.sbuf(f"a_stg{i}", [128, 128], BF16) for i in range(4)]
        bn6 = [b.sbuf(f"a_bn6{i}", [128, 12], F32) for i in range(3)]
        bn2 = [b.sbuf(f"a_bn2{i}", [128, 2], F32) for i in range(3)]
        rstd = [b.sbuf(f"a_rstd{i}", [128, 1], F32) for i in range(3)]
        nmr = [b.sbuf(f"a_nmr{i}", [128, 1], F32) for i in range(3)]
        p_w = b.psum("a_pw", [128, D], F32)
        ptr = [b.psum(f"a_ptr{i}", [128, 128], BF16) for i in range(2)]

        b.dma(idf[:], ctx.t["id128"][:], reads=[ctx.dbuf("id128")], writes=[idf])
        b.op("dve", lambda e: e.tensor_copy(idb[:], idf[:]), reads=[idf], writes=[idb])
        b.dma(lnw_f[:], ctx.t["ln_w"][:], reads=[ctx.dbuf("ln_w")], writes=[lnw_f])
        b.dma(lnb_f[:], ctx.t["ln_b"][:], reads=[ctx.dbuf("ln_b")], writes=[lnb_f])
        b.op("dve", lambda e: e.memset(ones_f[:], 1.0), writes=[ones_f])
        for half in range(2):
            b.op("pe", lambda e, half=half: e.matmul(
                p_w[:, half * 512:(half + 1) * 512], ones_f[:], lnw_f[:, half * 512:(half + 1) * 512],
                start=True, stop=True), reads=[ones_f, lnw_f], writes=[p_w])
        b.op("dve", lambda e: e.tensor_copy(wb[:], p_w[:]), reads=[p_w], writes=[wb])
        for half in range(2):
            b.op("pe", lambda e, half=half: e.matmul(
                p_w[:, half * 512:(half + 1) * 512], ones_f[:], lnb_f[:, half * 512:(half + 1) * 512],
                start=True, stop=True), reads=[ones_f, lnb_f], writes=[p_w])
        b.op("dve", lambda e: e.tensor_copy(bb_[:], p_w[:]), reads=[p_w], writes=[bb_])

        ceps2 = b.constant(1e-12)
        ceps1 = b.constant(1e-5)
        for tc in range(NTC):
            i = tc % 3
            z, a_, xx, x1i, nbt = zt[i], att[i], xt[i], x1t[i], nb[i]
            n2, z2 = n2l[i], z2l[i]
            b6, b2_, rs, nm = bn6[i], bn2[i], rstd[i], nmr[i]
            b.dma(z[:], acc[tc * 128:(tc + 1) * 128, :], reads=[ctx.dbuf("acc", str(tc))], writes=[z])
            b.dma(a_[:], attn[tc * 128:(tc + 1) * 128, :], reads=[ctx.dbuf("attn")], writes=[a_])
            b.dma(xx[:], x[tc * 128:(tc + 1) * 128, :], reads=[ctx.dbuf("x")], writes=[xx])
            b.op("dve", lambda e, z=z, b6=b6: e.bn_stats(b6[:, 0:6], z[:, 0:512]), reads=[z], writes=[b6])
            b.op("dve", lambda e, z=z, b6=b6: e.bn_stats(b6[:, 6:12], z[:, 512:1024]), reads=[z], writes=[b6])
            b.op("dve", lambda e, b6=b6, b2_=b2_: e.bn_aggr(b2_[:], b6[:]), reads=[b6], writes=[b2_])
            b.op("act", lambda e, b2_=b2_, rs=rs: e.activation(rs[:], b2_[:, 1:2], AF.Sqrt, bias=ceps2[:, 0:1]),
                 reads=[b2_, ceps2], writes=[rs])
            b.op("dve", lambda e, rs=rs: e.reciprocal(rs[:], rs[:]), reads=[rs], writes=[rs])
            b.op("dve", lambda e, b2_=b2_, rs=rs, nm=nm: e.scalar_tensor_tensor(nm[:], b2_[:, 0:1], -1.0, rs[:], ALU.mult, ALU.mult),
                 reads=[b2_, rs], writes=[nm])
            b.op("dve", lambda e, z=z, rs=rs, nm=nm, n2=n2: e.tensor_scalar(n2[:], z[:], rs[:, 0:1], nm[:, 0:1], ALU.mult, ALU.add),
                 reads=[z, rs, nm], writes=[n2])
            b.op("dve", lambda e, n2=n2, z2=z2: e.tensor_mul(z2[:], n2[:], wb[:]), reads=[n2, wb], writes=[z2])
            b.op("pool", lambda e, z2=z2: e.tensor_add(z2[:], z2[:], bb_[:]), reads=[z2, bb_], writes=[z2])
            b.op("pool", lambda e, a_=a_, xx=xx: e.tensor_add(xx[:], xx[:], a_[:]), reads=[xx, a_], writes=[xx])
            b.op("dve", lambda e, xx=xx, x1i=x1i, z2=z2: e.tensor_add(x1i[:], xx[:], z2[:]), reads=[xx, z2], writes=[x1i])
            b.dma(x1[tc * 128:(tc + 1) * 128, :], x1i[:], reads=[x1i], writes=[ctx.dbuf("x1")])
            b.op("dve", lambda e, x1i=x1i, b6=b6: e.bn_stats(b6[:, 0:6], x1i[:, 0:512]), reads=[x1i], writes=[b6])
            b.op("dve", lambda e, x1i=x1i, b6=b6: e.bn_stats(b6[:, 6:12], x1i[:, 512:1024]), reads=[x1i], writes=[b6])
            b.op("dve", lambda e, b6=b6, b2_=b2_: e.bn_aggr(b2_[:], b6[:]), reads=[b6], writes=[b2_])
            b.op("act", lambda e, b2_=b2_, rs=rs: e.activation(rs[:], b2_[:, 1:2], AF.Sqrt, bias=ceps1[:, 0:1]),
                 reads=[b2_, ceps1], writes=[rs])
            b.op("dve", lambda e, rs=rs: e.reciprocal(rs[:], rs[:]), reads=[rs], writes=[rs])
            b.op("dve", lambda e, b2_=b2_, rs=rs, nm=nm: e.scalar_tensor_tensor(nm[:], b2_[:, 0:1], -1.0, rs[:], ALU.mult, ALU.mult),
                 reads=[b2_, rs], writes=[nm])
            b.op("dve", lambda e, x1i=x1i, nbt=nbt, rs=rs, nm=nm: e.tensor_scalar(nbt[:], x1i[:], rs[:, 0:1], nm[:, 0:1], ALU.mult, ALU.add),
                 reads=[x1i, rs, nm], writes=[nbt])
            for dc in range(NDC):
                p = ptr[dc % 2]
                s = stg[dc % 4]
                b.op("pe", lambda e, nbt=nbt, dc=dc, p=p: e.transpose(p[:], nbt[:, dc * 128:(dc + 1) * 128], idb[:]),
                     reads=[nbt, idb], writes=[p])
                b.op("dve", lambda e, p=p, s=s: e.tensor_copy(s[:], p[:]), reads=[p], writes=[s])
                b.dma(ln1bt[dc, :, tc * 128:(tc + 1) * 128], s[:], reads=[s], writes=[ctx.dbuf("ln1bt")])


def stage_lilora(b, ctx):
    x1 = ctx.t["x1"]
    ln1bt = ctx.t["ln1bt"]
    out = ctx.t["out"].rearrange("b l d -> (b l) d")
    with b.stage("lilora"):
        ldt_f = b.sbuf("l_ldt", [100, 4], F32)
        ldt_b = b.sbuf("l_ldtb", [100, 4], BF16)
        lut_f = b.sbuf("l_lut", [4, 50], F32)
        lut_b = b.sbuf("l_lutb", [4, 50], BF16)
        da_f = b.sbuf("l_da", [100, D], F32)
        da_b = b.sbuf("l_dab", [100, D], BF16)
        ua_f = b.sbuf("l_ua", [50, D], F32)
        ua_b = b.sbuf("l_uab", [50, D], BF16)
        a1_b = b.sbuf("l_a1b", [4, D], BF16)
        a2_b = b.sbuf("l_a2b", [50, D], BF16)
        wt_f = [b.sbuf(f"l_wt{i}", [128, D], F32) for i in range(2)]
        weff = b.sbuf("l_weff", [128, NDC * D], BF16)
        bl_f = b.sbuf("l_blf", [1, D], F32)
        ones_f = b.sbuf("l_ones", [1, 128], F32)
        bb2 = b.sbuf("l_bb2", [128, D], F32)
        ltb = [b.sbuf(f"l_ltb{i}", [128, NDC * 128], BF16) for i in range(3)]
        x1t = [b.sbuf(f"l_x1{i}", [128, D], F32) for i in range(3)]
        ot = [b.sbuf(f"l_ot{i}", [128, D], F32) for i in range(3)]
        p_a = b.psum("l_pa", [128, D], F32)
        p_o = [b.psum(f"l_po{i}", [128, 512], F32) for i in range(2)]

        b.dma(ldt_f[:], ctx.t["ldT"][:], reads=[ctx.dbuf("ldT")], writes=[ldt_f])
        b.dma(lut_f[:], ctx.t["luT"][:], reads=[ctx.dbuf("luT")], writes=[lut_f])
        b.dma(da_f[:], ctx.t["daux"][:], reads=[ctx.dbuf("daux")], writes=[da_f])
        b.dma(ua_f[:], ctx.t["uauxT"][:], reads=[ctx.dbuf("uauxT")], writes=[ua_f])
        b.dma(bl_f[:], ctx.t["bL1"][:], reads=[ctx.dbuf("bL1")], writes=[bl_f])
        for t_, s_ in ((ldt_b, ldt_f), (lut_b, lut_f), (da_b, da_f), (ua_b, ua_f)):
            b.op("dve", lambda e, t_=t_, s_=s_: e.tensor_copy(t_[:], s_[:]), reads=[s_], writes=[t_])
        b.op("dve", lambda e: e.memset(ones_f[:], 1.0), writes=[ones_f])
        for half in range(2):
            b.op("pe", lambda e, half=half: e.matmul(
                p_a[:, half * 512:(half + 1) * 512], ones_f[:], bl_f[:, half * 512:(half + 1) * 512],
                start=True, stop=True), reads=[ones_f, bl_f], writes=[p_a])
        b.op("dve", lambda e: e.tensor_copy(bb2[:], p_a[:]), reads=[p_a], writes=[bb2])
        for half in range(2):
            b.op("pe", lambda e, half=half: e.matmul(
                p_a[0:4, half * 512:(half + 1) * 512], ldt_b[:], da_b[:, half * 512:(half + 1) * 512],
                start=True, stop=True), reads=[ldt_b, da_b], writes=[p_a])
        b.op("dve", lambda e: e.tensor_copy(a1_b[:], p_a[0:4, :]), reads=[p_a], writes=[a1_b])
        for half in range(2):
            b.op("pe", lambda e, half=half: e.matmul(
                p_a[0:50, half * 512:(half + 1) * 512], lut_b[:], a1_b[:, half * 512:(half + 1) * 512],
                start=True, stop=True), reads=[lut_b, a1_b], writes=[p_a])
        b.op("dve", lambda e: e.tensor_copy(a2_b[:], p_a[0:50, :]), reads=[p_a], writes=[a2_b])
        for ic in range(NDC):
            wtf = wt_f[ic % 2]
            b.dma(wtf[:], ctx.t["Wt"][ic * 128:(ic + 1) * 128, :], reads=[ctx.dbuf("Wt")], writes=[wtf])
            for half in range(2):
                b.op("pe", lambda e, ic=ic, half=half: e.matmul(
                    p_a[:, half * 512:(half + 1) * 512], a2_b[:, ic * 128:(ic + 1) * 128],
                    ua_b[:, half * 512:(half + 1) * 512], start=True, stop=True),
                    reads=[a2_b, ua_b], writes=[p_a])
            b.op("dve", lambda e, ic=ic, wtf=wtf: e.tensor_add(weff[:, ic * D:(ic + 1) * D], p_a[:], wtf[:]),
                 reads=[p_a, wtf], writes=[weff])
        for tc in range(NTC):
            i = tc % 3
            lti, x1i, oti = ltb[i], x1t[i], ot[i]
            for dc in range(NDC):
                b.dma(lti[:, dc * 128:(dc + 1) * 128], ln1bt[dc, :, tc * 128:(tc + 1) * 128],
                      reads=[ctx.dbuf("ln1bt")], writes=[lti])
            b.dma(x1i[:], x1[tc * 128:(tc + 1) * 128, :], reads=[ctx.dbuf("x1")], writes=[x1i])
            for half in range(2):
                po = p_o[half]
                for ic in range(NDC):
                    b.op("pe", lambda e, lti=lti, po=po, ic=ic, half=half: e.matmul(
                        po[:, 0:512], lti[:, ic * 128:(ic + 1) * 128],
                        weff[:, ic * D + half * 512:ic * D + (half + 1) * 512],
                        start=(ic == 0), stop=(ic == NDC - 1)), reads=[lti, weff], writes=[po])
                b.op("dve", lambda e, po=po, x1i=x1i, oti=oti, half=half: e.tensor_add(
                    oti[:, half * 512:(half + 1) * 512], po[:, 0:512], x1i[:, half * 512:(half + 1) * 512]),
                    reads=[po, x1i], writes=[oti])
            b.op("pool", lambda e, oti=oti: e.tensor_add(oti[:], oti[:], bb2[:]), reads=[oti, bb2], writes=[oti])
            b.dma(out[tc * 128:(tc + 1) * 128, :], oti[:], reads=[oti], writes=[ctx.dbuf("out", str(tc))])


def stage_mamba_pre(b, ctx, bb):
    ln1bt = ctx.t["ln1bt"]
    inpt = ctx.t["inpt_bf"]
    xsb = ctx.t["xsb"]
    zsil = ctx.t["zsil"]
    t0 = bb * L
    with b.stage(f"mpre{bb}"):
        lb = b.sbuf("m_lb", [128, NDC * L], BF16)
        wsl = [b.sbuf(f"m_wsl{i}", [128, NDC * 512], BF16) for i in range(2)]
        xp_f = b.sbuf("m_xpf", [128, NIC * 96], F32)
        cwr = b.sbuf("m_cwr", [128, NIC * DCONV], F32)
        cbr = b.sbuf("m_cbr", [128, NIC], F32)
        xr = b.sbuf("m_xr", [128, L + DCONV - 1], F32)
        cv1 = b.sbuf("m_cv1", [128, L], F32)
        cv2 = b.sbuf("m_cv2", [128, L], F32)
        xs_t = b.sbuf("m_xst", [128, L], F32)
        zs_t = [b.sbuf(f"m_zst{i}", [128, L], F32) for i in range(2)]
        if "m_dt" not in ctx.shared:
            ctx.shared["m_dt"] = b.sbuf("m_dt_p", [64, L], F32, persistent=True)
            ctx.shared["m_bc"] = b.sbuf("m_bc_p", [32, L], BF16, persistent=True)
        p_xz = [b.psum(f"m_pxz{i}", [128, 512], F32) for i in range(2)]
        p_x = b.psum("m_px", [96, L], F32)

        b.dma(cwr[:], ctx.t["conv_wr"][:], reads=[ctx.dbuf("conv_wr")], writes=[cwr])
        b.dma(cbr[:], ctx.t["conv_br"][:], reads=[ctx.dbuf("conv_br")], writes=[cbr])
        xsrc = ctx.t["xpT"].rearrange("(ic p) k -> p ic k", p=128)
        b.dma(xp_f.rearrange("p (ic k) -> p ic k", ic=NIC), xsrc, reads=[ctx.dbuf("xpT")], writes=[xp_f])
        for dc in range(NDC):
            b.dma(lb[:, dc * L:(dc + 1) * L], ln1bt[dc, :, t0:t0 + L], reads=[ctx.dbuf("ln1bt")], writes=[lb])

        for eg in range(8):
            wi = wsl[eg % 2]
            for dc in range(NDC):
                b.dma(wi[:, dc * 512:(dc + 1) * 512],
                      inpt[dc * 128:(dc + 1) * 128, eg * 512:(eg + 1) * 512],
                      reads=[ctx.dbuf("inpt_bf")], writes=[wi])
            for ec in range(4):
                ic_g = eg * 4 + ec
                is_x = ic_g < NIC
                ic = ic_g if is_x else ic_g - NIC
                for lb4 in range(4):
                    p = p_xz[lb4 % 2]
                    for dc in range(NDC):
                        b.op("pe", lambda e, wi=wi, p=p, dc=dc, ec=ec, lb4=lb4: e.matmul(
                            p[:], wi[:, dc * 512 + ec * 128:dc * 512 + (ec + 1) * 128],
                            lb[:, dc * L + lb4 * 512:dc * L + (lb4 + 1) * 512],
                            start=(dc == 0), stop=(dc == NDC - 1)), reads=[wi, lb], writes=[p])
                    if is_x:
                        b.op("dve", lambda e, p=p, lb4=lb4: e.tensor_copy(
                            xr[:, DCONV - 1 + lb4 * 512:DCONV - 1 + (lb4 + 1) * 512], p[:]),
                            reads=[p], writes=[xr])
                    else:
                        zi = zs_t[ec % 2]
                        b.op("act", lambda e, p=p, lb4=lb4, zi=zi: e.activation(
                            zi[:, lb4 * 512:(lb4 + 1) * 512], p[:], AF.Silu), reads=[p], writes=[zi])
                if is_x:
                    b.op("dve", lambda e, ic=ic: e.memset(xr[:, 0:DCONV - 1], 0.0), writes=[xr])
                    b.op("dve", lambda e, ic=ic: e.tensor_scalar(cv1[:], xr[:, 0:L], cwr[:, ic * DCONV:ic * DCONV + 1], None, ALU.mult),
                         reads=[xr, cwr], writes=[cv1])
                    b.op("dve", lambda e, ic=ic: e.scalar_tensor_tensor(cv2[:], xr[:, 1:1 + L], cwr[:, ic * DCONV + 1:ic * DCONV + 2], cv1[:], ALU.mult, ALU.add),
                         reads=[xr, cwr, cv1], writes=[cv2])
                    b.op("dve", lambda e, ic=ic: e.scalar_tensor_tensor(cv1[:], xr[:, 2:2 + L], cwr[:, ic * DCONV + 2:ic * DCONV + 3], cv2[:], ALU.mult, ALU.add),
                         reads=[xr, cwr, cv2], writes=[cv1])
                    b.op("dve", lambda e, ic=ic: e.scalar_tensor_tensor(cv2[:], xr[:, 3:3 + L], cwr[:, ic * DCONV + 3:ic * DCONV + 4], cv1[:], ALU.mult, ALU.add),
                         reads=[xr, cwr, cv1], writes=[cv2])
                    b.op("act", lambda e, ic=ic: e.activation(xs_t[:], cv2[:], AF.Silu, bias=cbr[:, ic:ic + 1]),
                         reads=[cv2, cbr], writes=[xs_t])
                    b.dma(xsb[bb, ic], xs_t[:], reads=[xs_t], writes=[ctx.dbuf("xsb")])
                    for lb4 in range(4):
                        b.op("pe", lambda e, ic=ic, lb4=lb4: e.matmul(
                            p_x[:, lb4 * 512:(lb4 + 1) * 512], xp_f[:, ic * 96:(ic + 1) * 96],
                            xs_t[:, lb4 * 512:(lb4 + 1) * 512],
                            start=(ic == 0), stop=(ic == NIC - 1)), reads=[xp_f, xs_t], writes=[p_x])
                else:
                    b.dma(zsil[bb, ic], zs_t[ec % 2][:], reads=[zs_t[ec % 2]], writes=[ctx.dbuf("zsil")])
        b.op("act", lambda e: e.copy(ctx.shared["m_dt"][:], p_x[0:64, :]), reads=[p_x], writes=[ctx.shared["m_dt"]])
        b.op("dve", lambda e: e.tensor_copy(ctx.shared["m_bc"][:], p_x[64:96, :]), reads=[p_x], writes=[ctx.shared["m_bc"]])


def stage_mamba_scan(b, ctx, bb):
    xsb = ctx.t["xsb"]
    zsil = ctx.t["zsil"]
    outp = ctx.t["outpt_bf"]
    out = ctx.t["out"].rearrange("b l d -> (b l) d")
    t0 = bb * L
    with b.stage(f"mscan{bb}"):
        dtw_f = b.sbuf("s_dtwf", [64, DIN], F32)
        dtbr = b.sbuf("s_dtbr", [128, NIC], F32)
        alog = b.sbuf("s_alog", [128, NIC * DST], F32)
        a_sb = b.sbuf("s_a", [128, NIC * DST], F32)
        dr = b.sbuf("s_dr", [128, NIC], F32)
        sel_b = b.sbuf("s_selb", [32, 32 * 128], BF16)
        ones_c = b.constant(1.0)
        h_st = b.sbuf("s_hst", [128, NIC * DST], F32)
        xs_t = [b.sbuf(f"s_xst{i}", [128, 1024], F32) for i in range(2)]
        zs_t = b.sbuf("s_zst", [128, 1024], F32)
        dta = b.sbuf("s_dta", [128, 1024], F32)
        delta = [b.sbuf(f"s_delta{i}", [128, 1024], F32) for i in range(4)]
        dx = [b.sbuf(f"s_dx{i}", [128, 1024], F32) for i in range(4)]
        da_t2 = [b.sbuf(f"s_dat{i}", [128, 1024], F32) for i in range(2)]
        h_t2 = [b.sbuf(f"s_ht{i}", [128, 1024], F32) for i in range(2)]
        hc_t2 = [b.sbuf(f"s_hct{i}", [128, 1024], F32) for i in range(2)]
        hc_t = hc_t2[0]
        y_ac = [b.sbuf(f"s_y{i}", [128, 1024], F32) for i in range(4)]
        y_b = b.sbuf("s_yb", [128, NIC * 1024], BF16)
        ot = [b.sbuf(f"s_ot{i}", [128, 512], F32) for i in range(2)]
        wo_half = b.sbuf("s_wohalf", [128, NIC * 512], BF16)
        pbm_s = b.sbuf("s_pbms", [128, 1024], F32)
        pcm_s = b.sbuf("s_pcms", [128, 1024], F32)
        dbu4 = [b.sbuf(f"s_dbu4{i}", [128, 1024], F32) for i in range(4)]
        p_d = b.psum("s_pd", [128, 1024], F32)
        p_bm = b.psum("s_pbm", [128, 1024], F32)
        p_cm = b.psum("s_pcm", [128, 1024], F32)
        p_op = b.psum("s_pop", [128, 512], F32)

        b.dma(dtw_f[:], ctx.t["dtwT"][:], reads=[ctx.dbuf("dtwT")], writes=[dtw_f])
        b.dma(dtbr[:], ctx.t["dtbr"][:], reads=[ctx.dbuf("dtbr")], writes=[dtbr])
        b.dma(alog[:], ctx.t["alogr"][:], reads=[ctx.dbuf("alogr")], writes=[alog])
        b.dma(dr[:], ctx.t["Dr"][:], reads=[ctx.dbuf("Dr")], writes=[dr])
        b.dma(sel_b[:], ctx.t["sel32"][:], reads=[ctx.dbuf("sel32")], writes=[sel_b])
        b.op("act", lambda e: e.activation(a_sb[:], alog[:], AF.Exp), reads=[alog], writes=[a_sb])
        b.op("dve", lambda e: e.tensor_scalar_mul(a_sb[:], a_sb[:], -1.0), reads=[a_sb], writes=[a_sb])
        b.op("dve", lambda e: e.memset(h_st[:], 0.0), writes=[h_st])

        m_dt = ctx.shared["m_dt"]
        m_bc = ctx.shared["m_bc"]
        cbias = b.constant(1.0)

        for lh in range(2):
            l0 = lh * 1024
            for ds in range(4):
                for j in range(4):
                    ic = ds * 4 + j
                    xst = xs_t[j % 2]
                    b.dma(xst[:], xsb[bb, ic, :, l0:l0 + 1024], reads=[ctx.dbuf("xsb")], writes=[xst])
                    dlt = delta[j]
                    dxj = dx[j]
                    for h2 in range(2):
                        b.op("pe", lambda e, ic=ic, h2=h2: e.matmul(
                            p_d[:, h2 * 512:(h2 + 1) * 512], dtw_f[:, ic * 128:(ic + 1) * 128],
                            m_dt[:, l0 + h2 * 512:l0 + (h2 + 1) * 512], start=True, stop=True),
                            reads=[dtw_f, m_dt], writes=[p_d])
                    b.op("dve", lambda e, ic=ic: e.tensor_scalar(dta[:], p_d[:], dtbr[:, ic:ic + 1], None, ALU.add),
                         reads=[p_d, dtbr], writes=[dta])
                    b.op("act", lambda e, dxj=dxj: e.activation(dxj[:], dta[:], AF.Abs), reads=[dta], writes=[dxj])
                    b.op("act", lambda e, dxj=dxj, dlt=dlt: e.activation(dlt[:], dxj[:], AF.Exp, scale=-1.0), reads=[dxj], writes=[dlt])
                    b.op("act", lambda e, dlt=dlt, dxj=dxj: e.activation(dxj[:], dlt[:], AF.Ln, bias=cbias[:, 0:1]), reads=[dlt, cbias], writes=[dxj])
                    b.op("dve", lambda e, dlt=dlt: e.tensor_scalar_max(dlt[:], dta[:], 0.0), reads=[dta], writes=[dlt])
                    b.op("dve", lambda e, dlt=dlt, dxj=dxj: e.tensor_add(dlt[:], dlt[:], dxj[:]), reads=[dlt, dxj], writes=[dlt])
                    b.op("dve", lambda e, dlt=dlt, dxj=dxj, xst=xst: e.tensor_mul(dxj[:], dlt[:], xst[:]), reads=[dlt, xst], writes=[dxj])
                    b.op("dve", lambda e, j=j: e.memset(y_ac[j][:], 0.0), writes=[y_ac[j]])
                for n in range(DST):
                    for h2 in range(2):
                        b.op("pe", lambda e, n=n, h2=h2: e.matmul(
                            p_bm[:, h2 * 512:(h2 + 1) * 512], sel_b[:, n * 128:(n + 1) * 128],
                            m_bc[:, l0 + h2 * 512:l0 + (h2 + 1) * 512], start=True, stop=True),
                            reads=[sel_b, m_bc], writes=[p_bm])
                        b.op("pe", lambda e, n=n, h2=h2: e.matmul(
                            p_cm[:, h2 * 512:(h2 + 1) * 512], sel_b[:, (16 + n) * 128:(17 + n) * 128],
                            m_bc[:, l0 + h2 * 512:l0 + (h2 + 1) * 512], start=True, stop=True),
                            reads=[sel_b, m_bc], writes=[p_cm])
                    b.op("act", lambda e: e.copy(pbm_s[:], p_bm[:]), reads=[p_bm], writes=[pbm_s])
                    b.op("act", lambda e: e.copy(pcm_s[:], p_cm[:]), reads=[p_cm], writes=[pcm_s])
                    for j in range(4):
                        b.op("pool", lambda e, j=j: e.tensor_tensor(dbu4[j][:], dx[j][:], pbm_s[:], ALU.mult),
                             reads=[dx[j], pbm_s], writes=[dbu4[j]])
                    for j in range(4):
                        ic = ds * 4 + j
                        col = ic * DST + n
                        p2 = j % 2
                        da_t, dbu_t, h_t, hc_t = da_t2[p2], dbu4[j], h_t2[p2], hc_t2[p2]
                        b.op("act", lambda e, j=j, col=col, da_t=da_t: e.activation(
                            da_t[:], delta[j][:], AF.Exp, scale=a_sb[:, col:col + 1]),
                            reads=[delta[j], a_sb], writes=[da_t])
                        b.op("dve", lambda e, col=col, h_t=h_t, da_t=da_t, dbu_t=dbu_t: e.tensor_tensor_scan(
                            h_t[:], da_t[:], dbu_t[:], h_st[:, col:col + 1], ALU.mult, ALU.add),
                            reads=[da_t, dbu_t, h_st], writes=[h_t])
                        b.op("dve", lambda e, col=col, h_t=h_t: e.tensor_copy(h_st[:, col:col + 1], h_t[:, 1023:1024]),
                             reads=[h_t], writes=[h_st])
                        b.op("dve", lambda e, h_t=h_t, hc_t=hc_t: e.tensor_tensor(hc_t[:], h_t[:], pcm_s[:], ALU.mult),
                             reads=[h_t, pcm_s], writes=[hc_t])
                        b.op("pool", lambda e, j=j, hc_t=hc_t: e.tensor_add(y_ac[j][:], y_ac[j][:], hc_t[:]),
                             reads=[y_ac[j], hc_t], writes=[y_ac[j]])
                for j in range(4):
                    ic = ds * 4 + j
                    xst = xs_t[j % 2]
                    b.dma(zs_t[:], zsil[bb, ic, :, l0:l0 + 1024], reads=[ctx.dbuf("zsil")], writes=[zs_t])
                    b.dma(xst[:], xsb[bb, ic, :, l0:l0 + 1024], reads=[ctx.dbuf("xsb")], writes=[xst])
                    b.op("dve", lambda e, j=j, ic=ic, xst=xst: e.scalar_tensor_tensor(
                        hc_t[:], xst[:], dr[:, ic:ic + 1], y_ac[j][:], ALU.mult, ALU.add),
                        reads=[xst, dr, y_ac[j]], writes=[hc_t])
                    b.op("dve", lambda e, ic=ic: e.tensor_mul(
                        y_b[:, ic * 1024:(ic + 1) * 1024], hc_t[:], zs_t[:]),
                        reads=[hc_t, zs_t], writes=[y_b])
            for oh in range(2):
                for ic in range(NIC):
                    b.dma(wo_half[:, ic * 512:(ic + 1) * 512],
                          outp[ic * 128:(ic + 1) * 128, oh * 512:(oh + 1) * 512],
                          reads=[ctx.dbuf("outpt_bf")], writes=[wo_half])
                for ls in range(8):
                    tg = t0 + l0 + ls * 128
                    oti = ot[ls % 2]
                    b.dma(oti[:, 0:512], out[tg:tg + 128, oh * 512:(oh + 1) * 512],
                          reads=[ctx.dbuf("out", str(tg // 128) + "_" + str(oh))], writes=[oti])
                    for ic in range(NIC):
                        b.op("pe", lambda e, ic=ic, ls=ls: e.matmul(
                            p_op[:, 0:512],
                            y_b[:, ic * 1024 + ls * 128:ic * 1024 + ls * 128 + 128],
                            wo_half[:, ic * 512:(ic + 1) * 512],
                            start=(ic == 0), stop=(ic == NIC - 1)), reads=[y_b, wo_half], writes=[p_op])
                    b.op("dve", lambda e, oti=oti: e.tensor_add(oti[:, 0:512], oti[:, 0:512], p_op[:, 0:512]),
                         reads=[p_op], writes=[oti])
                    b.dma(out[tg:tg + 128, oh * 512:(oh + 1) * 512], oti[:, 0:512],
                          reads=[oti], writes=[ctx.dbuf("out", str(tg // 128) + "_" + str(oh))])


STAGE_DEFS = {}


def stage_l0front(b, ctx):
    es_p = ExitStack()
    stg = [TT(es_p.enter_context(b.nc.sbuf_tensor(b._uname("pp_s"), [128, 2048], F32, side="right")), Buf(f"pp_s{i}")) for i in range(3)]
    stb = [TT(es_p.enter_context(b.nc.sbuf_tensor(b._uname("pp_b"), [128, 2048], BF16, side="right")), Buf(f"pp_b{i}")) for i in range(3)]
    b.pump = prep_chunks(b, ctx, bufs=(stg, stb))
    stage_ln(b, ctx, "x", "acc", "ln0bt")
    stage_attn(b, ctx)
    if b.pump is not None:
        with b.stage("prepdrain"):
            for _ in b.pump:
                pass
        b.pump = None
    es_p.close()


def _register_stages():
    STAGE_DEFS.clear()
    STAGE_DEFS["s0"] = (stage_prep, (), ("w1t_bf", "w2t_bf", "inpt_bf", "outpt_bf"), False)
    STAGE_DEFS["l0f"] = (stage_l0front, (),
                         ("acc", "ln0bt", "attn", "w1t_bf", "w2t_bf", "inpt_bf", "outpt_bf"), False)
    STAGE_DEFS["s1"] = (lambda b, ctx: stage_ln(b, ctx, "x", "acc", "ln0bt"), (), ("acc", "ln0bt"), False)
    STAGE_DEFS["s2"] = (stage_attn, ("ln0bt",), ("attn",), False)
    STAGE_DEFS["s3r"] = (stage_router, ("ln0bt", "acc"), ("acc",), False)
    STAGE_DEFS["s3e"] = (stage_experts, ("ln0bt", "acc", "w1t_bf", "w2t_bf"), ("acc",), False)
    STAGE_DEFS["s4"] = (stage_assembly, ("acc", "attn"), ("x1", "ln1bt"), False)
    def tail1(b, ctx):
        with b.stage("tail1"):
            stage_assembly(b, ctx)
            stage_lilora(b, ctx)
    STAGE_DEFS["tail1"] = (tail1, ("acc", "attn"), ("x1", "ln1bt"), True)
    STAGE_DEFS["s6"] = (stage_lilora, ("x1", "ln1bt"), (), True)
    STAGE_DEFS["m0p"] = (lambda b, ctx: stage_mamba_pre(b, ctx, 0), ("ln1bt", "inpt_bf"), ("xsb", "zsil"), False)
    STAGE_DEFS["m0s"] = (lambda b, ctx: stage_mamba_scan(b, ctx, 0), ("xsb", "zsil", "outpt_bf"), (), True)
    STAGE_DEFS["m1p"] = (lambda b, ctx: stage_mamba_pre(b, ctx, 1), ("ln1bt", "inpt_bf"), ("xsb", "zsil"), False)
    STAGE_DEFS["m1s"] = (lambda b, ctx: stage_mamba_scan(b, ctx, 1), ("xsb", "zsil", "outpt_bf"), (), True)


_register_stages()


def build_program(stage_names):
    nc = bass.Bass("TRN2", target_bir_lowering=False)
    inter_in, inter_out, with_out = [], [], False
    produced = set()
    for sn in stage_names:
        fn, ii, io, wo = STAGE_DEFS[sn]
        for nm in ii:
            if nm not in produced and nm not in inter_in:
                inter_in.append(nm)
        for nm in io:
            produced.add(nm)
            if nm not in inter_out:
                inter_out.append(nm)
        with_out = with_out or wo
    if FUSED:
        pass
    assert not (set(inter_in) & set(inter_out)), (inter_in, inter_out)
    internal = FUSED and with_out
    ctx = Ctx(nc, inter_in=inter_in, inter_out=inter_out, with_out=with_out, internal=internal)
    if internal:
        inter_out = []
    es = ExitStack()
    with es:
        b = Builder(nc, es)
        for v in (1e-5, 1e-12, 1.0):
            b.constant(v)
        for sn in stage_names:
            STAGE_DEFS[sn][0](b, ctx)
    return nc, ctx, inter_in, inter_out, with_out


def prep_host_inputs(inputs):
    g = {k: np.asarray(v) for k, v in inputs.items()}
    shared = {}
    shared["w_qkv"] = np.ascontiguousarray(g["L0_attn_w_qkv"])
    shared["w_o"] = np.ascontiguousarray(g["L0_attn_w_o"])
    shared["wrT"] = np.ascontiguousarray(g["L0_ffn_Wr"].T)
    shared["br"] = np.ascontiguousarray(g["L0_ffn_br"].reshape(1, E))
    shared["w1T"] = np.ascontiguousarray(np.transpose(g["L0_ffn_W1"], (0, 2, 1)))
    shared["b1r"] = np.ascontiguousarray(g["L0_ffn_b1"].reshape(E, NFC, 128).transpose(2, 0, 1).reshape(128, E * NFC))
    shared["w2T"] = np.ascontiguousarray(np.transpose(g["L0_ffn_W2"], (0, 2, 1)))
    shared["b2"] = np.ascontiguousarray(g["L0_ffn_b2"])
    shared["ln_w"] = np.ascontiguousarray(g["L0_ffn_ln_w"].reshape(1, D))
    shared["ln_b"] = np.ascontiguousarray(g["L0_ffn_ln_b"].reshape(1, D))
    shared["inpT"] = np.ascontiguousarray(g["L1_attn_in_proj_w"].T)
    shared["conv_wr"] = np.ascontiguousarray(g["L1_attn_conv_w"].reshape(NIC, 128, DCONV).transpose(1, 0, 2).reshape(128, NIC * DCONV))
    shared["conv_br"] = np.ascontiguousarray(g["L1_attn_conv_b"].reshape(NIC, 128).T)
    shared["xpT"] = np.ascontiguousarray(g["L1_attn_x_proj_w"].T)
    shared["dtwT"] = np.ascontiguousarray(g["L1_attn_dt_proj_w"].T)
    shared["dtbr"] = np.ascontiguousarray(g["L1_attn_dt_proj_b"].reshape(NIC, 128).T)
    shared["alogr"] = np.ascontiguousarray(g["L1_attn_A_log"].reshape(NIC, 128, DST).transpose(1, 0, 2).reshape(128, NIC * DST))
    shared["Dr"] = np.ascontiguousarray(g["L1_attn_D"].reshape(NIC, 128).T)
    shared["outpT"] = np.ascontiguousarray(g["L1_attn_out_proj_w"].T)
    shared["Wt"] = np.ascontiguousarray(g["L1_ffn_W"].T)
    shared["bL1"] = np.ascontiguousarray(g["L1_ffn_b"].reshape(1, D))
    shared["ldT"] = np.ascontiguousarray(g["L1_ffn_lora_down"].T)
    shared["luT"] = np.ascontiguousarray(g["L1_ffn_lora_up"].T)
    shared["daux"] = np.ascontiguousarray(g["L1_ffn_down_aux"])
    shared["uauxT"] = np.ascontiguousarray(g["L1_ffn_up_aux"].T)
    shared["id128"] = np.eye(128, dtype=np.float32)
    sel = np.zeros((32, 32, 128), dtype=np.float32)
    for n in range(32):
        sel[n, n, :] = 1.0
    shared["sel32"] = np.ascontiguousarray(sel.transpose(1, 0, 2).reshape(32, 32 * 128)).astype(ml_dtypes.bfloat16)
    xs = [np.ascontiguousarray(g["L0_attn_inputs"][2 * c:2 * c + 2]) for c in range(8)]
    return shared, xs


def run_stage_groups(inputs, groups):
    shared, xs = prep_host_inputs(inputs)
    carry = [dict() for _ in range(8)]
    out = None
    for gi, grp in enumerate(groups):
        nc, ctx, inter_in, inter_out, with_out = build_program(grp)
        in_maps = []
        for c in range(8):
            m = dict(shared)
            m["x"] = xs[c]
            for nm in inter_in:
                m[nm] = carry[c][nm]
            in_maps.append(m)
        res = run_bass_kernel_spmd(nc, in_maps, core_ids=list(range(8)))
        for c in range(8):
            for nm in inter_out:
                carry[c][nm] = np.asarray(res.results[c][nm])
        if with_out:
            out = [np.asarray(res.results[c]["out"]) for c in range(8)]
    return carry, out


GROUPS_UNFUSED = [["s0"], ["s1", "s3r", "s3e"], ["s2"], ["s4"],
                  ["s6", "m0p", "m0s", "m1p", "m1s"]]
GROUPS_FUSED = [["l0f", "s3r", "s3e", "tail1", "m0p", "m0s", "m1p", "m1s"]]


def kernel(**inputs):
    groups = GROUPS_FUSED if FUSED else GROUPS_UNFUSED
    carry, out = run_stage_groups(inputs, groups)
    return np.concatenate(out, axis=0).reshape(16, L, D)


# version 22 (final)
# speedup vs baseline: 1.0956x; 1.0511x over previous
import numpy as np
import ml_dtypes
from contextlib import ExitStack

import concourse.bass as bass
import concourse.mybir as mybir
from concourse.bass_utils import run_bass_kernel_spmd

F32 = mybir.dt.float32
BF16 = mybir.dt.bfloat16
AF = mybir.ActivationFunctionType
ALU = mybir.AluOpType
AX = mybir.AxisListType

FUSED = True
SAME_ENGINE_SYNC = True

ENGINES = ("pe", "act", "dve", "pool", "qs")
_ENG_ATTR = {"pe": "tensor", "act": "scalar", "dve": "vector", "pool": "gpsimd", "qs": "sync"}
NQS = 12
NQP = 8


class Buf:
    __slots__ = ("name", "last_write", "readers", "exclusive", "last_access")

    def __init__(self, name, exclusive=False):
        self.name = name
        self.last_write = None
        self.readers = []
        self.exclusive = exclusive
        self.last_access = None


class Node:
    __slots__ = ("stream", "chan", "fn", "deps", "seq", "flush_id", "milestone", "rank", "tag")

    def __init__(self, stream, chan, fn, seq, flush_id, tag=""):
        self.stream = stream
        self.chan = chan
        self.fn = fn
        self.deps = []
        self.seq = seq
        self.flush_id = flush_id
        self.milestone = False
        self.rank = None
        self.tag = tag


class TT:
    __slots__ = ("h", "buf")

    def __init__(self, h, buf):
        self.h = h
        self.buf = buf

    def __getitem__(self, idx):
        return self.h[idx]

    @property
    def shape(self):
        return self.h.shape

    def rearrange(self, *a, **k):
        return self.h.rearrange(*a, **k)


class Builder:
    def __init__(self, nc, es):
        self.nc = nc
        self.es = es
        self.sems = {e: es.enter_context(nc.semaphore(f"sem_{e}")) for e in ("pe", "act", "dve", "pool")}
        for i in range(NQS):
            self.sems[f"qs{i}"] = es.enter_context(nc.semaphore(f"qsem{i}"))
        for i in range(NQP):
            self.sems[f"qp{i}"] = es.enter_context(nc.semaphore(f"qpsem{i}"))
        self.streams = {e: [] for e in ENGINES}
        self.seqs = {e: 0 for e in ENGINES}
        self.ranks = {e: 0 for e in ("pe", "act", "dve", "pool")}
        self.ranks.update({f"qs{i}": 0 for i in range(NQS)})
        self.ranks.update({f"qp{i}": 0 for i in range(NQP)})
        self.qs_count = 0
        self.qp_count = 0
        self.flush_id = 0
        self.stage_es = None
        self._consts = {}
        self.pump = None

    def pump_n(self, k):
        if self.pump is None:
            return
        for _ in range(k):
            try:
                next(self.pump)
            except StopIteration:
                self.pump = None
                return

    def constant(self, value):
        if value not in self._consts:
            t = self.es.enter_context(self.nc.sbuf_tensor(f"const_{len(self._consts)}", [128, 1], F32, side="right"))
            tt = TT(t, Buf(f"const_{value}"))
            self.op("dve", lambda e, tt=tt, value=value: e.memset(tt[:], float(value)), writes=[tt])
            self._consts[value] = tt
        return self._consts[value]

    def _uname(self, name):
        self._ctr = getattr(self, "_ctr", 0) + 1
        return f"{name}_{self._ctr}"

    def sbuf(self, name, shape, dtype, persistent=False):
        es = self.es if persistent else self.stage_es
        side = "right" if persistent else None
        t = es.enter_context(self.nc.sbuf_tensor(self._uname(name), shape, dtype, side=side))
        return TT(t, Buf(name))

    def psum(self, name, shape, dtype=F32):
        t = self.stage_es.enter_context(self.nc.psum_tensor(self._uname(name), shape, dtype))
        return TT(t, Buf(name, exclusive=True))

    def op(self, stream, fn, reads=(), writes=(), tag=""):
        chan = "qs" if stream == "qs" else stream
        n = Node(stream, chan, fn, self.seqs[stream], self.flush_id, tag)
        self.seqs[stream] += 1
        for r in reads:
            bf = r.buf if isinstance(r, TT) else r
            if bf.last_write is not None and bf.last_write is not n:
                n.deps.append(bf.last_write)
            if bf.exclusive and bf.last_access is not None and bf.last_access is not n:
                n.deps.append(bf.last_access)
            bf.readers.append(n)
            if bf.exclusive:
                bf.last_access = n
        for w in writes:
            bf = w.buf if isinstance(w, TT) else w
            if bf.last_write is not None and bf.last_write is not n:
                n.deps.append(bf.last_write)
            for rd in bf.readers:
                if rd is not n:
                    n.deps.append(rd)
            if bf.exclusive and bf.last_access is not None and bf.last_access is not n:
                n.deps.append(bf.last_access)
            bf.readers = []
            bf.last_write = n
            if bf.exclusive:
                bf.last_access = n
        self.streams[stream].append(n)
        return n

    def dma(self, out_ap, in_ap, reads=(), writes=(), tag="", stream="qs"):
        def fn(eng, out_ap=out_ap, in_ap=in_ap):
            return eng.dma_start(out=out_ap, in_=in_ap)
        n = self.op(stream, fn, reads=reads, writes=writes, tag=tag)
        n.chan = "qs"
        n.milestone = True
        return n

    def stage(self, name):
        builder = self

        class _Stage:
            def __enter__(self):
                if builder.stage_es is not None:
                    self.nested = True
                    return builder
                self.nested = False
                builder.stage_es = ExitStack()
                builder.stage_es.__enter__()
                return builder

            def __exit__(self, et, ev, tb):
                if self.nested:
                    return False
                if et is None:
                    builder.flush(name)
                builder.stage_es.__exit__(et, ev, tb)
                builder.stage_es = None
                return False

        return _Stage()

    def flush(self, name):
        fid = self.flush_id
        def skip_edge(d, n):
            if d.stream != n.stream:
                return False
            if d.stream == "pe":
                return True
            return not SAME_ENGINE_SYNC

        for e in ENGINES:
            for n in self.streams[e]:
                for d in n.deps:
                    if d.flush_id != fid:
                        continue
                    if skip_edge(d, n):
                        continue
                    d.milestone = True
        for e in ENGINES:
            for n in self.streams[e]:
                if not n.milestone:
                    continue
                if n.chan == "qs":
                    if n.stream == "pool":
                        lane = self.qp_count % NQP
                        self.qp_count += 1
                        n.chan = f"qp{lane}"
                    else:
                        lane = self.qs_count % NQS
                        self.qs_count += 1
                        n.chan = f"qs{lane}"
                self.ranks[n.chan] += 1
                n.rank = self.ranks[n.chan]

        sems = self.sems
        final_qs = {c: self.ranks[c] for c in self.ranks if c.startswith("q")}

        def waitval(d):
            return d.rank * 16 if d.chan.startswith("q") else d.rank

        with self.nc.Block(name) as block:
            for e in ENGINES:
                nodes = self.streams[e]
                if not nodes and e != "qs":
                    continue

                def body(eng, nodes=nodes, e=e):
                    vclock = {c: 0 for c in self.ranks}
                    for n in nodes:
                        deps = {id(d): d for d in n.deps}.values()
                        for d in sorted(deps, key=lambda d: (d.chan, d.rank or 0)):
                            if d.flush_id != fid or not d.milestone:
                                continue
                            if skip_edge(d, n):
                                continue
                            if vclock[d.chan] >= d.rank:
                                continue
                            eng.wait_ge(sems[d.chan], waitval(d))
                            vclock[d.chan] = d.rank
                        if n.milestone and n.chan.startswith("q") and n.rank > 1 and vclock[n.chan] < n.rank - 1:
                            eng.wait_ge(sems[n.chan], (n.rank - 1) * 16)
                            vclock[n.chan] = n.rank - 1
                        ins = n.fn(eng)
                        if n.milestone:
                            ins.then_inc(sems[n.chan], 16 if n.chan.startswith("q") else 1)
                    if e == "qs":
                        for c, r in final_qs.items():
                            if r > 0:
                                eng.wait_ge(sems[c], r * 16)

                getattr(block, _ENG_ATTR[e])(body)

        self.streams = {e: [] for e in ENGINES}
        self.flush_id += 1


B = 2
L = 2048
D = 1024
T = B * L
NH, HD = 16, 64
E, F = 8, 4096
DIN, DST, DCONV, DTR = 2048, 16, 4, 64

NDC = D // 128
NTC = T // 128
NFC = F // 128
NIC = DIN // 128

INTERMEDIATES = {
    "acc": ([T, D], F32),
    "ln0bt": ([NDC, 128, T], BF16),
    "attn": ([T, D], F32),
    "x1": ([T, D], F32),
    "ln1bt": ([NDC, 128, T], BF16),
    "w1t_bf": ([E, D, F], BF16),
    "w2t_bf": ([E, F, D], BF16),
    "inpt_bf": ([D, 2 * DIN], BF16),
    "outpt_bf": ([DIN, D], BF16),
    "xsb": ([B, NIC, 128, L], F32),
    "zsil": ([B, NIC, 128, L], F32),
}

EXT_IN = {
    "x": ([B, L, D], F32),
    "w_qkv": ([NH, 3, D, HD], F32),
    "w_o": ([D, D], F32),
    "wrT": ([D, E], F32),
    "br": ([1, E], F32),
    "w1T": ([E, D, F], F32),
    "b1r": ([128, E * NFC], F32),
    "w2T": ([E, F, D], F32),
    "b2": ([E, D], F32),
    "ln_w": ([1, D], F32),
    "ln_b": ([1, D], F32),
    "inpT": ([D, 2 * DIN], F32),
    "conv_wr": ([128, NIC * DCONV], F32),
    "conv_br": ([128, NIC], F32),
    "xpT": ([DIN, 96], F32),
    "dtwT": ([DTR, DIN], F32),
    "dtbr": ([128, NIC], F32),
    "alogr": ([128, NIC * DST], F32),
    "Dr": ([128, NIC], F32),
    "outpT": ([DIN, D], F32),
    "Wt": ([D, D], F32),
    "bL1": ([1, D], F32),
    "ldT": ([100, 4], F32),
    "luT": ([4, 50], F32),
    "daux": ([100, D], F32),
    "uauxT": ([50, D], F32),
    "id128": ([128, 128], F32),
    "sel32": ([32, 32 * 128], BF16),
}

OUTPUT = ("out", [B, L, D], F32)


class Ctx:
    def __init__(self, nc, inter_in=(), inter_out=(), with_out=False, internal=False):
        self.nc = nc
        self.t = {}
        for name, (shape, dt) in EXT_IN.items():
            self.t[name] = nc.dram_tensor(name, shape, dt, kind="ExternalInput")
        for name in inter_in:
            shape, dt = INTERMEDIATES[name]
            self.t[name] = nc.dram_tensor(name, shape, dt, kind="ExternalInput")
        for name in inter_out:
            shape, dt = INTERMEDIATES[name]
            self.t[name] = nc.dram_tensor(name, shape, dt, kind="Internal" if internal else "ExternalOutput")
        if with_out:
            self.t["out"] = nc.dram_tensor(OUTPUT[0], OUTPUT[1], OUTPUT[2], kind="ExternalOutput")
        self.bufs = {}
        self.shared = {}

    def dbuf(self, name, sub=""):
        key = (name, sub)
        if key not in self.bufs:
            self.bufs[key] = Buf(f"{name}.{sub}")
        return self.bufs[key]


def stage_ln(b, ctx, src_name, dst_f32_name, dst_bt_name, eps=1e-5, tagp="ln"):
    src = ctx.t[src_name]
    src_flat = src if len(src.shape) == 2 else src.rearrange("b l d -> (b l) d")
    dst_f32 = ctx.t[dst_f32_name] if dst_f32_name else None
    dst_bt = ctx.t[dst_bt_name] if dst_bt_name else None
    with b.stage(tagp):
        xt = [b.sbuf(f"{tagp}_x{i}", [128, D], F32) for i in range(2)]
        nt = [b.sbuf(f"{tagp}_n{i}", [128, D], F32) for i in range(2)]
        nb = [b.sbuf(f"{tagp}_nb{i}", [128, D], BF16) for i in range(2)]
        stg = [b.sbuf(f"{tagp}_stg{i}", [128, 128], BF16) for i in range(4)]
        bn6 = [b.sbuf(f"{tagp}_bn6{i}", [128, 12], F32) for i in range(2)]
        bn2 = [b.sbuf(f"{tagp}_bn2{i}", [128, 2], F32) for i in range(2)]
        rstd = [b.sbuf(f"{tagp}_rstd{i}", [128, 1], F32) for i in range(2)]
        nmr = [b.sbuf(f"{tagp}_nmr{i}", [128, 1], F32) for i in range(2)]
        idb = b.sbuf(f"{tagp}_idb", [128, 128], BF16)
        idf = b.sbuf(f"{tagp}_idf", [128, 128], F32)
        ptr = [b.psum(f"{tagp}_ptr{i}", [128, 128], BF16) for i in range(2)]

        b.dma(idf[:], ctx.t["id128"][:], reads=[ctx.dbuf("id128")], writes=[idf])
        b.op("dve", lambda e: e.tensor_copy(idb[:], idf[:]), reads=[idf], writes=[idb])

        for tc in range(NTC):
            b.pump_n(4)
            i = tc % 2
            x, n_, nbt = xt[i], nt[i], nb[i]
            b6, b2_, rs, nm = bn6[i], bn2[i], rstd[i], nmr[i]
            b.dma(x[:], src_flat[tc * 128:(tc + 1) * 128, :], reads=[ctx.dbuf(src_name)], writes=[x])
            b.op("dve", lambda e, x=x, b6=b6: e.bn_stats(b6[:, 0:6], x[:, 0:512]), reads=[x], writes=[b6])
            b.op("dve", lambda e, x=x, b6=b6: e.bn_stats(b6[:, 6:12], x[:, 512:1024]), reads=[x], writes=[b6])
            b.op("dve", lambda e, b6=b6, b2_=b2_: e.bn_aggr(b2_[:], b6[:]), reads=[b6], writes=[b2_])
            ceps = b.constant(float(eps))
            b.op("act", lambda e, b2_=b2_, rs=rs, ceps=ceps: e.activation(rs[:], b2_[:, 1:2], AF.Sqrt, bias=ceps[:, 0:1]),
                 reads=[b2_, ceps], writes=[rs])
            b.op("dve", lambda e, rs=rs: e.reciprocal(rs[:], rs[:]), reads=[rs], writes=[rs])
            b.op("dve", lambda e, b2_=b2_, rs=rs, nm=nm: e.scalar_tensor_tensor(nm[:], b2_[:, 0:1], -1.0, rs[:], ALU.mult, ALU.mult),
                 reads=[b2_, rs], writes=[nm])
            if dst_f32 is not None:
                b.op("dve", lambda e, x=x, n_=n_, rs=rs, nm=nm: e.tensor_scalar(n_[:], x[:], rs[:, 0:1], nm[:, 0:1], ALU.mult, ALU.add),
                     reads=[x, rs, nm], writes=[n_])
                b.dma(dst_f32[tc * 128:(tc + 1) * 128, :], n_[:], reads=[n_], writes=[ctx.dbuf(dst_f32_name)])
            if dst_bt is not None:
                b.op("dve", lambda e, x=x, nbt=nbt, rs=rs, nm=nm: e.tensor_scalar(nbt[:], x[:], rs[:, 0:1], nm[:, 0:1], ALU.mult, ALU.add),
                     reads=[x, rs, nm], writes=[nbt])
                for dc in range(NDC):
                    p = ptr[dc % 2]
                    s = stg[dc % 4]
                    b.op("pe", lambda e, nbt=nbt, dc=dc, p=p: e.transpose(p[:], nbt[:, dc * 128:(dc + 1) * 128], idb[:]),
                         reads=[nbt, idb], writes=[p])
                    b.op("dve", lambda e, p=p, s=s: e.tensor_copy(s[:], p[:]), reads=[p], writes=[s])
                    b.dma(dst_bt[dc, :, tc * 128:(tc + 1) * 128], s[:], reads=[s], writes=[ctx.dbuf(dst_bt_name)])


def prep_chunks(b, ctx, bufs=None):
    jobs = [
        ("w1T", "w1t_bf", E * D, F),
        ("w2T", "w2t_bf", E * F, D),
        ("inpT", "inpt_bf", D, 2 * DIN),
        ("outpT", "outpt_bf", DIN, D),
    ]
    if bufs is None:
        stg = [b.sbuf(f"pp_s{i}", [128, 2048], F32, persistent=True) for i in range(3)]
        stb = [b.sbuf(f"pp_b{i}", [128, 2048], BF16, persistent=True) for i in range(3)]
    else:
        stg, stb = bufs
    eng_rr = ["dve", "act"]
    k = 0
    for src_name, dst_name, rows, cols in jobs:
        src = ctx.t[src_name]
        dst = ctx.t[dst_name]
        src2 = src.rearrange(_flat2(src.shape))
        dst2 = dst.rearrange(_flat2(dst.shape))
        cb = min(cols, 2048)
        ncb = cols // cb
        g = max(1, 2048 // cols)
        rpc = 128 * g
        assert rows % rpc == 0 and cols % cb == 0
        for ch in range(rows // rpc):
            for cbi in range(ncb):
                i = k % 3
                eng = eng_rr[k % 2]
                s, sb = stg[i], stb[i]
                w = g * cb
                sview = src2[ch * rpc:(ch + 1) * rpc, cbi * cb:(cbi + 1) * cb].rearrange("(g p) c -> p g c", p=128)
                dview = dst2[ch * rpc:(ch + 1) * rpc, cbi * cb:(cbi + 1) * cb].rearrange("(g p) c -> p g c", p=128)
                b.dma(s[:, 0:w].rearrange("p (g c) -> p g c", g=g), sview, reads=[ctx.dbuf(src_name)], writes=[s], stream="pool")
                b.op("pool", lambda e, s=s, sb=sb, w=w: e.tensor_copy(sb[:, 0:w], s[:, 0:w]), reads=[s], writes=[sb])
                b.dma(dview, sb[:, 0:w].rearrange("p (g c) -> p g c", g=g), reads=[sb], writes=[ctx.dbuf(dst_name)], stream="pool")
                k += 1
                yield


def stage_prep(b, ctx):
    with b.stage("prep"):
        for _ in prep_chunks(b, ctx):
            pass


def _flat2(shape):
    nd = len(shape)
    if nd == 2:
        return "a b -> a b"
    names = [chr(ord('a') + i) for i in range(nd)]
    return f"{' '.join(names)} -> ({' '.join(names[:-1])}) {names[-1]}"


def stage_attn(b, ctx):
    ln0bt = ctx.t["ln0bt"]
    attn = ctx.t["attn"]
    with b.stage("attn"):
        wq_f = b.sbuf("at_wqf", [128, 3 * NDC * HD], F32)
        wq_b = b.sbuf("at_wqb", [128, 3 * NDC * HD], BF16)
        wo_f = b.sbuf("at_wof", [128, D], F32)
        wo_b = b.sbuf("at_wob", [128, NDC * D], BF16)
        lb = b.sbuf("at_lb", [128, NDC * L], BF16)
        qT = b.sbuf("at_qT", [64, L], BF16)
        kT = b.sbuf("at_kT", [64, L], BF16)
        vb = b.sbuf("at_vb", [128, 16 * 65], BF16)
        et = [b.sbuf(f"at_et{i}", [128, 1024], BF16) for i in range(4)]
        stmp = b.sbuf("at_stmp", [1, L], F32)
        s_all = b.sbuf("at_sall", [16, L], F32)
        s_inv = b.sbuf("at_sinv", [16, L], F32)
        s_invb = b.sbuf("at_sinvb", [16, L], BF16)
        ct_un = b.sbuf("at_ctun", [128, NDC * L], BF16)
        ct = ct_un
        sel_b = b.sbuf("at_selb", [32, 32 * 128], BF16)
        wo_stage = [b.sbuf(f"at_wos{i}", [128, D], F32) for i in range(2)]
        p_s = [b.psum(f"at_ps{i}", [128, 1024], F32) for i in range(2)]
        p_o = b.psum("at_po", [65, L], F32)

        b.dma(sel_b[:], ctx.t["sel32"][:], reads=[ctx.dbuf("sel32")], writes=[sel_b])
        for ic in range(NDC):
            b.dma(wo_f[:], ctx.t["w_o"][ic * 128:(ic + 1) * 128, :], reads=[ctx.dbuf("w_o")], writes=[wo_f])
            b.op("dve", lambda e, ic=ic: e.tensor_copy(wo_b[:, ic * D:(ic + 1) * D], wo_f[:]),
                 reads=[wo_f], writes=[wo_b])
        for kc in range(16):
            b.op("dve", lambda e, kc=kc: e.memset(vb[:, kc * 65 + 64:kc * 65 + 65], 1.0), writes=[vb])

        for bb in range(B):
            t0 = bb * L
            for dc in range(NDC):
                b.dma(lb[:, dc * L:(dc + 1) * L], ln0bt[dc, :, t0:t0 + L],
                      reads=[ctx.dbuf("ln0bt")], writes=[lb])
            for h in range(NH):
                b.pump_n(4)
                for a in range(3):
                    wsrc = ctx.t["w_qkv"][h, a].rearrange("(dc p) n -> p dc n", p=128)
                    wdst = wq_f[:, a * NDC * HD:(a + 1) * NDC * HD].rearrange("p (dc n) -> p dc n", dc=NDC)
                    b.dma(wdst, wsrc, reads=[ctx.dbuf("w_qkv")], writes=[wq_f])
                b.op("dve", lambda e: e.tensor_copy(wq_b[:], wq_f[:]), reads=[wq_f], writes=[wq_b])
                for a, dstT, scl in ((0, qT, 0.125), (1, kT, 1.0)):
                    for hb in range(2):
                        ps = p_s[hb]
                        for qb2 in range(2):
                            c0 = qb2 * 512
                            q0 = hb * 1024 + qb2 * 512
                            for dc in range(NDC):
                                b.op("pe", lambda e, a=a, dc=dc, c0=c0, q0=q0, ps=ps: e.matmul(
                                    ps[0:64, c0:c0 + 512],
                                    wq_b[:, (a * NDC + dc) * HD:(a * NDC + dc + 1) * HD],
                                    lb[:, dc * L + q0:dc * L + q0 + 512],
                                    start=(dc == 0), stop=(dc == NDC - 1)),
                                    reads=[wq_b, lb], writes=[ps])
                        b.op("dve", lambda e, dstT=dstT, hb=hb, scl=scl, ps=ps: e.tensor_scalar_mul(
                            dstT[:, hb * 1024:(hb + 1) * 1024], ps[0:64, :], scl), reads=[ps], writes=[dstT])
                for kc in range(16):
                    ps = p_s[kc % 2]
                    for dc in range(NDC):
                        b.op("pe", lambda e, kc=kc, dc=dc, ps=ps: e.matmul(
                            ps[:, 0:64],
                            lb[:, dc * L + kc * 128:dc * L + (kc + 1) * 128],
                            wq_b[:, (2 * NDC + dc) * HD:(2 * NDC + dc + 1) * HD],
                            start=(dc == 0), stop=(dc == NDC - 1)),
                            reads=[lb, wq_b], writes=[ps])
                    b.op("dve", lambda e, kc=kc, ps=ps: e.tensor_copy(vb[:, kc * 65:kc * 65 + 64], ps[:, 0:64]),
                         reads=[ps], writes=[vb])
                for kc in range(16):
                    for hb in range(2):
                        ps = p_s[hb]
                        eti = et[2 * (kc % 2) + hb]
                        for qb2 in range(2):
                            q0 = hb * 1024 + qb2 * 512
                            b.op("pe", lambda e, kc=kc, q0=q0, qb2=qb2, ps=ps: e.matmul(
                                ps[:, qb2 * 512:(qb2 + 1) * 512],
                                kT[:, kc * 128:(kc + 1) * 128],
                                qT[:, q0:q0 + 512],
                                start=True, stop=True), reads=[kT, qT], writes=[ps])
                        b.op("act", lambda e, eti=eti, ps=ps: e.activation(eti[:], ps[:], AF.Exp),
                             reads=[ps], writes=[eti])
                        for qb2 in range(2):
                            q0 = hb * 1024 + qb2 * 512
                            b.op("pe", lambda e, kc=kc, eti=eti, q0=q0, qb2=qb2: e.matmul(
                                p_o[:, q0:q0 + 512],
                                vb[:, kc * 65:(kc + 1) * 65],
                                eti[:, qb2 * 512:(qb2 + 1) * 512],
                                start=(kc == 0), stop=(kc == 15)), reads=[vb, eti], writes=[p_o])
                b.op("act", lambda e: e.copy(stmp[:], p_o[64:65, :]), reads=[p_o], writes=[stmp])
                b.dma(s_all[h:h + 1, :], stmp[:], reads=[stmp], writes=[s_all], tag="s_gather")
                ic, r0 = h // 2, (h % 2) * 64
                b.op("dve", lambda e, ic=ic, r0=r0: e.tensor_copy(ct_un[r0:r0 + 64, ic * L:(ic + 1) * L], p_o[0:64, :]),
                     reads=[p_o], writes=[ct_un])
            b.op("dve", lambda e: e.reciprocal(s_inv[:], s_all[:]), reads=[s_all], writes=[s_inv])
            b.op("dve", lambda e: e.tensor_copy(s_invb[:], s_inv[:]), reads=[s_inv], writes=[s_invb])
            for h in range(NH):
                ic, r0 = h // 2, (h % 2) * 64
                ps = p_s[h % 2]
                for hb in range(2):
                    b.op("pe", lambda e, h=h, hb=hb, ps=ps: e.matmul(
                        ps[0:64, hb * 512:(hb + 1) * 512],
                        sel_b[0:16, h * 128:h * 128 + 64],
                        s_invb[:, hb * 512:(hb + 1) * 512],
                        start=True, stop=True), reads=[sel_b, s_invb], writes=[ps])
                    b.op("pe", lambda e, h=h, hb=hb, ps=ps: e.matmul(
                        ps[64:128, hb * 512:(hb + 1) * 512],
                        sel_b[0:16, h * 128 + 64:h * 128 + 128],
                        s_invb[:, 1024 + hb * 512:1024 + (hb + 1) * 512],
                        start=True, stop=True), reads=[sel_b, s_invb], writes=[ps])
                b.op("dve", lambda e, ic=ic, r0=r0, ps=ps: e.tensor_tensor(
                    ct[r0:r0 + 64, ic * L:ic * L + 1024], ct_un[r0:r0 + 64, ic * L:ic * L + 1024],
                    ps[0:64, :], ALU.mult), reads=[ct_un, ps], writes=[ct])
                b.op("dve", lambda e, ic=ic, r0=r0, ps=ps: e.tensor_tensor(
                    ct[r0:r0 + 64, ic * L + 1024:ic * L + 2048], ct_un[r0:r0 + 64, ic * L + 1024:ic * L + 2048],
                    ps[64:128, :], ALU.mult), reads=[ct_un, ps], writes=[ct])
            for tc in range(16):
                ps = p_s[tc % 2]
                for half in range(2):
                    for ic in range(NDC):
                        b.op("pe", lambda e, tc=tc, ic=ic, half=half, ps=ps: e.matmul(
                            ps[:, half * 512:(half + 1) * 512],
                            ct[:, ic * L + tc * 128:ic * L + tc * 128 + 128],
                            wo_b[:, ic * D + half * 512:ic * D + (half + 1) * 512],
                            start=(ic == 0), stop=(ic == NDC - 1)), reads=[ct, wo_b], writes=[ps])
                ws = wo_stage[tc % 2]
                b.op("dve", lambda e, ws=ws, ps=ps: e.tensor_copy(ws[:], ps[:]), reads=[ps], writes=[ws])
                b.dma(attn[t0 + tc * 128:t0 + (tc + 1) * 128, :], ws[:],
                      reads=[ws], writes=[ctx.dbuf("attn")])


def stage_router(b, ctx):
    ln0bt = ctx.t["ln0bt"]
    acc = ctx.t["acc"]
    with b.stage("router"):
        cw_all = b.sbuf("r_cwall", [128, NTC * E], F32, persistent=True)
        cwt_all = b.sbuf("r_cwtall", [8, NTC * 128], BF16, persistent=True)
        ctx.shared["cw_all"] = cw_all
        ctx.shared["cwt_all"] = cwt_all
        wr_f = b.sbuf("r_wrf", [128, NDC * E], F32)
        wr_b = b.sbuf("r_wrb", [128, NDC * E], BF16)
        br_f = b.sbuf("r_brf", [1, E], F32)
        br_b = b.sbuf("r_brb", [1, E], BF16)
        b2_f = b.sbuf("r_b2f", [E, D], F32)
        b2_b = b.sbuf("r_b2b", [E, D], BF16)
        ones_b = b.sbuf("r_ones", [1, 128], BF16)
        idf = b.sbuf("r_idf", [128, 128], F32)
        lt = [b.sbuf(f"r_lt{i}", [128, NDC * 128], BF16) for i in range(2)]
        lg = b.sbuf("r_lg", [128, E], F32)
        m1 = b.sbuf("r_m1", [128, 1], F32)
        m2 = b.sbuf("r_m2", [128, 1], F32)
        nm1 = b.sbuf("r_nm1", [128, 1], F32)
        msk = b.sbuf("r_msk", [128, E], F32)
        lg2 = b.sbuf("r_lg2", [128, E], F32)
        sel = b.sbuf("r_sel", [128, E], F32)
        expt = b.sbuf("r_expt", [128, E], F32)
        wun = b.sbuf("r_wun", [128, E], F32)
        zz = b.sbuf("r_z", [128, 1], F32)
        zi = b.sbuf("r_zi", [128, 1], F32)
        ctf = b.sbuf("r_ctf", [8, 128], F32)
        at = [b.sbuf(f"r_at{i}", [128, D], F32) for i in range(2)]
        ot = [b.sbuf(f"r_ot{i}", [128, D], F32) for i in range(2)]
        p_r = b.psum("r_pr", [128, 512], F32)
        p_ct = b.psum("r_pct", [8, 128], F32)
        p_b2 = b.psum("r_pb2", [128, D], F32)

        wsrc = ctx.t["wrT"].rearrange("(dc p) e -> p dc e", p=128)
        b.dma(wr_f.rearrange("p (dc e) -> p dc e", dc=NDC), wsrc, reads=[ctx.dbuf("wrT")], writes=[wr_f])
        b.op("dve", lambda e: e.tensor_copy(wr_b[:], wr_f[:]), reads=[wr_f], writes=[wr_b])
        b.dma(br_f[:], ctx.t["br"][:], reads=[ctx.dbuf("br")], writes=[br_f])
        b.op("dve", lambda e: e.tensor_copy(br_b[:], br_f[:]), reads=[br_f], writes=[br_b])
        b.dma(b2_f[:], ctx.t["b2"][:], reads=[ctx.dbuf("b2")], writes=[b2_f])
        b.op("dve", lambda e: e.tensor_copy(b2_b[:], b2_f[:]), reads=[b2_f], writes=[b2_b])
        b.op("dve", lambda e: e.memset(ones_b[:], 1.0), writes=[ones_b])
        b.dma(idf[:], ctx.t["id128"][:], reads=[ctx.dbuf("id128")], writes=[idf])

        for tc in range(NTC):
            lti = lt[tc % 2]
            for dc in range(NDC):
                b.dma(lti[:, dc * 128:(dc + 1) * 128], ln0bt[dc, :, tc * 128:(tc + 1) * 128],
                      reads=[ctx.dbuf("ln0bt")], writes=[lti])
            for dc in range(NDC):
                b.op("pe", lambda e, lti=lti, dc=dc: e.matmul(
                    p_r[:, 0:E], lti[:, dc * 128:(dc + 1) * 128], wr_b[:, dc * E:(dc + 1) * E],
                    start=(dc == 0), stop=False), reads=[lti, wr_b], writes=[p_r])
            b.op("pe", lambda e: e.matmul(p_r[:, 0:E], ones_b[:], br_b[:], start=False, stop=True),
                 reads=[ones_b, br_b], writes=[p_r])
            b.op("dve", lambda e: e.tensor_copy(lg[:], p_r[:, 0:E]), reads=[p_r], writes=[lg])
            b.op("dve", lambda e: e.reduce_max(m1[:], lg[:], axis=AX.X), reads=[lg], writes=[m1])
            b.op("dve", lambda e: e.tensor_scalar(msk[:], lg[:], m1[:, 0:1], None, ALU.is_ge), reads=[lg, m1], writes=[msk])
            b.op("dve", lambda e: e.scalar_tensor_tensor(lg2[:], msk[:], -1e30, lg[:], ALU.mult, ALU.add),
                 reads=[msk, lg], writes=[lg2])
            b.op("dve", lambda e: e.reduce_max(m2[:], lg2[:], axis=AX.X), reads=[lg2], writes=[m2])
            b.op("dve", lambda e: e.tensor_scalar(sel[:], lg[:], m2[:, 0:1], None, ALU.is_ge), reads=[lg, m2], writes=[sel])
            b.op("dve", lambda e: e.tensor_scalar_mul(nm1[:], m1[:], -1.0), reads=[m1], writes=[nm1])
            b.op("act", lambda e: e.activation(expt[:], lg[:], AF.Exp, bias=nm1[:, 0:1]), reads=[lg, nm1], writes=[expt])
            b.op("dve", lambda e: e.tensor_mul(wun[:], sel[:], expt[:]), reads=[sel, expt], writes=[wun])
            b.op("dve", lambda e: e.reduce_sum(zz[:], wun[:], axis=AX.X), reads=[wun], writes=[zz])
            b.op("dve", lambda e: e.reciprocal(zi[:], zz[:]), reads=[zz], writes=[zi])
            b.op("dve", lambda e, tc=tc: e.tensor_scalar(cw_all[:, tc * E:(tc + 1) * E], wun[:], zi[:, 0:1], None, ALU.mult),
                 reads=[wun, zi], writes=[cw_all])
            b.op("pe", lambda e, tc=tc: e.transpose(p_ct[:], cw_all[:, tc * E:(tc + 1) * E], idf[:]),
                 reads=[cw_all, idf], writes=[p_ct])
            b.op("dve", lambda e, tc=tc: e.tensor_copy(cwt_all[:, tc * 128:(tc + 1) * 128], p_ct[:]),
                 reads=[p_ct], writes=[cwt_all])
            ati, oti = at[tc % 2], ot[tc % 2]
            b.dma(ati[:], acc[tc * 128:(tc + 1) * 128, :], reads=[ctx.dbuf("acc", str(tc))], writes=[ati])
            for half in range(2):
                b.op("pe", lambda e, tc=tc, half=half: e.matmul(
                    p_b2[:, half * 512:(half + 1) * 512], cwt_all[:, tc * 128:(tc + 1) * 128],
                    b2_b[:, half * 512:(half + 1) * 512], start=True, stop=True),
                    reads=[cwt_all, b2_b], writes=[p_b2])
            b.op("dve", lambda e, ati=ati, oti=oti: e.tensor_add(oti[:], ati[:], p_b2[:]),
                 reads=[ati, p_b2], writes=[oti])
            b.dma(acc[tc * 128:(tc + 1) * 128, :], oti[:], reads=[oti], writes=[ctx.dbuf("acc", str(tc))])


def stage_experts(b, ctx):
    ln0bt = ctx.t["ln0bt"]
    acc = ctx.t["acc"]
    w1t = ctx.t["w1t_bf"]
    w2t = ctx.t["w2t_bf"]
    cw_all = ctx.shared["cw_all"]
    with b.stage("experts"):
        b1_sb = b.sbuf("e_b1", [128, E * NFC], F32)
        w2_b = b.sbuf("e_w2", [128, NFC * D], BF16)
        w1_sl = [b.sbuf(f"e_w1{i}", [128, NDC * 512], BF16) for i in range(2)]
        lt = [b.sbuf(f"e_lt{i}", [128, NDC * 1024], BF16) for i in range(2)]
        h_bf = b.sbuf("e_h", [128, NFC * 1024], BF16)
        at = [b.sbuf(f"e_at{i}", [128, D], F32) for i in range(2)]
        ot = [b.sbuf(f"e_ot{i}", [128, D], F32) for i in range(2)]
        p_h = [b.psum(f"e_ph{i}", [128, 1024], F32) for i in range(2)]
        p_y = [b.psum(f"e_py{i}", [128, 1024], F32) for i in range(2)]

        b.dma(b1_sb[:], ctx.t["b1r"][:], reads=[ctx.dbuf("b1r")], writes=[b1_sb])

        for ee in range(E):
            for q4 in range(4):
                nsub = NFC // 4
                wsrc = w2t[ee, q4 * nsub * 128:(q4 + 1) * nsub * 128, :].rearrange("(fc p) o -> p fc o", p=128)
                wdst = w2_b[:, q4 * nsub * D:(q4 + 1) * nsub * D].rearrange("p (fc o) -> p fc o", fc=nsub)
                b.dma(wdst, wsrc, reads=[ctx.dbuf("w2t_bf")], writes=[w2_b])
            for tblk in range(4):
                lti = lt[tblk % 2]
                b.dma(lti.rearrange("p (dc t) -> p dc t", dc=NDC),
                      ln0bt[:, :, tblk * 1024:(tblk + 1) * 1024].rearrange("dc p t -> p dc t"),
                      reads=[ctx.dbuf("ln0bt")], writes=[lti])
                for fg in range(8):
                    w1i = w1_sl[fg % 2]
                    b.dma(w1i.rearrange("p (dc f) -> p dc f", dc=NDC),
                          w1t[ee].rearrange("(dc p) f -> p dc f", p=128)[:, :, fg * 512:(fg + 1) * 512],
                          reads=[ctx.dbuf("w1t_bf")], writes=[w1i])
                    for fcl in range(4):
                        fc = fg * 4 + fcl
                        p = p_h[fc % 2]
                        for dc in range(NDC):
                            for h2 in range(2):
                                b.op("pe", lambda e, w1i=w1i, lti=lti, p=p, dc=dc, h2=h2, fcl=fcl: e.matmul(
                                    p[:, h2 * 512:(h2 + 1) * 512],
                                    w1i[:, dc * 512 + fcl * 128:dc * 512 + (fcl + 1) * 128],
                                    lti[:, dc * 1024 + h2 * 512:dc * 1024 + (h2 + 1) * 512],
                                    start=(dc == 0), stop=(dc == NDC - 1)),
                                    reads=[w1i, lti], writes=[p])
                        b.op("act", lambda e, p=p, fc=fc, ee=ee: e.activation(
                            h_bf[:, fc * 1024:(fc + 1) * 1024], p[:], AF.Gelu,
                            bias=b1_sb[:, ee * NFC + fc:ee * NFC + fc + 1]),
                            reads=[p, b1_sb], writes=[h_bf])
                for tc8 in range(8):
                    tgl = tblk * 8 + tc8
                    p = p_y[tc8 % 2]
                    ati, oti = at[tc8 % 2], ot[tc8 % 2]
                    b.dma(ati[:], acc[tgl * 128:(tgl + 1) * 128, :], reads=[ctx.dbuf("acc", str(tgl))], writes=[ati])
                    for fc in range(NFC):
                        for h2 in range(2):
                            b.op("pe", lambda e, p=p, fc=fc, tc8=tc8, h2=h2: e.matmul(
                                p[:, h2 * 512:(h2 + 1) * 512],
                                h_bf[:, fc * 1024 + tc8 * 128:fc * 1024 + tc8 * 128 + 128],
                                w2_b[:, fc * D + h2 * 512:fc * D + (h2 + 1) * 512],
                                start=(fc == 0), stop=(fc == NFC - 1)),
                                reads=[h_bf, w2_b], writes=[p])
                    b.op("dve", lambda e, p=p, ati=ati, oti=oti, tgl=tgl, ee=ee: e.scalar_tensor_tensor(
                        oti[:], p[:], cw_all[:, tgl * E + ee:tgl * E + ee + 1], ati[:], ALU.mult, ALU.add),
                        reads=[p, cw_all, ati], writes=[oti])
                    b.dma(acc[tgl * 128:(tgl + 1) * 128, :], oti[:], reads=[oti], writes=[ctx.dbuf("acc", str(tgl))])


def stage_assembly(b, ctx):
    acc = ctx.t["acc"]
    attn = ctx.t["attn"]
    x = ctx.t["x"].rearrange("b l d -> (b l) d")
    x1 = ctx.t["x1"]
    ln1bt = ctx.t["ln1bt"]
    with b.stage("asm"):
        lnw_f = b.sbuf("a_lnw", [1, D], F32)
        lnb_f = b.sbuf("a_lnb", [1, D], F32)
        ones_f = b.sbuf("a_ones", [1, 128], F32)
        wb = b.sbuf("a_wb", [128, D], F32)
        bb_ = b.sbuf("a_bb", [128, D], F32)
        idb = b.sbuf("a_idb", [128, 128], BF16)
        idf = b.sbuf("a_idf", [128, 128], F32)
        zt = [b.sbuf(f"a_z{i}", [128, D], F32) for i in range(3)]
        att = [b.sbuf(f"a_at{i}", [128, D], F32) for i in range(3)]
        xt = [b.sbuf(f"a_x{i}", [128, D], F32) for i in range(3)]
        n2l = [b.sbuf(f"a_n2{i}", [128, D], F32) for i in range(3)]
        z2l = [b.sbuf(f"a_z2{i}", [128, D], F32) for i in range(3)]
        x1t = [b.sbuf(f"a_x1{i}", [128, D], F32) for i in range(3)]
        nb = [b.sbuf(f"a_nb{i}", [128, D], BF16) for i in range(3)]
        stg = [b.sbuf(f"a_stg{i}", [128, 128], BF16) for i in range(4)]
        bn6 = [b.sbuf(f"a_bn6{i}", [128, 12], F32) for i in range(3)]
        bn2 = [b.sbuf(f"a_bn2{i}", [128, 2], F32) for i in range(3)]
        rstd = [b.sbuf(f"a_rstd{i}", [128, 1], F32) for i in range(3)]
        nmr = [b.sbuf(f"a_nmr{i}", [128, 1], F32) for i in range(3)]
        p_w = b.psum("a_pw", [128, D], F32)
        ptr = [b.psum(f"a_ptr{i}", [128, 128], BF16) for i in range(2)]

        b.dma(idf[:], ctx.t["id128"][:], reads=[ctx.dbuf("id128")], writes=[idf])
        b.op("dve", lambda e: e.tensor_copy(idb[:], idf[:]), reads=[idf], writes=[idb])
        b.dma(lnw_f[:], ctx.t["ln_w"][:], reads=[ctx.dbuf("ln_w")], writes=[lnw_f])
        b.dma(lnb_f[:], ctx.t["ln_b"][:], reads=[ctx.dbuf("ln_b")], writes=[lnb_f])
        b.op("dve", lambda e: e.memset(ones_f[:], 1.0), writes=[ones_f])
        for half in range(2):
            b.op("pe", lambda e, half=half: e.matmul(
                p_w[:, half * 512:(half + 1) * 512], ones_f[:], lnw_f[:, half * 512:(half + 1) * 512],
                start=True, stop=True), reads=[ones_f, lnw_f], writes=[p_w])
        b.op("dve", lambda e: e.tensor_copy(wb[:], p_w[:]), reads=[p_w], writes=[wb])
        for half in range(2):
            b.op("pe", lambda e, half=half: e.matmul(
                p_w[:, half * 512:(half + 1) * 512], ones_f[:], lnb_f[:, half * 512:(half + 1) * 512],
                start=True, stop=True), reads=[ones_f, lnb_f], writes=[p_w])
        b.op("dve", lambda e: e.tensor_copy(bb_[:], p_w[:]), reads=[p_w], writes=[bb_])

        ceps2 = b.constant(1e-12)
        ceps1 = b.constant(1e-5)
        for tc in range(NTC):
            i = tc % 3
            z, a_, xx, x1i, nbt = zt[i], att[i], xt[i], x1t[i], nb[i]
            n2, z2 = n2l[i], z2l[i]
            b6, b2_, rs, nm = bn6[i], bn2[i], rstd[i], nmr[i]
            b.dma(z[:], acc[tc * 128:(tc + 1) * 128, :], reads=[ctx.dbuf("acc", str(tc))], writes=[z])
            b.dma(a_[:], attn[tc * 128:(tc + 1) * 128, :], reads=[ctx.dbuf("attn")], writes=[a_])
            b.dma(xx[:], x[tc * 128:(tc + 1) * 128, :], reads=[ctx.dbuf("x")], writes=[xx])
            b.op("dve", lambda e, z=z, b6=b6: e.bn_stats(b6[:, 0:6], z[:, 0:512]), reads=[z], writes=[b6])
            b.op("dve", lambda e, z=z, b6=b6: e.bn_stats(b6[:, 6:12], z[:, 512:1024]), reads=[z], writes=[b6])
            b.op("dve", lambda e, b6=b6, b2_=b2_: e.bn_aggr(b2_[:], b6[:]), reads=[b6], writes=[b2_])
            b.op("act", lambda e, b2_=b2_, rs=rs: e.activation(rs[:], b2_[:, 1:2], AF.Sqrt, bias=ceps2[:, 0:1]),
                 reads=[b2_, ceps2], writes=[rs])
            b.op("dve", lambda e, rs=rs: e.reciprocal(rs[:], rs[:]), reads=[rs], writes=[rs])
            b.op("dve", lambda e, b2_=b2_, rs=rs, nm=nm: e.scalar_tensor_tensor(nm[:], b2_[:, 0:1], -1.0, rs[:], ALU.mult, ALU.mult),
                 reads=[b2_, rs], writes=[nm])
            b.op("dve", lambda e, z=z, rs=rs, nm=nm, n2=n2: e.tensor_scalar(n2[:], z[:], rs[:, 0:1], nm[:, 0:1], ALU.mult, ALU.add),
                 reads=[z, rs, nm], writes=[n2])
            b.op("dve", lambda e, n2=n2, z2=z2: e.tensor_mul(z2[:], n2[:], wb[:]), reads=[n2, wb], writes=[z2])
            b.op("pool", lambda e, z2=z2: e.tensor_add(z2[:], z2[:], bb_[:]), reads=[z2, bb_], writes=[z2])
            b.op("pool", lambda e, a_=a_, xx=xx: e.tensor_add(xx[:], xx[:], a_[:]), reads=[xx, a_], writes=[xx])
            b.op("dve", lambda e, xx=xx, x1i=x1i, z2=z2: e.tensor_add(x1i[:], xx[:], z2[:]), reads=[xx, z2], writes=[x1i])
            b.dma(x1[tc * 128:(tc + 1) * 128, :], x1i[:], reads=[x1i], writes=[ctx.dbuf("x1")])
            b.op("dve", lambda e, x1i=x1i, b6=b6: e.bn_stats(b6[:, 0:6], x1i[:, 0:512]), reads=[x1i], writes=[b6])
            b.op("dve", lambda e, x1i=x1i, b6=b6: e.bn_stats(b6[:, 6:12], x1i[:, 512:1024]), reads=[x1i], writes=[b6])
            b.op("dve", lambda e, b6=b6, b2_=b2_: e.bn_aggr(b2_[:], b6[:]), reads=[b6], writes=[b2_])
            b.op("act", lambda e, b2_=b2_, rs=rs: e.activation(rs[:], b2_[:, 1:2], AF.Sqrt, bias=ceps1[:, 0:1]),
                 reads=[b2_, ceps1], writes=[rs])
            b.op("dve", lambda e, rs=rs: e.reciprocal(rs[:], rs[:]), reads=[rs], writes=[rs])
            b.op("dve", lambda e, b2_=b2_, rs=rs, nm=nm: e.scalar_tensor_tensor(nm[:], b2_[:, 0:1], -1.0, rs[:], ALU.mult, ALU.mult),
                 reads=[b2_, rs], writes=[nm])
            b.op("dve", lambda e, x1i=x1i, nbt=nbt, rs=rs, nm=nm: e.tensor_scalar(nbt[:], x1i[:], rs[:, 0:1], nm[:, 0:1], ALU.mult, ALU.add),
                 reads=[x1i, rs, nm], writes=[nbt])
            for dc in range(NDC):
                p = ptr[dc % 2]
                s = stg[dc % 4]
                b.op("pe", lambda e, nbt=nbt, dc=dc, p=p: e.transpose(p[:], nbt[:, dc * 128:(dc + 1) * 128], idb[:]),
                     reads=[nbt, idb], writes=[p])
                b.op("dve", lambda e, p=p, s=s: e.tensor_copy(s[:], p[:]), reads=[p], writes=[s])
                b.dma(ln1bt[dc, :, tc * 128:(tc + 1) * 128], s[:], reads=[s], writes=[ctx.dbuf("ln1bt")])


def stage_lilora(b, ctx):
    x1 = ctx.t["x1"]
    ln1bt = ctx.t["ln1bt"]
    out = ctx.t["out"].rearrange("b l d -> (b l) d")
    with b.stage("lilora"):
        ldt_f = b.sbuf("l_ldt", [100, 4], F32)
        ldt_b = b.sbuf("l_ldtb", [100, 4], BF16)
        lut_f = b.sbuf("l_lut", [4, 50], F32)
        lut_b = b.sbuf("l_lutb", [4, 50], BF16)
        da_f = b.sbuf("l_da", [100, D], F32)
        da_b = b.sbuf("l_dab", [100, D], BF16)
        ua_f = b.sbuf("l_ua", [50, D], F32)
        ua_b = b.sbuf("l_uab", [50, D], BF16)
        a1_b = b.sbuf("l_a1b", [4, D], BF16)
        a2_b = b.sbuf("l_a2b", [50, D], BF16)
        wt_f = [b.sbuf(f"l_wt{i}", [128, D], F32) for i in range(2)]
        weff = b.sbuf("l_weff", [128, NDC * D], BF16)
        bl_f = b.sbuf("l_blf", [1, D], F32)
        ones_f = b.sbuf("l_ones", [1, 128], F32)
        bb2 = b.sbuf("l_bb2", [128, D], F32)
        ltb = [b.sbuf(f"l_ltb{i}", [128, NDC * 128], BF16) for i in range(3)]
        x1t = [b.sbuf(f"l_x1{i}", [128, D], F32) for i in range(3)]
        ot = [b.sbuf(f"l_ot{i}", [128, D], F32) for i in range(3)]
        p_a = b.psum("l_pa", [128, D], F32)
        p_o = [b.psum(f"l_po{i}", [128, 512], F32) for i in range(2)]

        b.dma(ldt_f[:], ctx.t["ldT"][:], reads=[ctx.dbuf("ldT")], writes=[ldt_f])
        b.dma(lut_f[:], ctx.t["luT"][:], reads=[ctx.dbuf("luT")], writes=[lut_f])
        b.dma(da_f[:], ctx.t["daux"][:], reads=[ctx.dbuf("daux")], writes=[da_f])
        b.dma(ua_f[:], ctx.t["uauxT"][:], reads=[ctx.dbuf("uauxT")], writes=[ua_f])
        b.dma(bl_f[:], ctx.t["bL1"][:], reads=[ctx.dbuf("bL1")], writes=[bl_f])
        for t_, s_ in ((ldt_b, ldt_f), (lut_b, lut_f), (da_b, da_f), (ua_b, ua_f)):
            b.op("dve", lambda e, t_=t_, s_=s_: e.tensor_copy(t_[:], s_[:]), reads=[s_], writes=[t_])
        b.op("dve", lambda e: e.memset(ones_f[:], 1.0), writes=[ones_f])
        for half in range(2):
            b.op("pe", lambda e, half=half: e.matmul(
                p_a[:, half * 512:(half + 1) * 512], ones_f[:], bl_f[:, half * 512:(half + 1) * 512],
                start=True, stop=True), reads=[ones_f, bl_f], writes=[p_a])
        b.op("dve", lambda e: e.tensor_copy(bb2[:], p_a[:]), reads=[p_a], writes=[bb2])
        for half in range(2):
            b.op("pe", lambda e, half=half: e.matmul(
                p_a[0:4, half * 512:(half + 1) * 512], ldt_b[:], da_b[:, half * 512:(half + 1) * 512],
                start=True, stop=True), reads=[ldt_b, da_b], writes=[p_a])
        b.op("dve", lambda e: e.tensor_copy(a1_b[:], p_a[0:4, :]), reads=[p_a], writes=[a1_b])
        for half in range(2):
            b.op("pe", lambda e, half=half: e.matmul(
                p_a[0:50, half * 512:(half + 1) * 512], lut_b[:], a1_b[:, half * 512:(half + 1) * 512],
                start=True, stop=True), reads=[lut_b, a1_b], writes=[p_a])
        b.op("dve", lambda e: e.tensor_copy(a2_b[:], p_a[0:50, :]), reads=[p_a], writes=[a2_b])
        for ic in range(NDC):
            wtf = wt_f[ic % 2]
            b.dma(wtf[:], ctx.t["Wt"][ic * 128:(ic + 1) * 128, :], reads=[ctx.dbuf("Wt")], writes=[wtf])
            for half in range(2):
                b.op("pe", lambda e, ic=ic, half=half: e.matmul(
                    p_a[:, half * 512:(half + 1) * 512], a2_b[:, ic * 128:(ic + 1) * 128],
                    ua_b[:, half * 512:(half + 1) * 512], start=True, stop=True),
                    reads=[a2_b, ua_b], writes=[p_a])
            b.op("dve", lambda e, ic=ic, wtf=wtf: e.tensor_add(weff[:, ic * D:(ic + 1) * D], p_a[:], wtf[:]),
                 reads=[p_a, wtf], writes=[weff])
        for tc in range(NTC):
            i = tc % 3
            lti, x1i, oti = ltb[i], x1t[i], ot[i]
            for dc in range(NDC):
                b.dma(lti[:, dc * 128:(dc + 1) * 128], ln1bt[dc, :, tc * 128:(tc + 1) * 128],
                      reads=[ctx.dbuf("ln1bt")], writes=[lti])
            b.dma(x1i[:], x1[tc * 128:(tc + 1) * 128, :], reads=[ctx.dbuf("x1")], writes=[x1i])
            for half in range(2):
                po = p_o[half]
                for ic in range(NDC):
                    b.op("pe", lambda e, lti=lti, po=po, ic=ic, half=half: e.matmul(
                        po[:, 0:512], lti[:, ic * 128:(ic + 1) * 128],
                        weff[:, ic * D + half * 512:ic * D + (half + 1) * 512],
                        start=(ic == 0), stop=(ic == NDC - 1)), reads=[lti, weff], writes=[po])
                b.op("dve", lambda e, po=po, x1i=x1i, oti=oti, half=half: e.tensor_add(
                    oti[:, half * 512:(half + 1) * 512], po[:, 0:512], x1i[:, half * 512:(half + 1) * 512]),
                    reads=[po, x1i], writes=[oti])
            b.op("pool", lambda e, oti=oti: e.tensor_add(oti[:], oti[:], bb2[:]), reads=[oti, bb2], writes=[oti])
            b.dma(out[tc * 128:(tc + 1) * 128, :], oti[:], reads=[oti], writes=[ctx.dbuf("out", str(tc))])


def stage_mamba_pre(b, ctx, bb):
    ln1bt = ctx.t["ln1bt"]
    inpt = ctx.t["inpt_bf"]
    xsb = ctx.t["xsb"]
    zsil = ctx.t["zsil"]
    t0 = bb * L
    with b.stage(f"mpre{bb}"):
        lb = b.sbuf("m_lb", [128, NDC * L], BF16)
        wsl = [b.sbuf(f"m_wsl{i}", [128, NDC * 512], BF16) for i in range(2)]
        xp_f = b.sbuf("m_xpf", [128, NIC * 96], F32)
        cwr = b.sbuf("m_cwr", [128, NIC * DCONV], F32)
        cbr = b.sbuf("m_cbr", [128, NIC], F32)
        xr = b.sbuf("m_xr", [128, L + DCONV - 1], F32)
        cv1 = b.sbuf("m_cv1", [128, L], F32)
        cv2 = b.sbuf("m_cv2", [128, L], F32)
        xs_t = b.sbuf("m_xst", [128, L], F32)
        zs_t = [b.sbuf(f"m_zst{i}", [128, L], F32) for i in range(2)]
        if "m_dt" not in ctx.shared:
            ctx.shared["m_dt"] = b.sbuf("m_dt_p", [64, L], F32, persistent=True)
            ctx.shared["m_bc"] = b.sbuf("m_bc_p", [32, L], BF16, persistent=True)
        p_xz = [b.psum(f"m_pxz{i}", [128, 512], F32) for i in range(2)]
        p_x = b.psum("m_px", [96, L], F32)

        b.dma(cwr[:], ctx.t["conv_wr"][:], reads=[ctx.dbuf("conv_wr")], writes=[cwr])
        b.dma(cbr[:], ctx.t["conv_br"][:], reads=[ctx.dbuf("conv_br")], writes=[cbr])
        xsrc = ctx.t["xpT"].rearrange("(ic p) k -> p ic k", p=128)
        b.dma(xp_f.rearrange("p (ic k) -> p ic k", ic=NIC), xsrc, reads=[ctx.dbuf("xpT")], writes=[xp_f])
        for dc in range(NDC):
            b.dma(lb[:, dc * L:(dc + 1) * L], ln1bt[dc, :, t0:t0 + L], reads=[ctx.dbuf("ln1bt")], writes=[lb])

        for eg in range(8):
            wi = wsl[eg % 2]
            for dc in range(NDC):
                b.dma(wi[:, dc * 512:(dc + 1) * 512],
                      inpt[dc * 128:(dc + 1) * 128, eg * 512:(eg + 1) * 512],
                      reads=[ctx.dbuf("inpt_bf")], writes=[wi])
            for ec in range(4):
                ic_g = eg * 4 + ec
                is_x = ic_g < NIC
                ic = ic_g if is_x else ic_g - NIC
                for lb4 in range(4):
                    p = p_xz[lb4 % 2]
                    for dc in range(NDC):
                        b.op("pe", lambda e, wi=wi, p=p, dc=dc, ec=ec, lb4=lb4: e.matmul(
                            p[:], wi[:, dc * 512 + ec * 128:dc * 512 + (ec + 1) * 128],
                            lb[:, dc * L + lb4 * 512:dc * L + (lb4 + 1) * 512],
                            start=(dc == 0), stop=(dc == NDC - 1)), reads=[wi, lb], writes=[p])
                    if is_x:
                        b.op("dve", lambda e, p=p, lb4=lb4: e.tensor_copy(
                            xr[:, DCONV - 1 + lb4 * 512:DCONV - 1 + (lb4 + 1) * 512], p[:]),
                            reads=[p], writes=[xr])
                    else:
                        zi = zs_t[ec % 2]
                        b.op("act", lambda e, p=p, lb4=lb4, zi=zi: e.activation(
                            zi[:, lb4 * 512:(lb4 + 1) * 512], p[:], AF.Silu), reads=[p], writes=[zi])
                if is_x:
                    b.op("dve", lambda e, ic=ic: e.memset(xr[:, 0:DCONV - 1], 0.0), writes=[xr])
                    b.op("dve", lambda e, ic=ic: e.tensor_scalar(cv1[:], xr[:, 0:L], cwr[:, ic * DCONV:ic * DCONV + 1], None, ALU.mult),
                         reads=[xr, cwr], writes=[cv1])
                    b.op("dve", lambda e, ic=ic: e.scalar_tensor_tensor(cv2[:], xr[:, 1:1 + L], cwr[:, ic * DCONV + 1:ic * DCONV + 2], cv1[:], ALU.mult, ALU.add),
                         reads=[xr, cwr, cv1], writes=[cv2])
                    b.op("dve", lambda e, ic=ic: e.scalar_tensor_tensor(cv1[:], xr[:, 2:2 + L], cwr[:, ic * DCONV + 2:ic * DCONV + 3], cv2[:], ALU.mult, ALU.add),
                         reads=[xr, cwr, cv2], writes=[cv1])
                    b.op("dve", lambda e, ic=ic: e.scalar_tensor_tensor(cv2[:], xr[:, 3:3 + L], cwr[:, ic * DCONV + 3:ic * DCONV + 4], cv1[:], ALU.mult, ALU.add),
                         reads=[xr, cwr, cv1], writes=[cv2])
                    b.op("act", lambda e, ic=ic: e.activation(xs_t[:], cv2[:], AF.Silu, bias=cbr[:, ic:ic + 1]),
                         reads=[cv2, cbr], writes=[xs_t])
                    b.dma(xsb[bb, ic], xs_t[:], reads=[xs_t], writes=[ctx.dbuf("xsb")])
                    for lb4 in range(4):
                        b.op("pe", lambda e, ic=ic, lb4=lb4: e.matmul(
                            p_x[:, lb4 * 512:(lb4 + 1) * 512], xp_f[:, ic * 96:(ic + 1) * 96],
                            xs_t[:, lb4 * 512:(lb4 + 1) * 512],
                            start=(ic == 0), stop=(ic == NIC - 1)), reads=[xp_f, xs_t], writes=[p_x])
                else:
                    b.dma(zsil[bb, ic], zs_t[ec % 2][:], reads=[zs_t[ec % 2]], writes=[ctx.dbuf("zsil")])
        b.op("act", lambda e: e.copy(ctx.shared["m_dt"][:], p_x[0:64, :]), reads=[p_x], writes=[ctx.shared["m_dt"]])
        b.op("dve", lambda e: e.tensor_copy(ctx.shared["m_bc"][:], p_x[64:96, :]), reads=[p_x], writes=[ctx.shared["m_bc"]])


def stage_mamba_scan(b, ctx, bb):
    xsb = ctx.t["xsb"]
    zsil = ctx.t["zsil"]
    outp = ctx.t["outpt_bf"]
    out = ctx.t["out"].rearrange("b l d -> (b l) d")
    t0 = bb * L
    with b.stage(f"mscan{bb}"):
        dtw_f = b.sbuf("s_dtwf", [64, DIN], F32)
        dtbr = b.sbuf("s_dtbr", [128, NIC], F32)
        alog = b.sbuf("s_alog", [128, NIC * DST], F32)
        a_sb = b.sbuf("s_a", [128, NIC * DST], F32)
        dr = b.sbuf("s_dr", [128, NIC], F32)
        sel_b = b.sbuf("s_selb", [32, 32 * 128], BF16)
        ones_c = b.constant(1.0)
        h_st = b.sbuf("s_hst", [128, NIC * DST], F32)
        xs_t = [b.sbuf(f"s_xst{i}", [128, 1024], F32) for i in range(2)]
        zs_t = b.sbuf("s_zst", [128, 1024], F32)
        dta = b.sbuf("s_dta", [128, 1024], F32)
        delta = [b.sbuf(f"s_delta{i}", [128, 1024], F32) for i in range(4)]
        dx = [b.sbuf(f"s_dx{i}", [128, 1024], F32) for i in range(4)]
        da_t2 = [b.sbuf(f"s_dat{i}", [128, 1024], F32) for i in range(2)]
        dbu_t2 = [b.sbuf(f"s_dbut{i}", [128, 1024], F32) for i in range(2)]
        h_t2 = [b.sbuf(f"s_ht{i}", [128, 1024], F32) for i in range(2)]
        hc_t2 = [b.sbuf(f"s_hct{i}", [128, 1024], F32) for i in range(2)]
        hc_t = hc_t2[0]
        y_ac = [b.sbuf(f"s_y{i}", [128, 1024], F32) for i in range(4)]
        y_b = b.sbuf("s_yb", [128, NIC * 1024], BF16)
        ot = [b.sbuf(f"s_ot{i}", [128, 512], F32) for i in range(2)]
        wo_half = b.sbuf("s_wohalf", [128, NIC * 512], BF16)
        p_d = b.psum("s_pd", [128, 1024], F32)
        p_bm = b.psum("s_pbm", [128, 1024], F32)
        p_cm = b.psum("s_pcm", [128, 1024], F32)
        p_op = b.psum("s_pop", [128, 512], F32)

        b.dma(dtw_f[:], ctx.t["dtwT"][:], reads=[ctx.dbuf("dtwT")], writes=[dtw_f])
        b.dma(dtbr[:], ctx.t["dtbr"][:], reads=[ctx.dbuf("dtbr")], writes=[dtbr])
        b.dma(alog[:], ctx.t["alogr"][:], reads=[ctx.dbuf("alogr")], writes=[alog])
        b.dma(dr[:], ctx.t["Dr"][:], reads=[ctx.dbuf("Dr")], writes=[dr])
        b.dma(sel_b[:], ctx.t["sel32"][:], reads=[ctx.dbuf("sel32")], writes=[sel_b])
        b.op("act", lambda e: e.activation(a_sb[:], alog[:], AF.Exp), reads=[alog], writes=[a_sb])
        b.op("dve", lambda e: e.tensor_scalar_mul(a_sb[:], a_sb[:], -1.0), reads=[a_sb], writes=[a_sb])
        b.op("dve", lambda e: e.memset(h_st[:], 0.0), writes=[h_st])

        m_dt = ctx.shared["m_dt"]
        m_bc = ctx.shared["m_bc"]
        cbias = b.constant(1.0)

        for lh in range(2):
            l0 = lh * 1024
            for ds in range(4):
                for j in range(4):
                    ic = ds * 4 + j
                    xst = xs_t[j % 2]
                    b.dma(xst[:], xsb[bb, ic, :, l0:l0 + 1024], reads=[ctx.dbuf("xsb")], writes=[xst])
                    dlt = delta[j]
                    dxj = dx[j]
                    for h2 in range(2):
                        b.op("pe", lambda e, ic=ic, h2=h2: e.matmul(
                            p_d[:, h2 * 512:(h2 + 1) * 512], dtw_f[:, ic * 128:(ic + 1) * 128],
                            m_dt[:, l0 + h2 * 512:l0 + (h2 + 1) * 512], start=True, stop=True),
                            reads=[dtw_f, m_dt], writes=[p_d])
                    b.op("dve", lambda e, ic=ic: e.tensor_scalar(dta[:], p_d[:], dtbr[:, ic:ic + 1], None, ALU.add),
                         reads=[p_d, dtbr], writes=[dta])
                    b.op("act", lambda e, dxj=dxj: e.activation(dxj[:], dta[:], AF.Abs), reads=[dta], writes=[dxj])
                    b.op("act", lambda e, dxj=dxj, dlt=dlt: e.activation(dlt[:], dxj[:], AF.Exp, scale=-1.0), reads=[dxj], writes=[dlt])
                    b.op("act", lambda e, dlt=dlt, dxj=dxj: e.activation(dxj[:], dlt[:], AF.Ln, bias=cbias[:, 0:1]), reads=[dlt, cbias], writes=[dxj])
                    b.op("dve", lambda e, dlt=dlt: e.tensor_scalar_max(dlt[:], dta[:], 0.0), reads=[dta], writes=[dlt])
                    b.op("dve", lambda e, dlt=dlt, dxj=dxj: e.tensor_add(dlt[:], dlt[:], dxj[:]), reads=[dlt, dxj], writes=[dlt])
                    b.op("dve", lambda e, dlt=dlt, dxj=dxj, xst=xst: e.tensor_mul(dxj[:], dlt[:], xst[:]), reads=[dlt, xst], writes=[dxj])
                    b.op("dve", lambda e, j=j: e.memset(y_ac[j][:], 0.0), writes=[y_ac[j]])
                for n in range(DST):
                    for h2 in range(2):
                        b.op("pe", lambda e, n=n, h2=h2: e.matmul(
                            p_bm[:, h2 * 512:(h2 + 1) * 512], sel_b[:, n * 128:(n + 1) * 128],
                            m_bc[:, l0 + h2 * 512:l0 + (h2 + 1) * 512], start=True, stop=True),
                            reads=[sel_b, m_bc], writes=[p_bm])
                        b.op("pe", lambda e, n=n, h2=h2: e.matmul(
                            p_cm[:, h2 * 512:(h2 + 1) * 512], sel_b[:, (16 + n) * 128:(17 + n) * 128],
                            m_bc[:, l0 + h2 * 512:l0 + (h2 + 1) * 512], start=True, stop=True),
                            reads=[sel_b, m_bc], writes=[p_cm])
                    for j in range(4):
                        ic = ds * 4 + j
                        col = ic * DST + n
                        p2 = j % 2
                        da_t, dbu_t, h_t, hc_t = da_t2[p2], dbu_t2[p2], h_t2[p2], hc_t2[p2]
                        b.op("act", lambda e, j=j, col=col, da_t=da_t: e.activation(
                            da_t[:], delta[j][:], AF.Exp, scale=a_sb[:, col:col + 1]),
                            reads=[delta[j], a_sb], writes=[da_t])
                        b.op("dve", lambda e, j=j, dbu_t=dbu_t: e.tensor_tensor(dbu_t[:], dx[j][:], p_bm[:], ALU.mult),
                             reads=[dx[j], p_bm], writes=[dbu_t])
                        b.op("dve", lambda e, col=col, h_t=h_t, da_t=da_t, dbu_t=dbu_t: e.tensor_tensor_scan(
                            h_t[:], da_t[:], dbu_t[:], h_st[:, col:col + 1], ALU.mult, ALU.add),
                            reads=[da_t, dbu_t, h_st], writes=[h_t])
                        b.op("dve", lambda e, col=col, h_t=h_t: e.tensor_copy(h_st[:, col:col + 1], h_t[:, 1023:1024]),
                             reads=[h_t], writes=[h_st])
                        b.op("dve", lambda e, h_t=h_t, hc_t=hc_t: e.tensor_tensor(hc_t[:], h_t[:], p_cm[:], ALU.mult),
                             reads=[h_t, p_cm], writes=[hc_t])
                        b.op("pool", lambda e, j=j, hc_t=hc_t: e.tensor_add(y_ac[j][:], y_ac[j][:], hc_t[:]),
                             reads=[y_ac[j], hc_t], writes=[y_ac[j]])
                for j in range(4):
                    ic = ds * 4 + j
                    xst = xs_t[j % 2]
                    b.dma(zs_t[:], zsil[bb, ic, :, l0:l0 + 1024], reads=[ctx.dbuf("zsil")], writes=[zs_t])
                    b.dma(xst[:], xsb[bb, ic, :, l0:l0 + 1024], reads=[ctx.dbuf("xsb")], writes=[xst])
                    b.op("dve", lambda e, j=j, ic=ic, xst=xst: e.scalar_tensor_tensor(
                        hc_t[:], xst[:], dr[:, ic:ic + 1], y_ac[j][:], ALU.mult, ALU.add),
                        reads=[xst, dr, y_ac[j]], writes=[hc_t])
                    b.op("dve", lambda e, ic=ic: e.tensor_mul(
                        y_b[:, ic * 1024:(ic + 1) * 1024], hc_t[:], zs_t[:]),
                        reads=[hc_t, zs_t], writes=[y_b])
            for oh in range(2):
                for ic in range(NIC):
                    b.dma(wo_half[:, ic * 512:(ic + 1) * 512],
                          outp[ic * 128:(ic + 1) * 128, oh * 512:(oh + 1) * 512],
                          reads=[ctx.dbuf("outpt_bf")], writes=[wo_half])
                for ls in range(8):
                    tg = t0 + l0 + ls * 128
                    oti = ot[ls % 2]
                    b.dma(oti[:, 0:512], out[tg:tg + 128, oh * 512:(oh + 1) * 512],
                          reads=[ctx.dbuf("out", str(tg // 128) + "_" + str(oh))], writes=[oti])
                    for ic in range(NIC):
                        b.op("pe", lambda e, ic=ic, ls=ls: e.matmul(
                            p_op[:, 0:512],
                            y_b[:, ic * 1024 + ls * 128:ic * 1024 + ls * 128 + 128],
                            wo_half[:, ic * 512:(ic + 1) * 512],
                            start=(ic == 0), stop=(ic == NIC - 1)), reads=[y_b, wo_half], writes=[p_op])
                    b.op("dve", lambda e, oti=oti: e.tensor_add(oti[:, 0:512], oti[:, 0:512], p_op[:, 0:512]),
                         reads=[p_op], writes=[oti])
                    b.dma(out[tg:tg + 128, oh * 512:(oh + 1) * 512], oti[:, 0:512],
                          reads=[oti], writes=[ctx.dbuf("out", str(tg // 128) + "_" + str(oh))])


STAGE_DEFS = {}


def stage_l0front(b, ctx):
    es_p = ExitStack()
    stg = [TT(es_p.enter_context(b.nc.sbuf_tensor(b._uname("pp_s"), [128, 2048], F32, side="right")), Buf(f"pp_s{i}")) for i in range(3)]
    stb = [TT(es_p.enter_context(b.nc.sbuf_tensor(b._uname("pp_b"), [128, 2048], BF16, side="right")), Buf(f"pp_b{i}")) for i in range(3)]
    b.pump = prep_chunks(b, ctx, bufs=(stg, stb))
    stage_ln(b, ctx, "x", "acc", "ln0bt")
    stage_attn(b, ctx)
    if b.pump is not None:
        with b.stage("prepdrain"):
            for _ in b.pump:
                pass
        b.pump = None
    es_p.close()


def _register_stages():
    STAGE_DEFS.clear()
    STAGE_DEFS["s0"] = (stage_prep, (), ("w1t_bf", "w2t_bf", "inpt_bf", "outpt_bf"), False)
    STAGE_DEFS["l0f"] = (stage_l0front, (),
                         ("acc", "ln0bt", "attn", "w1t_bf", "w2t_bf", "inpt_bf", "outpt_bf"), False)
    STAGE_DEFS["s1"] = (lambda b, ctx: stage_ln(b, ctx, "x", "acc", "ln0bt"), (), ("acc", "ln0bt"), False)
    STAGE_DEFS["s2"] = (stage_attn, ("ln0bt",), ("attn",), False)
    STAGE_DEFS["s3r"] = (stage_router, ("ln0bt", "acc"), ("acc",), False)
    STAGE_DEFS["s3e"] = (stage_experts, ("ln0bt", "acc", "w1t_bf", "w2t_bf"), ("acc",), False)
    STAGE_DEFS["s4"] = (stage_assembly, ("acc", "attn"), ("x1", "ln1bt"), False)
    def tail1(b, ctx):
        with b.stage("tail1"):
            stage_assembly(b, ctx)
            stage_lilora(b, ctx)
    STAGE_DEFS["tail1"] = (tail1, ("acc", "attn"), ("x1", "ln1bt"), True)
    STAGE_DEFS["s6"] = (stage_lilora, ("x1", "ln1bt"), (), True)
    STAGE_DEFS["m0p"] = (lambda b, ctx: stage_mamba_pre(b, ctx, 0), ("ln1bt", "inpt_bf"), ("xsb", "zsil"), False)
    STAGE_DEFS["m0s"] = (lambda b, ctx: stage_mamba_scan(b, ctx, 0), ("xsb", "zsil", "outpt_bf"), (), True)
    STAGE_DEFS["m1p"] = (lambda b, ctx: stage_mamba_pre(b, ctx, 1), ("ln1bt", "inpt_bf"), ("xsb", "zsil"), False)
    STAGE_DEFS["m1s"] = (lambda b, ctx: stage_mamba_scan(b, ctx, 1), ("xsb", "zsil", "outpt_bf"), (), True)


_register_stages()


def build_program(stage_names):
    nc = bass.Bass("TRN2", target_bir_lowering=False)
    inter_in, inter_out, with_out = [], [], False
    produced = set()
    for sn in stage_names:
        fn, ii, io, wo = STAGE_DEFS[sn]
        for nm in ii:
            if nm not in produced and nm not in inter_in:
                inter_in.append(nm)
        for nm in io:
            produced.add(nm)
            if nm not in inter_out:
                inter_out.append(nm)
        with_out = with_out or wo
    if FUSED:
        pass
    assert not (set(inter_in) & set(inter_out)), (inter_in, inter_out)
    internal = FUSED and with_out
    ctx = Ctx(nc, inter_in=inter_in, inter_out=inter_out, with_out=with_out, internal=internal)
    if internal:
        inter_out = []
    es = ExitStack()
    with es:
        b = Builder(nc, es)
        for v in (1e-5, 1e-12, 1.0):
            b.constant(v)
        for sn in stage_names:
            STAGE_DEFS[sn][0](b, ctx)
    return nc, ctx, inter_in, inter_out, with_out


def prep_host_inputs(inputs):
    g = {k: np.asarray(v) for k, v in inputs.items()}
    shared = {}
    shared["w_qkv"] = np.ascontiguousarray(g["L0_attn_w_qkv"])
    shared["w_o"] = np.ascontiguousarray(g["L0_attn_w_o"])
    shared["wrT"] = np.ascontiguousarray(g["L0_ffn_Wr"].T)
    shared["br"] = np.ascontiguousarray(g["L0_ffn_br"].reshape(1, E))
    shared["w1T"] = np.ascontiguousarray(np.transpose(g["L0_ffn_W1"], (0, 2, 1)))
    shared["b1r"] = np.ascontiguousarray(g["L0_ffn_b1"].reshape(E, NFC, 128).transpose(2, 0, 1).reshape(128, E * NFC))
    shared["w2T"] = np.ascontiguousarray(np.transpose(g["L0_ffn_W2"], (0, 2, 1)))
    shared["b2"] = np.ascontiguousarray(g["L0_ffn_b2"])
    shared["ln_w"] = np.ascontiguousarray(g["L0_ffn_ln_w"].reshape(1, D))
    shared["ln_b"] = np.ascontiguousarray(g["L0_ffn_ln_b"].reshape(1, D))
    shared["inpT"] = np.ascontiguousarray(g["L1_attn_in_proj_w"].T)
    shared["conv_wr"] = np.ascontiguousarray(g["L1_attn_conv_w"].reshape(NIC, 128, DCONV).transpose(1, 0, 2).reshape(128, NIC * DCONV))
    shared["conv_br"] = np.ascontiguousarray(g["L1_attn_conv_b"].reshape(NIC, 128).T)
    shared["xpT"] = np.ascontiguousarray(g["L1_attn_x_proj_w"].T)
    shared["dtwT"] = np.ascontiguousarray(g["L1_attn_dt_proj_w"].T)
    shared["dtbr"] = np.ascontiguousarray(g["L1_attn_dt_proj_b"].reshape(NIC, 128).T)
    shared["alogr"] = np.ascontiguousarray(g["L1_attn_A_log"].reshape(NIC, 128, DST).transpose(1, 0, 2).reshape(128, NIC * DST))
    shared["Dr"] = np.ascontiguousarray(g["L1_attn_D"].reshape(NIC, 128).T)
    shared["outpT"] = np.ascontiguousarray(g["L1_attn_out_proj_w"].T)
    shared["Wt"] = np.ascontiguousarray(g["L1_ffn_W"].T)
    shared["bL1"] = np.ascontiguousarray(g["L1_ffn_b"].reshape(1, D))
    shared["ldT"] = np.ascontiguousarray(g["L1_ffn_lora_down"].T)
    shared["luT"] = np.ascontiguousarray(g["L1_ffn_lora_up"].T)
    shared["daux"] = np.ascontiguousarray(g["L1_ffn_down_aux"])
    shared["uauxT"] = np.ascontiguousarray(g["L1_ffn_up_aux"].T)
    shared["id128"] = np.eye(128, dtype=np.float32)
    sel = np.zeros((32, 32, 128), dtype=np.float32)
    for n in range(32):
        sel[n, n, :] = 1.0
    shared["sel32"] = np.ascontiguousarray(sel.transpose(1, 0, 2).reshape(32, 32 * 128)).astype(ml_dtypes.bfloat16)
    xs = [np.ascontiguousarray(g["L0_attn_inputs"][2 * c:2 * c + 2]) for c in range(8)]
    return shared, xs


def run_stage_groups(inputs, groups):
    shared, xs = prep_host_inputs(inputs)
    carry = [dict() for _ in range(8)]
    out = None
    for gi, grp in enumerate(groups):
        nc, ctx, inter_in, inter_out, with_out = build_program(grp)
        in_maps = []
        for c in range(8):
            m = dict(shared)
            m["x"] = xs[c]
            for nm in inter_in:
                m[nm] = carry[c][nm]
            in_maps.append(m)
        res = run_bass_kernel_spmd(nc, in_maps, core_ids=list(range(8)))
        for c in range(8):
            for nm in inter_out:
                carry[c][nm] = np.asarray(res.results[c][nm])
        if with_out:
            out = [np.asarray(res.results[c]["out"]) for c in range(8)]
    return carry, out


GROUPS_UNFUSED = [["s0"], ["s1", "s3r", "s3e"], ["s2"], ["s4"],
                  ["s6", "m0p", "m0s", "m1p", "m1s"]]
GROUPS_FUSED = [["l0f", "s3r", "s3e", "tail1", "m0p", "m0s", "m1p", "m1s"]]


def kernel(**inputs):
    groups = GROUPS_FUSED if FUSED else GROUPS_UNFUSED
    carry, out = run_stage_groups(inputs, groups)
    return np.concatenate(out, axis=0).reshape(16, L, D)
